# version 09
# baseline (speedup 1.0000x reference)
import os
import numpy as np
import ml_dtypes
import contextlib
import concourse.bass as bass
import concourse.mybir as mybir
from concourse.bass_utils import run_bass_kernel_spmd

F32 = mybir.dt.float32
BF16 = mybir.dt.bfloat16
AL = mybir.AluOpType
AX = mybir.AxisListType
AF = mybir.ActivationFunctionType
BF = ml_dtypes.bfloat16
MAGICF = float(np.float32(1.5 * 2.0 ** 23))

T = 4096
D = 1024
HD = 4096
SEG = 512
NSEG = T // SEG
CH = 8
NP = 8
EPS = 1e-5


class Sched:
    ENGINES = ("pe", "dve", "act", "gps", "sync")

    def __init__(self, nc, st):
        self.serial = int(os.environ.get("KSERIAL", "0"))
        self.steps = {e: [] for e in self.ENGINES}
        self.count = {e: 0 for e in self.ENGINES}
        self.dma_count = {e: 0 for e in ("sync", "gps", "act")}
        self.sem = {e: st.enter_context(nc.semaphore(f"s_{e}")) for e in self.ENGINES}
        self.dsem = {e: st.enter_context(nc.semaphore(f"d_{e}"))
                     for e in ("sync", "gps", "act")}

    def _allbar(self, eng, isdma=False):
        m = self.serial
        deps = []
        engs = ()
        dmas = ()
        if m == 1:
            engs = tuple(e for e in self.ENGINES if e != eng)
            dmas = ("sync", "gps", "act")
        elif m == 2:
            if isdma:
                engs = tuple(e for e in self.ENGINES if e != eng)
            else:
                dmas = ("sync", "gps", "act")
        elif m == 3 and not isdma:
            if eng == "pe":
                engs = ("dve",)
            elif eng == "dve":
                engs = ("pe",)
        elif m == 4 and not isdma:
            if eng == "act":
                engs = ("dve", "pe")
            elif eng in ("dve", "pe"):
                engs = ("act",)
        for e in engs:
            if self.count[e]:
                deps.append((self.sem[e], self.count[e]))
        for q in dmas:
            if self.dma_count[q]:
                deps.append((self.dsem[q], self.dma_count[q]))
        return deps

    def on(self, eng, fn, deps=()):
        deps = tuple(d for d in deps if d)
        if self.serial:
            deps = deps + tuple(self._allbar(eng))
        self.count[eng] += 1
        tok = (self.sem[eng], self.count[eng])
        self.steps[eng].append(("c", fn, deps, tok))
        return tok

    def dma(self, queue, fn, deps=()):
        deps = tuple(d for d in deps if d)
        if self.serial:
            deps = deps + tuple(self._allbar(queue, isdma=True))
        self.dma_count[queue] += 16
        tok = (self.dsem[queue], self.dma_count[queue])
        self.steps[queue].append(("d", fn, deps, tok))
        return tok

    def emit(self, eng, engobj):
        waited = {}
        for kind, fn, deps, tok in self.steps[eng]:
            if kind == "d":
                prior = tok[1] - 16
                if prior > 0:
                    deps = deps + ((tok[0], prior),)
            for sem, val in deps:
                key = id(sem)
                if waited.get(key, -1) >= val:
                    continue
                engobj.wait_ge(sem, val)
                waited[key] = val
            inst = fn(engobj)
            assert inst is not None, f"step on {eng} returned None"
            inst.then_inc(tok[0], 16 if kind == "d" else 1)

    def final_wait(self, eng):
        def fn(engobj):
            for q in ("sync", "gps", "act"):
                if self.dma_count[q]:
                    engobj.wait_ge(self.dsem[q], self.dma_count[q])
            for e in self.ENGINES:
                if e != eng and self.count[e]:
                    engobj.wait_ge(self.sem[e], self.count[e])
            return engobj.nop()
        self.count[eng] += 1
        self.steps[eng].append(("c", fn, (), (self.sem[eng], self.count[eng])))


def TL(*ts):
    return [t for t in ts if t is not None]


def build_nc(debug=False):
    nc = bass.Bass("TRN2", target_bir_lowering=False)

    xT = nc.dram_tensor("xT", [D, T], F32, kind="ExternalInput").ap()
    wqT = nc.dram_tensor("wqT", [D, D], F32, kind="ExternalInput").ap()
    wkT = nc.dram_tensor("wkT", [D, D], F32, kind="ExternalInput").ap()
    wvT = nc.dram_tensor("wvT", [D, D], F32, kind="ExternalInput").ap()
    woT = nc.dram_tensor("woT", [D, D], F32, kind="ExternalInput").ap()
    w1T = nc.dram_tensor("w1T", [D, HD], F32, kind="ExternalInput").ap()
    w2T = nc.dram_tensor("w2T", [HD, D], F32, kind="ExternalInput").ap()
    taps_in = nc.dram_tensor("taps", [128, 96], F32, kind="ExternalInput").ap()
    cb_in = nc.dram_tensor("convb", [128, 32], F32, kind="ExternalInput").ap()
    ab_in = nc.dram_tensor("abraw", [16, 2], F32, kind="ExternalInput").ap()
    c_identbf = nc.dram_tensor("identbf", [128, 128], BF16, kind="ExternalInput").ap()
    c_ident16 = nc.dram_tensor("ident16f", [16, 16], F32, kind="ExternalInput").ap()
    c_i2bf = nc.dram_tensor("i2bf", [128, 64], BF16, kind="ExternalInput").ap()
    c_masklow = nc.dram_tensor("masklow", [128, 512], BF16, kind="ExternalInput").ap()
    c_maskup = nc.dram_tensor("maskup", [128, 512], BF16, kind="ExternalInput").ap()
    c_maskneg = nc.dram_tensor("maskneg", [128, 512], BF16, kind="ExternalInput").ap()
    c_ones = nc.dram_tensor("onesc", [128, 2], BF16, kind="ExternalInput").ap()
    c_ones1 = nc.dram_tensor("ones1", [1, 128], BF16, kind="ExternalInput").ap()
    c_ones1f = nc.dram_tensor("ones1f", [1, 128], F32, kind="ExternalInput").ap()
    c_hsum = nc.dram_tensor("hsum2", [128, 2], BF16, kind="ExternalInput").ap()
    c_exp2 = nc.dram_tensor("expander2", [2, 128], BF16, kind="ExternalInput").ap()

    outT = nc.dram_tensor("outT", [D, T], F32, kind="ExternalOutput").ap()
    x2scr = nc.dram_tensor("x2scr", [D, T], F32, kind="Internal").ap()
    x2hiscr = nc.dram_tensor("x2hiscr", [D, T], BF16, kind="Internal").ap()
    x2loscr = nc.dram_tensor("x2loscr", [D, T], BF16, kind="Internal").ap()
    xq1scr = nc.dram_tensor("xq1scr", [T + 16, D], BF16, kind="Internal").ap()
    xq2scr = nc.dram_tensor("xq2scr", [T, HD], BF16, kind="Internal").ap()

    dbg = {}
    if debug:
        for nm, shp in [("d_q", [128, NP, SEG]), ("d_k", [128, NP, SEG]),
                        ("d_v", [128, NP, SEG]), ("d_kh", [128, NP, SEG]),
                        ("d_ln", [128, 8, SEG]), ("d_attn", [128, NP, SEG]),
                        ("d_ab", [128, 16]), ("d_r0", [128, 512]),
                        ("d_x0", [128, 512]), ("d_m1", [128, 512]),
                        ("d_att0", [128, 512]), ("d_n1", [128, 512]),
                        ("d_x1", [128, 512]), ("d_m2", [128, 512]),
                        ("d_x2s", [128, 8, SEG]), ("d_b1x2", [128, 1024]),
                        ("d_b1hi", [128, 1024]), ("d_b1lo", [128, 1024]),
                        ("d_xq1s", [256, 1024]), ("d_xq2s", [128, HD]),
                        ("d_xq1w", [128, 8, 272]),
                        ("d_d2", [128, 4]),
                        ("d_b1st", [128, 16]),
                        ("d_att1", [128, 512]), ("d_co1", [128, 512]),
                        ("d_khtm0", [128, 512]),
                        ("d_xq1", [128, D]), ("d_sh", [128, 32, 256]),
                        ("d_xq2", [128, HD])]:
            dbg[nm] = nc.dram_tensor(nm, shp, F32, kind="ExternalOutput").ap()

    st = contextlib.ExitStack()
    with st:
        def sb(name, shape, dt=F32):
            return st.enter_context(nc.sbuf_tensor(name, shape, dt))

        def mkpsum(name):
            return st.enter_context(nc.psum_tensor(name, [128, 512], F32))

        wU = sb("wU", [128, 8, 4 * D], BF16)
        w_attn = [wU[:, :, i * D:(i + 1) * D] for i in range(4)]
        w1qv = wU.rearrange("p a b -> p (a b)").rearrange("p (a b) -> p a b", a=8)

        u2 = sb("u2", [128, 32 * D], BF16)
        w2qv = u2.rearrange("p (a b) -> p a b", a=32)
        _o = [0]

        def carve2(n):
            v = u2[:, _o[0]:_o[0] + n]
            _o[0] += n
            return v
        af_flat = carve2(8 * SEG)
        attn_fm3 = af_flat.rearrange("p (a b) -> p a b", a=8)
        solN_P = [{k: carve2(512) for k in (1, 2, 4, 8, 16)} for _ in range(2)]
        solNT_P = [{k: carve2(512) for k in (1, 2, 4, 8, 16, 32)} for _ in range(2)]
        khtm_P = [carve2(512) for _ in range(2)]
        vtm_P = [carve2(512) for _ in range(2)]
        r_stk_P = [carve2(512) for _ in range(2)]
        ya_P = [carve2(512) for _ in range(2)]
        yb_P = [carve2(512) for _ in range(2)]
        pexp_P = [carve2(512) for _ in range(2)]
        pn_P = [carve2(512) for _ in range(2)]
        ptb_P = [carve2(512) for _ in range(2)]
        att_sb_P = [carve2(512) for _ in range(2)]
        co_bf_P = [carve2(512) for _ in range(2)]
        mA = carve2(512)
        mB = carve2(512)
        mtA = carve2(512)
        mtB = carve2(512)
        rows_bf = carve2(1024)
        norm2 = af_flat[0:2, 0:4096]
        _st0 = _o[0]
        x2st_hi = [carve2(512) for _ in range(2)]
        x2st_lo = [carve2(512) for _ in range(2)]
        b1_hi = u2[:, _st0:_st0 + 1024]
        b1_lo = u2[:, _st0 + 1024:_st0 + 2048]
        b1_sq = carve2(1024)
        assert _o[0] <= 32 * D, _o[0]

        u3 = sb("u3", [128, 6144], F32)
        xs3 = u3[:, 0:4096].rearrange("p (a b) -> p a b", a=8)
        sm_f_P = [u3[:, 4096:4608], u3[:, 4608:5120]]
        ftmp = u3[:, 4096:4608]
        rows_f = u3[:, 4096:5120]
        b1_x2tm = u3[:, 5120:6144]
        b1_ctm = b1_x2tm
        hwin = u3[:, 5120:5378]
        c1win = u3[:, 5378:5634]
        c2win = c1win
        c3win = u3[:, 5634:5890]

        u4 = sb("u4", [128, 24576], BF16)
        xbf3 = u4[:, 0:4096].rearrange("p (a b) -> p a b", a=8)
        qf3 = u4[:, 4096:8192].rearrange("p (a b) -> p a b", a=8)
        kf3 = u4[:, 8192:12288].rearrange("p (a b) -> p a b", a=8)
        vf3 = u4[:, 12288:16384].rearrange("p (a b) -> p a b", a=8)
        kh3 = u4[:, 16384:20480].rearrange("p (a b) -> p a b", a=8)
        kha3 = u4[:, 20480:24576].rearrange("p (a b) -> p a b", a=8)
        sqscr = u4[:, 24064:24576]
        sqscr = carve2(512)
        assert _o[0] <= 32 * D, _o[0]
        zmar = u4[:, 3072:4096]
        b2_sh3 = u4[:, 0:8192].rearrange("p (a b) -> p a b", a=32)
        b2_xq1w = u4[:, 8192:10368].rearrange("p (a b) -> p a b", a=8)
        b2_xq2b = [u4[:, 10368 + i * 512:10368 + (i + 1) * 512] for i in range(2)]
        b2_xq2tm = u4[:, 11392:15488]
        b2_stage = u4[:, 15488:23680]
        stage_f = b2_stage.bitcast(F32)

        identbf = sb("identbf_s", [128, 128], BF16)
        ident16 = sb("ident16_s", [16, 16], F32)
        i2bf = sb("i2bf_s", [128, 64], BF16)
        masklow = sb("masklow_s", [128, 512], BF16)
        maskup = sb("maskup_s", [128, 512], BF16)
        maskneg = sb("maskneg_s", [128, 512], BF16)
        onesc = sb("onesc_s", [128, 2], BF16)
        onescf = sb("onescf_s", [128, 1], F32)
        ones1 = sb("ones1_s", [1, 128], BF16)
        ones1f = sb("ones1f_s", [1, 128], F32)
        hsum2 = sb("hsum2_s", [128, 2], BF16)
        exp2 = sb("exp2_s", [2, 128], BF16)
        ab_sb = sb("ab_sb", [16, 2], F32)
        ab_row = sb("ab_row", [1, 32], F32)
        atab = sb("atab", [128, NP], F32)
        btab = sb("btab", [128, NP], F32)
        stat_sc = sb("stat_sc", [128, 32], F32)
        taps_sb = sb("taps_sb", [128, 96], F32)
        cb_sb = sb("cb_sb", [128, 32], F32)
        wstat = sb("wstat", [128, 40], F32)
        mwcol = sb("mwcol", [128, 4], F32)
        d2tab = sb("d2tab", [128, 4], BF16)
        d2row = sb("d2row", [1, 512], BF16)
        d2bc = sb("d2bc", [128, 512], BF16)
        eps_col = sb("eps_col", [128, 1], F32)
        magic_col = sb("magic_col", [128, 1], F32)

        PB = [mkpsum(f"pb{i}") for i in range(6)]
        BM = mkpsum("pbM")
        BMT = mkpsum("pbMT")

        sch = Sched(nc, st)
        bank_guard = {id(b): None for b in PB + [BM, BMT]}
        bank_i = [0]

        def next_bank():
            b = PB[bank_i[0] % 6]
            bank_i[0] += 1
            return b, bank_guard[id(b)]

        def set_guard(b, tok):
            bank_guard[id(b)] = tok

        def pair_mms(e, b, lhs3, rhs3, lcol, rcol):
            last = None
            for p in range(NP):
                lh = lhs3[:, p, lcol] if lcol is not None else lhs3[:, p, :]
                rh = rhs3[:, p, rcol] if rcol is not None else rhs3[:, p, :]
                last = e.matmul(b[0:64, 64 * p:64 * p + 64], lh[0:64], rh[0:64],
                                start=True, stop=True, tile_position=(0, 0))
                last = e.matmul(b[64:128, 64 * p:64 * p + 64], lh[64:128],
                                rh[64:128], start=True, stop=True,
                                tile_position=(64, 64))
            return last

        def pair_trans(e, b, src3, col):
            last = None
            for p in range(NP):
                sp = src3[:, p, col] if col is not None else src3[:, p, :]
                last = e.matmul(b[0:64, 64 * p:64 * p + 64], sp[0:64],
                                identbf[0:64, 0:64], start=True, stop=True,
                                tile_position=(0, 0))
                last = e.matmul(b[64:128, 64 * p:64 * p + 64], sp[64:128],
                                i2bf[64:128, :], start=True, stop=True,
                                tile_position=(64, 64))
            return last

        cl = []
        for dst, src in ((identbf, c_identbf), (ident16, c_ident16),
                         (i2bf, c_i2bf), (masklow, c_masklow), (maskup, c_maskup),
                         (maskneg, c_maskneg), (onesc, c_ones), (ones1, c_ones1),
                         (ones1f, c_ones1f), (hsum2, c_hsum), (exp2, c_exp2),
                         (ab_sb, ab_in), (taps_sb, taps_in), (cb_sb, cb_in)):
            cl.append(sch.dma("sync", (lambda e, dst=dst, src=src:
                                       e.dma_start(out=dst[:], in_=src))))
        t_consts = cl[-1]
        t_w = []
        for i, w in enumerate((wqT, wkT, wvT, woT)):
            wv_ = w.rearrange("(a p) o -> p a o", p=128)
            t_w.append(sch.dma("gps", (lambda e, i=i, wv_=wv_:
                                       e.dma_start(out=w_attn[i], in_=wv_))))
        t_of = sch.on("dve", lambda e: (e.memset(onescf[:], 1.0),
                                        e.drain())[-1])
        t_eps = sch.on("dve", lambda e: (e.memset(eps_col[:], EPS),
                                         e.drain())[-1])
        t_mgc = sch.on("dve", lambda e: (e.memset(magic_col[:], MAGICF),
                                         e.drain())[-1])
        t_zm = sch.on("dve", lambda e: e.memset(zmar[0:16, :], 0.0))
        t_zs1 = sch.dma("sync", lambda e: e.dma_start(out=xq1scr[0:8, :],
                                                      in_=zmar[0:8, :]), deps=[t_zm])
        t_zs2 = sch.dma("sync", lambda e: e.dma_start(out=xq1scr[T + 8:T + 16, :],
                                                      in_=zmar[8:16, :]), deps=[t_zm])

        t_sig = sch.on("act", lambda e: e.activation(ab_sb[:], ab_sb[:], AF.Sigmoid),
                       deps=[cl[11]])
        t_s1 = sch.on("dve", lambda e: (e.tensor_scalar(
            out=ab_sb[:, 0:1], in0=ab_sb[:, 0:1], scalar1=0.5, scalar2=None,
            op0=AL.mult), e.drain())[-1], deps=[t_sig])
        t_s2 = sch.on("dve", lambda e: (e.tensor_scalar(
            out=ab_sb[:, 1:2], in0=ab_sb[:, 1:2], scalar1=2.0, scalar2=None,
            op0=AL.mult), e.drain())[-1], deps=[t_sig])
        bA, gA = next_bank()
        t_abt = sch.on("pe", (lambda e, b=bA: (
            e.matmul(b[0:1, 0:16], ab_sb[0:16, 0:1], ident16[:, :],
                     start=True, stop=True),
            e.matmul(b[0:1, 16:32], ab_sb[0:16, 1:2], ident16[:, :],
                     start=True, stop=True))[-1]), deps=TL(t_s1, t_s2, gA, cl[1]))
        t_abr = sch.on("dve", (lambda e, b=bA: e.tensor_copy(ab_row[0:1, 0:16],
                                                             b[0:1, 0:16])),
                       deps=[t_abt])
        t_abr2 = sch.on("dve", (lambda e, b=bA: (e.tensor_copy(
            ab_row[0:1, 16:32], b[0:1, 16:32]), e.drain())[-1]),
                        deps=[t_abt])
        set_guard(bA, t_abr2)
        bA, gA = next_bank()

        def mk_ab(e, b=bA):
            last = None
            for p in range(NP):
                for h in range(2):
                    for which in range(2):
                        rhs = ab_row[0:1, 16 * which + 2 * p + h:
                                     16 * which + 2 * p + h + 1]
                        tp = (0, 0) if h == 0 else (0, 64)
                        last = e.matmul(b[64 * h:64 * h + 64,
                                          8 * which + p:8 * which + p + 1],
                                        ones1f[0:1, 0:64], rhs,
                                        start=True, stop=True, tile_position=tp)
            return last
        t_abm = sch.on("pe", mk_ab, deps=TL(t_abr, t_abr2, gA, cl[8]))

        def ev_ab(e, b=bA):
            e.tensor_copy(atab[:, :], b[:, 0:8])
            e.tensor_copy(btab[:, :], b[:, 8:16])
            return e.drain()
        t_abe = sch.on("dve", ev_ab, deps=[t_abm])
        set_guard(bA, t_abe)

        if debug:
            t_dab = sch.on("dve", lambda e: (
                e.tensor_copy(stat_sc[:, 0:8], atab[:, :]),
                e.tensor_copy(stat_sc[:, 8:16], btab[:, :]))[-1], deps=[t_abe])
            sch.dma("gps", lambda e: e.dma_start(out=dbg["d_ab"],
                                                 in_=stat_sc[:, 0:16]),
                    deps=[t_dab])
        t_mz = sch.on("dve", lambda e: (
            e.memset(mA[:], 0.0), e.memset(mtA[:], 0.0))[-1])

        xT_v = xT.rearrange("(a p) t -> p a t", p=128)
        x2scr_v = x2scr.rearrange("(a p) t -> p a t", p=128)
        x2hi_v = x2hiscr.rearrange("(a p) t -> p a t", p=128)
        x2lo_v = x2loscr.rearrange("(a p) t -> p a t", p=128)
        outT_v = outT.rearrange("(a p) t -> p a t", p=128)
        w1T_v = w1T.rearrange("(a p) o -> p a o", p=128)
        w2T_v = w2T.rearrange("(a p) o -> p a o", p=128)

        RR = lambda t: t.rearrange("p (a b) -> p a b", a=NP)
        solNv_P = [{k: RR(v) for k, v in d.items()} for d in solN_P]
        solNTv_P = [{k: RR(v) for k, v in d.items()} for d in solNT_P]
        n2v = norm2.rearrange("p (a b) -> p a b", a=8)

        tok_seg_done = None
        tok_x2_store = {}
        tok_m_ready = t_mz
        last_xq1 = t_zs2
        tok_b1_prev = None

        for s in range(NSEG):
            seg = slice(s * SEG, (s + 1) * SEG)
            m_base, m_run = (mA, mB) if s % 2 == 0 else (mB, mA)
            mt_base, mt_run = (mtA, mtB) if s % 2 == 0 else (mtB, mtA)
            m_runv = m_run.rearrange("p (a b) -> p a b", a=NP)
            mt_runv = mt_run.rearrange("p (a b) -> p a b", a=NP)

            t_ldx = sch.dma("sync", (lambda e, seg=seg: e.dma_start(
                out=xs3[:, :, :], in_=xT_v[:, :, seg])), deps=TL(tok_seg_done))

            def cast_x(e):
                last = None
                for c in range(8):
                    last = e.tensor_copy(xbf3[:, c, :], xs3[:, c, :])
                return last
            t_cast = sch.on("dve", cast_x, deps=TL(
                t_ldx, t_zs1 if s == 0 else None, t_zs2 if s == 0 else None))

            bS0, g0 = next_bank()
            bS1, g1 = next_bank()
            st_last = None
            for c in range(8):
                t_sqc = sch.on("act", (lambda e, c=c: e.activation(
                    sqscr[:, :], xbf3[:, c, :], AF.Square)),
                    deps=TL(t_cast, st_last))
                sch.on("pe", (lambda e, c=c, b=bS0: e.matmul(
                    b[0:1, 0:512], onesc[:, 0:1], xbf3[:, c, :],
                    start=(c == 0), stop=(c == 7))),
                    deps=TL(t_cast, t_consts, g0 if c == 0 else None, *t_w))
                st_last = sch.on("pe", (lambda e, c=c, b=bS1: e.matmul(
                    b[0:1, 0:512], onesc[:, 0:1], sqscr[:, :],
                    start=(c == 0), stop=(c == 7))),
                    deps=TL(t_sqc, g1 if c == 0 else None))
            t_stat = st_last

            def rowmath(e, b0=bS0, b1=bS1):
                e.tensor_scalar(out=rows_f[0:1, 0:512], in0=b0[0:1, 0:512],
                                scalar1=1.0 / D, scalar2=None, op0=AL.mult)
                e.tensor_scalar(out=rows_bf[0:1, 0:512], in0=rows_f[0:1, 0:512],
                                scalar1=-1.0, scalar2=None, op0=AL.mult)
                e.tensor_tensor(rows_f[0:1, 512:1024], rows_f[0:1, 0:512],
                                rows_f[0:1, 0:512], AL.mult)
                e.tensor_scalar(out=rows_f[0:1, 0:512], in0=b1[0:1, 0:512],
                                scalar1=1.0 / D, scalar2=None, op0=AL.mult)
                return e.tensor_tensor(rows_f[0:1, 0:512], rows_f[0:1, 0:512],
                                       rows_f[0:1, 512:1024], AL.subtract)
            t_rm = sch.on("dve", rowmath, deps=[t_stat])
            t_sd = sch.on("act", lambda e: e.activation(
                rows_f[0:1, 512:1024], rows_f[0:1, 0:512], AF.Sqrt,
                bias=eps_col[0:1, 0:1], scale=1.0), deps=TL(t_rm, t_eps))
            t_rr = sch.on("dve", lambda e: e.reciprocal(
                rows_bf[0:1, 512:1024], rows_f[0:1, 512:1024]), deps=[t_sd])
            set_guard(bS0, t_rm)
            set_guard(bS1, t_rm)

            bB0, g0 = next_bank()
            bB1, g1 = next_bank()
            t_bc = sch.on("pe", (lambda e, b0=bB0, b1=bB1: (
                e.matmul(b0[:, 0:512], ones1[0:1, :], rows_bf[0:1, 0:512],
                         start=True, stop=True),
                e.matmul(b1[:, 0:512], ones1[0:1, :], rows_bf[0:1, 512:1024],
                         start=True, stop=True))[-1]), deps=TL(t_rm, t_rr, g0, g1))

            def ln_apply(e, b0=bB0, b1=bB1):
                last = None
                for c in range(8):
                    e.tensor_tensor(ftmp[:, :], xs3[:, c, :], b0[:, 0:512], AL.add)
                    last = e.tensor_tensor(xbf3[:, c, :], ftmp[:, :], b1[:, 0:512],
                                           AL.mult)
                return last
            t_ln = sch.on("dve", ln_apply, deps=TL(t_bc, t_stat))
            set_guard(bB0, t_ln)
            set_guard(bB1, t_ln)
            if debug and s == 0:
                sch.dma("gps", lambda e: e.dma_start(out=dbg["d_ln"],
                                                     in_=xbf3[:, :, :]),
                        deps=[t_ln])

            evac_toks = []
            for wi, dst in ((0, qf3), (1, kf3), (2, vf3)):
                for ot in range(8):
                    b, g = next_bank()

                    def mmproj(e, wi=wi, ot=ot, b=b):
                        last = None
                        for kc in range(8):
                            last = e.matmul(b[:, 0:512],
                                            w_attn[wi][:, kc, 128 * ot:128 * ot + 128],
                                            xbf3[:, kc, :],
                                            start=(kc == 0), stop=(kc == 7))
                        return last
                    t_mm = sch.on("pe", mmproj, deps=TL(t_ln, g, *t_w))
                    t_ev = sch.on("dve", (lambda e, dst=dst, ot=ot, b=b:
                                          e.tensor_copy(dst[:, ot, :], b[:, 0:512])),
                                  deps=[t_mm])
                    set_guard(b, t_ev)
                    evac_toks.append(t_ev)
            t_qkv = evac_toks[-1]

            nrm_toks = []
            sq_prev = None
            for p in range(NP):
                t_ks = sch.on("act", (lambda e, p=p: e.activation(
                    sqscr[:, :], kf3[:, p, :], AF.Square)),
                    deps=TL(evac_toks[8 + p], sq_prev, t_stat))
                b, g = next_bank()
                t_nm = sch.on("pe", (lambda e, b=b: e.matmul(
                    b[0:2, 0:512], hsum2[:, :], sqscr[:, :],
                    start=True, stop=True)), deps=TL(t_ks, g))
                sq_prev = t_nm
                t_ne = sch.on("dve", (lambda e, p=p, b=b: e.tensor_copy(
                    n2v[0:2, p, :], b[0:2, 0:512])), deps=[t_nm])
                set_guard(b, t_ne)
                nrm_toks.append(t_ne)
            t_sd2 = sch.on("act", lambda e: e.activation(
                norm2[0:2, :], norm2[0:2, :], AF.Sqrt), deps=nrm_toks)
            t_mx2 = sch.on("dve", lambda e: e.tensor_scalar_max(
                norm2[0:2, :], norm2[0:2, :], 1e-12), deps=[t_sd2])
            t_rsc = sch.on("dve", lambda e: e.reciprocal(
                norm2[0:2, :], norm2[0:2, :]), deps=[t_mx2])
            kh_toks = []
            for p in range(NP):
                b, g = next_bank()
                t_bm = sch.on("pe", (lambda e, p=p, b=b: e.matmul(
                    b[:, 0:512], exp2[0:2, :], n2v[0:2, p, :],
                    start=True, stop=True)), deps=TL(t_rsc, g))
                t_khe = sch.on("dve", (lambda e, p=p, b=b: e.tensor_tensor(
                    kh3[:, p, :], kf3[:, p, :], b[:, 0:512], AL.mult)), deps=[t_bm])
                t_kha = sch.on("dve", (lambda e, p=p: e.tensor_scalar(
                    out=kha3[:, p, :], in0=kh3[:, p, :], scalar1=atab[:, p:p + 1],
                    scalar2=None, op0=AL.mult)), deps=TL(t_khe, t_abe))
                set_guard(b, t_kha)
                kh_toks.append(t_kha)
            t_khall = kh_toks[-1]
            if debug and s == 0:
                for nm, src, dep in (("d_q", qf3, t_qkv), ("d_k", kf3, t_qkv),
                                     ("d_v", vf3, t_qkv), ("d_kh", kh3, t_khall)):
                    sch.dma("gps", (lambda e, nm=nm, src=src: e.dma_start(
                        out=dbg[nm], in_=src[:, :, :])), deps=[dep])

            mstate = {"prev_mevac": tok_m_ready}
            CSTATE = {}

            def cvars(c):
                par = c % 2
                d = dict(
                    col=slice(64 * c, 64 * c + 64), par=par,
                    khtm=khtm_P[par], vtm=vtm_P[par], r_stk=r_stk_P[par],
                    ya=ya_P[par], yb=yb_P[par], pexp=pexp_P[par],
                    pn=pn_P[par], ptb=ptb_P[par], att_sb=att_sb_P[par],
                    co_bf=co_bf_P[par], sm_f=sm_f_P[par],
                    solN=solN_P[par], solNT=solNT_P[par],
                    solNv=solNv_P[par], solNTv=solNTv_P[par],
                    so=16 * par,
                )
                d["khtmv"] = RR(d["khtm"])
                d["vtmv"] = RR(d["vtm"])
                d["ptv"] = RR(d["ptb"])
                d["pexpv"] = RR(d["pexp"])
                d["r_stkv"] = RR(d["r_stk"])
                d["yav"], d["ybv"] = RR(d["ya"]), RR(d["yb"])
                d["co_bfv"] = RR(d["co_bf"])
                d["pnv"] = RR(d["pn"])
                return d

            def stage_head_units(c):
                v = CSTATE[c] = cvars(c)
                units = []

                def u_t1():
                    b, g = next_bank()
                    t1 = sch.on("pe", (lambda e, b=b, col=v["col"]: pair_trans(
                        e, b, kh3, col)), deps=TL(t_khall, g, t_consts))
                    v["t1e"] = sch.on("dve", (lambda e, b=b, khtm=v["khtm"]:
                                              e.tensor_copy(khtm[:, :],
                                                            b[:, 0:512])),
                                      deps=[t1])
                    set_guard(b, v["t1e"])
                units.append(u_t1)

                def u_t2():
                    b, g = next_bank()
                    t2 = sch.on("pe", (lambda e, b=b, col=v["col"]: pair_trans(
                        e, b, vf3, col)), deps=TL(t_qkv, g))
                    v["t2e"] = sch.on("act", (lambda e, b=b, vtm=v["vtm"]: e.copy(
                        vtm[:, :], b[:, 0:512])), deps=[t2])
                    set_guard(b, v["t2e"])
                units.append(u_t2)

                def u_g():
                    b, g = next_bank()
                    tg = sch.on("pe", (lambda e, b=b, col=v["col"]: pair_mms(
                        e, b, kh3, kha3, col, col)), deps=TL(t_khall, g))
                    tn1 = sch.on("dve", (lambda e, b=b, solN=v["solN"]:
                                         e.tensor_tensor(solN[1][:, :],
                                                         b[:, 0:512],
                                                         masklow[:, :], AL.mult)),
                                 deps=[tg])
                    tnt1 = sch.on("dve", (lambda e, b=b, solNT=v["solNT"]:
                                          e.tensor_tensor(solNT[1][:, :],
                                                          b[:, 0:512],
                                                          maskup[:, :], AL.mult)),
                                  deps=[tg])
                    set_guard(b, tnt1)
                    v["form_tok"] = {1: (tn1, tnt1)}
                units.append(u_g)

                def u_s():
                    so = v["so"]
                    b, g = next_bank()
                    ts_ = sch.on("pe", (lambda e, b=b, col=v["col"]: pair_mms(
                        e, b, qf3, kh3, col, col)), deps=TL(t_qkv, t_khall, g))
                    tsm = sch.on("dve", (lambda e, b=b, sm_f=v["sm_f"]:
                                         e.tensor_tensor(sm_f[:, :], b[:, 0:512],
                                                         maskneg[:, :], AL.add)),
                                 deps=[ts_])
                    set_guard(b, tsm)
                    tmx = sch.on("dve", (lambda e, sm_f=v["sm_f"], so=so: (
                        e.reduce_max(stat_sc[:, so:so + 1], sm_f[:, :],
                                     axis=AX.X), e.drain())[-1]), deps=[tsm])
                    tnb = sch.on("dve", (lambda e, so=so: (e.tensor_scalar(
                        out=stat_sc[:, so + 1:so + 2],
                        in0=stat_sc[:, so:so + 1], scalar1=-0.125, scalar2=None,
                        op0=AL.mult), e.drain())[-1]), deps=[tmx])
                    texp = sch.on("act", (lambda e, sm_f=v["sm_f"],
                                          pexp=v["pexp"], so=so: e.activation(
                                              pexp[:, :], sm_f[:, :], AF.Exp,
                                              bias=stat_sc[:, so + 1:so + 2],
                                              scale=0.125)), deps=[tnb])
                    trs = sch.on("dve", (lambda e, pexpv=v["pexpv"], so=so: (
                        e.reduce_sum(stat_sc[:, so + 2:so + 10], pexpv,
                                     axis=AX.X), e.drain())[-1]), deps=[texp])
                    trr2 = sch.on("dve", (lambda e, so=so: (e.reciprocal(
                        stat_sc[:, so + 2:so + 10],
                        stat_sc[:, so + 2:so + 10]), e.drain())[-1]),
                        deps=[trs])

                    def pnorm(e, pexp=v["pexp"], pn=v["pn"], so=so):
                        for p in range(NP):
                            e.tensor_scalar(
                                out=pn[:, 64 * p:64 * p + 64],
                                in0=pexp[:, 64 * p:64 * p + 64],
                                scalar1=stat_sc[:, so + 2 + p:so + 3 + p],
                                scalar2=None, op0=AL.mult)
                        return e.drain()
                    v["tpn"] = sch.on("dve", pnorm, deps=[trr2])
                units.append(u_s)

                for lv_src, lv_dst in ((1, 2), (2, 4), (4, 8), (8, 16), (16, 32)):
                    def u_form(lv_src=lv_src, lv_dst=lv_dst):
                        bN, gN = next_bank()
                        bT_, gT = next_bank()
                        last_only = (lv_dst == 32)

                        def form(e, bN=bN, bT_=bT_, srcN=v["solNv"][lv_src],
                                 srcNT=v["solNTv"][lv_src], last_only=last_only):
                            last = None
                            if not last_only:
                                last = pair_mms(e, bN, srcNT, srcN, None, None)
                            last = pair_mms(e, bT_, srcN, srcNT, None, None)
                            return last
                        tf = sch.on("pe", form,
                                    deps=TL(*v["form_tok"][lv_src], gN, gT))
                        toks = []
                        if not last_only:
                            tevN = sch.on("act", (lambda e, lv=lv_dst, bN=bN,
                                                  solN=v["solN"]: e.copy(
                                                      solN[lv][:, :],
                                                      bN[:, 0:512])), deps=[tf])
                            set_guard(bN, tevN)
                            toks.append(tevN)
                        if lv_dst in (4,):
                            tevT = sch.on("act", (lambda e, lv=lv_dst, bT_=bT_,
                                                  solNT=v["solNT"]: e.copy(
                                                      solNT[lv][:, :],
                                                      bT_[:, 0:512])), deps=[tf])
                        else:
                            tevT = sch.on("dve", (lambda e, lv=lv_dst, bT_=bT_,
                                                  solNT=v["solNT"]:
                                                  e.tensor_copy(
                                                      solNT[lv][:, :],
                                                      bT_[:, 0:512])), deps=[tf])
                        set_guard(bT_, tevT)
                        toks.append(tevT)
                        v["form_tok"][lv_dst] = tuple(toks)
                    units.append(u_form)
                return units

            def stage_tail_units(c):
                v = CSTATE[c]
                units = []

                def u_r():
                    r_src = mt_base if c == 0 else mt_run
                    r_srcv = RR(r_src)
                    b, g = next_bank()
                    tr = sch.on("pe", (lambda e, b=b, col=v["col"], mtv=r_srcv:
                                       pair_mms(e, b, kha3, mtv, col, None)),
                                deps=TL(t_khall, g, mstate["prev_mevac"]))

                    def rbuild(e, b=b, r_stk=v["r_stk"], vtm=v["vtm"]):
                        for p in range(NP):
                            e.scalar_tensor_tensor(
                                out=r_stk[:, 64 * p:64 * p + 64],
                                in0=vtm[:, 64 * p:64 * p + 64],
                                scalar=btab[:, p:p + 1],
                                in1=b[:, 64 * p:64 * p + 64],
                                op0=AL.mult, op1=AL.subtract)
                        return e.drain()
                    v["trb"] = sch.on("dve", rbuild,
                                      deps=TL(tr, v["t2e"], t_abe))
                    set_guard(b, v["trb"])
                    v["ysrc"], v["ysrcv"] = v["r_stk"], v["r_stkv"]
                    v["ycur"] = v["ya"]
                    v["ytok"] = v["trb"]
                units.append(u_r)

                for k in (32, 16, 8, 4, 2, 1):
                    def u_ap(k=k):
                        b, g = next_bank()
                        tap = sch.on("pe", (lambda e, b=b, ntv=v["solNTv"][k],
                                            yv=v["ysrcv"]: pair_mms(
                                                e, b, ntv, yv, None, None)),
                                     deps=TL(*v["form_tok"][k], v["ytok"], g))
                        op = AL.subtract if k == 1 else AL.add
                        ycur = v["ycur"]
                        tye = sch.on("dve", (lambda e, b=b, ysrc=v["ysrc"],
                                             ycur=ycur, op=op: e.tensor_tensor(
                                                 ycur[:, :], ysrc[:, :],
                                                 b[:, 0:512], op)), deps=[tap])
                        set_guard(b, tye)
                        nxt = v["yb"] if ycur is v["ya"] else v["ya"]
                        v["ysrc"] = ycur
                        v["ysrcv"] = v["yav"] if ycur is v["ya"] else v["ybv"]
                        v["ycur"] = nxt
                        v["ytok"] = tye
                    units.append(u_ap)

                def u_pt():
                    v["xfinv"] = v["ysrcv"]
                    v["t_x"] = v["ytok"]
                    b, g = next_bank()
                    tpt = sch.on("pe", (lambda e, b=b, pnv=v["pnv"]: pair_trans(
                        e, b, pnv, None)), deps=TL(v["tpn"], g))
                    v["tpte"] = sch.on("act", (lambda e, b=b, ptb=v["ptb"]:
                                               e.copy(ptb[:, :], b[:, 0:512])),
                                       deps=[tpt])
                    set_guard(b, v["tpte"])
                units.append(u_pt)

                def u_av():
                    bAV, gAV = next_bank()
                    tav = sch.on("pe", (lambda e, b=bAV, ptv=v["ptv"],
                                        vtmv=v["vtmv"]: pair_mms(
                                            e, b, ptv, vtmv, None, None)),
                                 deps=TL(v["tpte"], v["t2e"], gAV))
                    v["tave"] = sch.on("dve", (lambda e, b=bAV,
                                               att_sb=v["att_sb"]:
                                               e.tensor_copy(att_sb[:, :],
                                                             b[:, 0:512])),
                                       deps=[tav])
                    set_guard(bAV, v["tave"])
                units.append(u_av)

                def u_me():
                    mem_src = m_base if c == 0 else m_run
                    msrcv = RR(mem_src)
                    bME, gME = next_bank()
                    tme = sch.on("pe", (lambda e, b=bME, col=v["col"],
                                        mv=msrcv: pair_mms(
                                            e, b, qf3, mv, col, None)),
                                 deps=TL(t_qkv, mstate["prev_mevac"], gME))
                    v["tco"] = sch.on("dve", (lambda e, b=bME, co_bf=v["co_bf"],
                                              att_sb=v["att_sb"]:
                                              e.scalar_tensor_tensor(
                                                  out=co_bf[:, :],
                                                  in0=b[:, 0:512], scalar=0.1,
                                                  in1=att_sb[:, :],
                                                  op0=AL.mult, op1=AL.add)),
                                      deps=TL(tme, v["tave"]))
                    set_guard(bME, v["tco"])
                units.append(u_me)

                def u_of():
                    bOF, gOF = next_bank()
                    tof = sch.on("pe", (lambda e, b=bOF, co_bfv=v["co_bfv"]:
                                        pair_trans(e, b, co_bfv, None)),
                                 deps=TL(v["tco"], gOF))
                    v["t_outfm_ev"] = sch.on("dve", (lambda e, b=bOF, c=c:
                                                     e.tensor_copy(
                                                         attn_fm3[:, :,
                                                                  64 * c:
                                                                  64 * c + 64],
                                                         b[:, 0:512].rearrange(
                                                             "p (a b) -> p a b",
                                                             a=NP))),
                                             deps=[tof])
                    set_guard(bOF, v["t_outfm_ev"])
                units.append(u_of)

                def u_m():
                    def mupd(e, xv=v["xfinv"], khtmv=v["khtmv"]):
                        last = None
                        for p in range(NP):
                            last = e.matmul(BM[0:64, 64 * p:64 * p + 64],
                                            xv[:, p, :][0:64],
                                            khtmv[:, p, :][0:64],
                                            start=True, stop=True,
                                            tile_position=(0, 0))
                            last = e.matmul(BM[64:128, 64 * p:64 * p + 64],
                                            xv[:, p, :][64:128],
                                            khtmv[:, p, :][64:128],
                                            start=True, stop=True,
                                            tile_position=(64, 64))
                        for p in range(NP):
                            last = e.matmul(BMT[0:64, 64 * p:64 * p + 64],
                                            khtmv[:, p, :][0:64],
                                            xv[:, p, :][0:64],
                                            start=True, stop=True,
                                            tile_position=(0, 0))
                            last = e.matmul(BMT[64:128, 64 * p:64 * p + 64],
                                            khtmv[:, p, :][64:128],
                                            xv[:, p, :][64:128],
                                            start=True, stop=True,
                                            tile_position=(64, 64))
                        return last
                    t_mupd = sch.on("pe", mupd, deps=TL(
                        v["t_x"], v["t1e"], mstate["prev_mevac"],
                        bank_guard[id(BM)], bank_guard[id(BMT)]))

                    def mevac(e, c=c, mb=m_base, mr=m_run, mtb=mt_base,
                              mtr=mt_run):
                        srcm = mb if c == 0 else mr
                        srcmt = mtb if c == 0 else mtr
                        e.tensor_tensor(mr[:, :], srcm[:, :], BM[:, 0:512],
                                        AL.add)
                        return e.tensor_tensor(mtr[:, :], srcmt[:, :],
                                               BMT[:, 0:512], AL.add)
                    t_mevac = sch.on("dve", mevac, deps=TL(t_mupd, v["tco"]))
                    set_guard(BM, t_mevac)
                    set_guard(BMT, t_mevac)
                    mstate["prev_mevac"] = t_mevac
                units.append(u_m)
                return units

            def interleave(a, b):
                out = []
                ia = ib = 0
                while ia < len(a) or ib < len(b):
                    if ia < len(a):
                        out.append(a[ia]); ia += 1
                    if ib < len(b):
                        out.append(b[ib]); ib += 1
                return out

            pending = stage_head_units(0)
            for u in pending:
                u()
            for c in range(CH):
                tail = stage_tail_units(c)
                head = stage_head_units(c + 1) if c + 1 < CH else []
                for u in interleave(tail, head):
                    u()
            t_outfm_ev = CSTATE[CH - 1]["t_outfm_ev"]
            prev_mevac = mstate["prev_mevac"]

            tok_m_ready = prev_mevac
            if debug and s == 0:
                sch.dma("gps", lambda e: e.dma_start(
                    out=dbg["d_attn"], in_=attn_fm3[:, :, :]), deps=[t_outfm_ev])

            x2_sts = []
            hi_sts = []
            lo_sts = []
            res_toks = []
            prev_hi = [tok_b1_prev, tok_b1_prev]
            for ot in range(8):
                b, g = next_bank()

                def mmwo(e, ot=ot, b=b):
                    last = None
                    for kc in range(8):
                        last = e.matmul(b[:, 0:512],
                                        w_attn[3][:, kc, 128 * ot:128 * ot + 128],
                                        attn_fm3[:, kc, :],
                                        start=(kc == 0), stop=(kc == 7))
                    return last
                t_mm = sch.on("pe", mmwo, deps=TL(t_outfm_ev, g))
                t_res = sch.on("dve", (lambda e, ot=ot, b=b: e.tensor_tensor(
                    xs3[:, ot, :], xs3[:, ot, :], b[:, 0:512], AL.add)),
                    deps=[t_mm])
                set_guard(b, t_res)
                res_toks.append(t_res)
                hi = x2st_hi[ot % 2]
                lo = x2st_lo[ot % 2]
                t_hi = sch.on("dve", (lambda e, ot=ot, hi=hi: e.tensor_copy(
                    hi, xs3[:, ot, :])), deps=TL(t_res, prev_hi[ot % 2]))
                t_lo = sch.on("dve", (lambda e, ot=ot, hi=hi, lo=lo:
                                      e.scalar_tensor_tensor(
                                          out=lo, in0=hi, scalar=-1.0,
                                          in1=xs3[:, ot, :], op0=AL.mult,
                                          op1=AL.add)), deps=[t_hi])
                s2 = sch.dma("act", (lambda e, ot=ot, seg=seg, hi=hi: e.dma_start(
                    out=x2hi_v[:, ot, seg], in_=hi)), deps=[t_hi])
                s3 = sch.dma("gps", (lambda e, ot=ot, seg=seg, lo=lo: e.dma_start(
                    out=x2lo_v[:, ot, seg], in_=lo)), deps=[t_lo])
                hi_sts.append(s2)
                lo_sts.append(s3)
                prev_hi[ot % 2] = s3
            s1 = sch.dma("sync", (lambda e, seg=seg: e.dma_start(
                out=x2scr_v[:, :, seg], in_=xs3[:, :, :])), deps=res_toks)
            x2_sts = [s1] + hi_sts + lo_sts
            tok_x2_store[s] = x2_sts
            if debug and s == 0:
                sch.dma("gps", lambda e: e.dma_start(
                    out=dbg["d_x2s"], in_=xs3[:, :, :]), deps=[x2_sts[-1]])
            if debug and s == 0:
                sch.dma("gps", lambda e: e.dma_start(
                    out=dbg.setdefault("d_x2_unused", dbg["d_ln"]),
                    in_=xs3[:, :, :]), deps=[x2_sts[-1]]) if False else None

            b1_sts = []
            prev_q = None
            for tb in range(4):
                tok0 = s * SEG + tb * 128
                tcs = slice(tok0, tok0 + 128)
                t_h = sch.dma("sync", (lambda e, tcs=tcs: e.dma_start_transpose(
                    b1_hi[:, :], x2hiscr[:, tcs])),
                    deps=TL(hi_sts[-1], hi_sts[-2], prev_q))
                t_l = sch.dma("act", (lambda e, tcs=tcs: e.dma_start_transpose(
                    b1_lo[:, :], x2loscr[:, tcs])),
                    deps=TL(lo_sts[-1], lo_sts[-2], prev_q))
                t_rb = sch.on("dve", lambda e: e.scalar_tensor_tensor(
                    out=b1_x2tm[:, :], in0=b1_hi[:, :], scalar=1.0,
                    in1=b1_lo[:, :], op0=AL.mult, op1=AL.add,
                    accum_out=stat_sc[:, 26:27]), deps=[t_h, t_l])
                t_s2_ = sch.on("dve", lambda e: (e.scalar_tensor_tensor(
                    out=b1_sq[:, :], in0=b1_hi[:, :], scalar=1.0, in1=b1_hi[:, :],
                    op0=AL.mult, op1=AL.mult,
                    accum_out=stat_sc[:, 27:28]), e.drain())[-1],
                    deps=[t_h, t_rb])

                def b1math(e):
                    sc = stat_sc
                    e.tensor_scalar(out=sc[:, 28:29], in0=sc[:, 26:27],
                                    scalar1=1.0 / D, scalar2=None, op0=AL.mult)
                    e.drain()
                    e.tensor_scalar(out=sc[:, 29:30], in0=sc[:, 28:29],
                                    scalar1=-1.0, scalar2=None, op0=AL.mult)
                    e.tensor_tensor(sc[:, 30:31], sc[:, 28:29], sc[:, 28:29],
                                    AL.mult)
                    e.drain()
                    e.scalar_tensor_tensor(
                        out=sc[:, 31:32], in0=sc[:, 27:28], scalar=1.0 / D,
                        in1=sc[:, 30:31], op0=AL.mult, op1=AL.subtract)
                    return e.drain()
                t_bm_ = sch.on("dve", b1math, deps=[t_rb, t_s2_])
                t_bsd = sch.on("act", lambda e: e.activation(
                    stat_sc[:, 30:31], stat_sc[:, 31:32], AF.Sqrt,
                    bias=eps_col[:, 0:1], scale=1.0), deps=TL(t_bm_, t_eps))
                t_br = sch.on("dve", lambda e: (e.reciprocal(
                    stat_sc[:, 31:32], stat_sc[:, 30:31]), e.drain())[-1],
                    deps=[t_bsd])
                t_c_ = sch.on("dve", lambda e: e.tensor_scalar(
                    out=b1_ctm[:, :], in0=b1_x2tm[:, :],
                    scalar1=stat_sc[:, 29:30], scalar2=None, op0=AL.add),
                    deps=[t_bm_])
                t_mxq = sch.on("dve", lambda e: (e.reduce_max(
                    stat_sc[:, 26:27], b1_ctm[:, :], axis=AX.X,
                    apply_absolute_value=True), e.drain())[-1], deps=[t_c_])

                def b1math2(e):
                    sc = stat_sc
                    e.tensor_tensor(sc[:, 27:28], sc[:, 31:32], sc[:, 26:27],
                                    AL.mult)
                    e.drain()
                    e.tensor_scalar_max(sc[:, 27:28], sc[:, 27:28], 1e-5)
                    e.drain()
                    e.reciprocal(sc[:, 28:29], sc[:, 27:28])
                    e.drain()
                    e.tensor_scalar(out=sc[:, 28:29], in0=sc[:, 28:29],
                                    scalar1=127.0, scalar2=None, op0=AL.mult)
                    e.drain()
                    e.tensor_tensor(sc[:, 29:30], sc[:, 31:32], sc[:, 28:29],
                                    AL.mult)
                    e.tensor_scalar(out=sc[:, 30:31], in0=sc[:, 27:28],
                                    scalar1=1.0 / 127.0, scalar2=None,
                                    op0=AL.mult)
                    return e.drain()
                t_bm2 = sch.on("dve", b1math2, deps=[t_mxq, t_br])
                t_t1 = sch.on("dve", lambda e: e.tensor_scalar(
                    out=b1_ctm[:, :], in0=b1_ctm[:, :], scalar1=stat_sc[:, 29:30],
                    scalar2=MAGICF, op0=AL.mult, op1=AL.add), deps=[t_bm2])
                t_xq1 = sch.on("dve", lambda e: e.tensor_scalar(
                    out=b1_hi[:, :], in0=b1_ctm[:, :], scalar1=MAGICF,
                    scalar2=stat_sc[:, 30:31], op0=AL.subtract, op1=AL.mult),
                    deps=[t_t1])
                if debug and s == 0 and tb == 0:
                    sch.dma("gps", lambda e: e.dma_start(
                        out=dbg["d_b1hi"], in_=b1_hi[:, :]), deps=[t_h, t_rb])
                    sch.dma("gps", lambda e: e.dma_start(
                        out=dbg["d_b1lo"], in_=b1_lo[:, :]), deps=[t_l, t_rb])
                    sch.dma("gps", lambda e: e.dma_start(
                        out=dbg["d_b1x2"], in_=b1_x2tm[:, :]), deps=[t_rb, t_c_])
                    sch.dma("gps", lambda e: e.dma_start(
                        out=dbg["d_b1st"], in_=stat_sc[:, :]), deps=[t_bm2])
                t_st_ = sch.dma("gps", (lambda e, tok0=tok0: e.dma_start(
                    out=xq1scr[8 + tok0:8 + tok0 + 128, :], in_=b1_hi[:, :])),
                    deps=[t_xq1])
                b1_sts.append(t_st_)
                prev_q = t_st_
                if debug and s == 0 and tb == 0:
                    sch.dma("gps", lambda e: e.dma_start(
                        out=dbg["d_xq1"], in_=b1_hi[:, :]), deps=[t_xq1])
            tok_b1_prev = b1_sts[-1]
            last_xq1 = b1_sts[-1]
            tok_seg_done = s1

        def ternarize(w_v, nchunk, ncols, wq_dst3, mw_col, dep0):
            tred = []
            prev = dep0
            for kc in range(nchunk):
                t_ld = sch.dma("sync", (lambda e, kc=kc, w_v=w_v, ncols=ncols:
                                        e.dma_start(out=stage_f[:, 0:ncols],
                                                    in_=w_v[:, kc, :])),
                               deps=TL(prev))
                t_r = sch.on("dve", (lambda e, kc=kc, ncols=ncols: (
                    e.tensor_reduce(
                        wstat[:, kc:kc + 1], stage_f[:, 0:ncols], axis=AX.X,
                        op=AL.add, apply_absolute_value=True),
                    e.drain())[-1]), deps=[t_ld])
                prev = t_r
                tred.append(t_r)
            t_r2 = sch.on("dve", (lambda e, nchunk=nchunk: (e.tensor_reduce(
                wstat[:, 36:37], wstat[:, 0:nchunk], axis=AX.X, op=AL.add),
                e.drain())[-1]), deps=tred)
            b, g = next_bank()
            t_sum = sch.on("pe", (lambda e, b=b: e.matmul(
                b[0:1, 0:1], onescf[:, :], wstat[:, 36:37], start=True,
                stop=True)), deps=TL(t_r2, t_of, g))

            def mmath(e, b=b, n=nchunk * ncols * 128):
                e.tensor_scalar(out=wstat[0:1, 37:38], in0=b[0:1, 0:1],
                                scalar1=1.0 / n, scalar2=None, op0=AL.mult)
                e.drain()
                e.tensor_scalar_max(wstat[0:1, 37:38], wstat[0:1, 37:38], 1e-5)
                e.drain()
                e.reciprocal(wstat[0:1, 38:39], wstat[0:1, 37:38])
                e.tensor_copy(wstat[0:1, 39:40], wstat[0:1, 37:38])
                return e.drain()
            t_mm_ = sch.on("dve", mmath, deps=[t_sum])
            set_guard(b, t_mm_)
            b2_, g2 = next_bank()
            t_bcm = sch.on("pe", (lambda e, b=b2_: (
                e.matmul(b[:, 0:1], ones1f[0:1, :], wstat[0:1, 38:39],
                         start=True, stop=True),
                e.matmul(b[:, 1:2], ones1f[0:1, :], wstat[0:1, 39:40],
                         start=True, stop=True))[-1]), deps=TL(t_mm_, g2))
            t_mwc = sch.on("dve", (lambda e, b=b2_, mw=mw_col: (
                e.tensor_copy(mwcol[:, mw:mw + 1], b[:, 0:1]),
                e.tensor_copy(mwcol[:, mw + 1:mw + 2], b[:, 1:2]),
                e.drain())[-1]), deps=[t_bcm])
            set_guard(b2_, t_mwc)
            prev = t_mwc
            last = None
            for kc in range(nchunk):
                t_ld = sch.dma("sync", (lambda e, kc=kc, w_v=w_v, ncols=ncols:
                                        e.dma_start(out=stage_f[:, 0:ncols],
                                                    in_=w_v[:, kc, :])),
                               deps=TL(prev))

                def tern(e, kc=kc, ncols=ncols, mw=mw_col, wq_dst3=wq_dst3):
                    e.tensor_scalar(out=stage_f[:, 0:ncols],
                                    in0=stage_f[:, 0:ncols],
                                    scalar1=mwcol[:, mw:mw + 1], scalar2=MAGICF,
                                    op0=AL.mult, op1=AL.add)
                    e.tensor_scalar(out=stage_f[:, 0:ncols],
                                    in0=stage_f[:, 0:ncols], scalar1=MAGICF,
                                    scalar2=None, op0=AL.subtract)
                    return e.tensor_scalar(out=wq_dst3[:, kc, :],
                                           in0=stage_f[:, 0:ncols], scalar1=1.0,
                                           scalar2=-1.0, op0=AL.min, op1=AL.max)
                t_t = sch.on("dve", tern, deps=[t_ld])
                prev = t_t
                last = t_t
            return last

        t_w1q = ternarize(w1T_v, 8, HD, w1qv, 0, tok_seg_done)
        t_w2q = ternarize(w2T_v, 32, D, w2qv, 2, t_w1q)
        t_tape = sch.on("dve", lambda e: e.tensor_scalar(
            out=taps_sb[:, :], in0=taps_sb[:, :], scalar1=mwcol[:, 1:2],
            scalar2=None, op0=AL.mult), deps=TL(t_w1q, t_consts))

        out_toks = []
        prev_w1_last = None
        prev_q2 = None
        prev_asm = t_w2q
        for s in range(NSEG):
            seg = slice(s * SEG, (s + 1) * SEG)
            t_ldx2 = sch.dma("sync", (lambda e, seg=seg: e.dma_start(
                out=xs3[:, :, :], in_=x2scr_v[:, :, seg])),
                deps=TL(prev_asm, tok_x2_store[s][0]))
            d2_toks = []
            blk_q2st = []
            for w in range(2):
                T0 = s * SEG + 256 * w
                t_lds = []
                for kc in range(8):
                    t_lw = sch.dma("sync" if kc % 2 == 0 else "act",
                                   (lambda e, kc=kc, T0=T0:
                                    e.dma_start_transpose(
                                        b2_xq1w[:, kc, :],
                                        xq1scr[T0:T0 + 272,
                                               128 * kc:128 * kc + 128])),
                                   deps=TL(last_xq1, prev_w1_last,
                                           t_tape if s == 0 and w == 0 else None))
                    t_lds.append(t_lw)
                if debug and s == 0 and w == 0:
                    sch.dma("gps", lambda e: e.dma_start(
                        out=dbg["d_xq1w"], in_=b2_xq1w[:, :, :]), deps=t_lds)
                sh_toks = []
                prev_chain = (None, None, None, None)
                for hc in range(32):
                    b, g = next_bank()

                    def mmw1(e, hc=hc, b=b):
                        last = None
                        for kc in range(8):
                            last = e.matmul(b[:, 0:258],
                                            w1qv[:, kc, 128 * hc:128 * hc + 128],
                                            b2_xq1w[:, kc, 7:265],
                                            start=(kc == 0), stop=(kc == 7))
                        return last
                    t_mm = sch.on("pe", mmw1, deps=TL(*t_lds, t_w1q, g))
                    t_ev = sch.on("act", (lambda e, b=b: e.copy(
                        hwin[:, :], b[:, 0:258])),
                        deps=TL(t_mm, prev_chain[1], prev_chain[2]))
                    set_guard(b, t_ev)
                    t_e1 = sch.on("act", (lambda e, hc=hc: e.activation(
                        c1win[:, :], hwin[:, 0:256], AF.Identity,
                        scale=taps_sb[:, hc:hc + 1])), deps=TL(t_ev, t_tape))
                    t_e2 = sch.on("dve", (lambda e, hc=hc: e.scalar_tensor_tensor(
                        out=c2win[:, :], in0=hwin[:, 1:257],
                        scalar=taps_sb[:, 32 + hc:33 + hc], in1=c1win[:, :],
                        op0=AL.mult, op1=AL.add)), deps=TL(t_e1, prev_chain[2]))
                    t_e3 = sch.on("dve", (lambda e, hc=hc: e.scalar_tensor_tensor(
                        out=c3win[:, :], in0=hwin[:, 2:258],
                        scalar=taps_sb[:, 64 + hc:65 + hc], in1=c2win[:, :],
                        op0=AL.mult, op1=AL.add)), deps=TL(t_e2, prev_chain[3]))
                    t_sl = sch.on("act", (lambda e, hc=hc: e.activation(
                        b2_sh3[:, hc, :], c3win[:, :], AF.Silu,
                        bias=cb_sb[:, hc:hc + 1], scale=1.0)),
                        deps=TL(t_e3, t_consts))
                    prev_chain = (t_ev, t_e2, t_e3, t_sl)
                    sh_toks.append(t_sl)
                    prev_w1_last = t_mm
                if debug and s == 0 and w == 0:
                    sch.dma("gps", lambda e: e.dma_start(
                        out=dbg["d_sh"], in_=b2_sh3[:, :, :]), deps=sh_toks)

                for tb in range(2):
                    blki = 2 * w + tb
                    tok0 = s * SEG + 128 * blki
                    banks8 = PB + [BM, BMT]
                    tr_toks = []
                    for bi in range(8):
                        b8 = banks8[bi]

                        def trx(e, bi=bi, b8=b8, tb=tb):
                            last = None
                            for j in range(4):
                                hc = bi * 4 + j
                                last = e.matmul(b8[:, 128 * j:128 * j + 128],
                                                b2_sh3[:, hc,
                                                       128 * tb:128 * tb + 128],
                                                identbf[:, :], start=True,
                                                stop=True)
                            return last
                        t_tr = sch.on("pe", trx, deps=TL(
                            *sh_toks, bank_guard[id(b8)], prev_q2))
                        tr_toks.append(t_tr)
                    mx_toks = []
                    for bi in range(8):
                        t_m = sch.on("dve", (lambda e, bi=bi, b8=banks8[bi]: (
                            e.reduce_max(
                                stat_sc[:, 2 + bi:3 + bi],
                                b8[:, 0:512], axis=AX.X,
                                apply_absolute_value=True),
                            e.drain() if bi == 7 else e.nop())[-1]),
                                     deps=[tr_toks[bi]])
                        mx_toks.append(t_m)
                    t_mxa = sch.on("dve", lambda e: (e.reduce_max(
                        stat_sc[:, 10:11], stat_sc[:, 2:10], axis=AX.X),
                        e.drain())[-1], deps=mx_toks)

                    def q2math(e, blki=blki):
                        sc = stat_sc
                        e.tensor_scalar_max(sc[:, 11:12], sc[:, 10:11], 1e-5)
                        e.drain()
                        e.reciprocal(sc[:, 12:13], sc[:, 11:12])
                        e.drain()
                        e.tensor_scalar(out=sc[:, 12:13], in0=sc[:, 12:13],
                                        scalar1=127.0, scalar2=None, op0=AL.mult)
                        e.scalar_tensor_tensor(
                            out=d2tab[:, blki:blki + 1], in0=sc[:, 11:12],
                            scalar=1.0 / 127.0, in1=mwcol[:, 3:4],
                            op0=AL.mult, op1=AL.mult)
                        return e.drain()
                    t_q2m = sch.on("dve", q2math, deps=TL(t_mxa, t_w2q))
                    d2_toks.append(t_q2m)
                    rp_toks = []
                    for bi in range(8):
                        t_r1 = sch.on("dve", (lambda e, bi=bi, b8=banks8[bi]:
                                              e.tensor_scalar(
                                                  out=ftmp[:, :], in0=b8[:, 0:512],
                                                  scalar1=stat_sc[:, 12:13],
                                                  scalar2=MAGICF, op0=AL.mult,
                                                  op1=AL.add)), deps=[t_q2m])
                        t_r2_ = sch.on("dve", (lambda e, bi=bi: e.tensor_scalar(
                            out=b2_xq2tm[:, 512 * bi:512 * bi + 512],
                            in0=ftmp[:, :], scalar1=MAGICF, scalar2=None,
                            op0=AL.subtract)), deps=[t_r1])
                        set_guard(banks8[bi], t_r2_)
                        rp_toks.append(t_r2_)
                    t_stq = sch.dma("gps", (lambda e, tok0=tok0: e.dma_start(
                        out=xq2scr[tok0:tok0 + 128, :], in_=b2_xq2tm[:, :])),
                        deps=rp_toks)
                    blk_q2st.append(t_stq)
                    prev_q2 = t_stq
                    if debug and s == 0 and blki == 0:
                        sch.dma("gps", lambda e: e.dma_start(
                            out=dbg["d_xq2"], in_=b2_xq2tm[:, :]), deps=rp_toks)

            bD, gD = next_bank()
            t_trd = sch.on("pe", (lambda e, b=bD: (
                [e.matmul(b[0:1, 128 * i:128 * i + 128], d2tab[:, i:i + 1],
                          identbf[:, :], start=True, stop=True)
                 for i in range(4)])[-1]), deps=TL(*d2_toks, gD))
            t_d2r = sch.on("dve", (lambda e, b=bD: (e.tensor_copy(
                d2row[0:1, :], b[0:1, 0:512]), e.drain())[-1]), deps=[t_trd])
            set_guard(bD, t_d2r)
            bD2, gD2 = next_bank()
            t_d2b = sch.on("pe", (lambda e, b=bD2: e.matmul(
                b[:, 0:512], ones1[0:1, :], d2row[0:1, :], start=True,
                stop=True)), deps=TL(t_d2r, gD2))
            t_d2bc = sch.on("dve", (lambda e, b=bD2: e.tensor_copy(
                d2bc[:, :], b[:, 0:512])), deps=[t_d2b])
            set_guard(bD2, t_d2bc)

            banks8 = PB + [BM, BMT]
            mm_toks = []
            ld_toks = []
            for kc in range(32):
                buf = b2_xq2b[kc % 2]
                t_lq = sch.dma("sync" if kc % 2 == 0 else "act",
                               (lambda e, kc=kc, buf=buf, seg=seg:
                                e.dma_start_transpose(
                                    buf,
                                    xq2scr[seg, 128 * kc:128 * kc + 128])),
                               deps=TL(blk_q2st[-1],
                                       mm_toks[kc - 2] if kc >= 2 else None))
                ld_toks.append(t_lq)
                depk = [t_lq, t_w2q]
                if kc == 0:
                    depk += [bank_guard[id(b_)] for b_ in banks8]
                    depk += [t_d2bc]

                def mmw2(e, kc=kc, buf=buf):
                    last = None
                    for ot in range(8):
                        last = e.matmul(banks8[ot][:, 0:512],
                                        w2qv[:, kc, 128 * ot:128 * ot + 128],
                                        buf, start=(kc == 0), stop=(kc == 31))
                    return last
                t_mmq = sch.on("pe", mmw2, deps=TL(*depk))
                mm_toks.append(t_mmq)
            asm_toks = []
            for ot in range(8):
                t_as1 = sch.on("dve", (lambda e, ot=ot, b=banks8[ot]:
                                       e.tensor_tensor(ftmp[:, :], b[:, 0:512],
                                                       d2bc[:, :], AL.mult)),
                               deps=TL(mm_toks[-1], t_d2bc,
                                       asm_toks[-1] if asm_toks else None))
                t_as2 = sch.on("dve", (lambda e, ot=ot: e.tensor_tensor(
                    xs3[:, ot, :], xs3[:, ot, :], ftmp[:, :], AL.add)),
                    deps=[t_as1, t_ldx2])
                set_guard(banks8[ot], t_as2)
                asm_toks.append(t_as2)
            t_out = sch.dma("sync", (lambda e, seg=seg: e.dma_start(
                out=outT_v[:, :, seg], in_=xs3[:, :, :])), deps=asm_toks)
            out_toks.append(t_out)
            prev_asm = t_out

        if debug:
            sch.dma("gps", lambda e: e.dma_start(
                out=dbg["d_xq1s"], in_=xq1scr[8:264, :]), deps=[last_xq1])
            sch.dma("gps", lambda e: e.dma_start(
                out=dbg["d_xq2s"], in_=xq2scr[0:128, :]), deps=[out_toks[-1]])
            t_dd2 = sch.on("dve", lambda e: e.tensor_copy(
                stat_sc[:, 0:4], d2tab[:, :]), deps=[out_toks[-1]])
            sch.dma("gps", lambda e: e.dma_start(
                out=dbg["d_d2"], in_=stat_sc[:, 0:4]), deps=[t_dd2])
        sch.final_wait("sync")

        blk = st.enter_context(nc.Block())
        st.enter_context(nc.allow_low_precision(
            reason="bf16 operands with fp32 psum accumulation are within this "
                   "problem's stated tolerance; precision-sensitive steps "
                   "(round/clip/stats/residual) are kept in fp32 explicitly"))

        @blk.tensor
        def _(eng):
            sch.emit("pe", eng)

        @blk.vector
        def _(eng):
            sch.emit("dve", eng)

        @blk.scalar
        def _(eng):
            sch.emit("act", eng)

        @blk.gpsimd
        def _(eng):
            sch.emit("gps", eng)

        @blk.sync
        def _(eng):
            sch.emit("sync", eng)

    return nc



def _consts():
    ident = np.eye(128)
    i2 = np.concatenate([np.eye(64), np.eye(64)], 0)
    low = np.tril(np.ones((64, 64)), -1)
    up = np.triu(np.ones((64, 64)), 1)
    low2 = np.concatenate([low, low], 0)
    up2 = np.concatenate([up, up], 0)
    neg = np.where(np.concatenate([up, up], 0) > 0, -1e30, 0.0)
    ones2 = np.zeros((128, 2))
    ones2[:, 0] = 1.0
    ones1 = np.ones((1, 128))
    hsum = np.zeros((128, 2))
    hsum[0:64, 0] = 1.0
    hsum[64:128, 1] = 1.0
    exp2 = np.zeros((2, 128))
    exp2[0, 0:64] = 1.0
    exp2[1, 64:128] = 1.0
    return dict(
        identbf=np.ascontiguousarray(ident.astype(BF)),
        ident16f=np.ascontiguousarray(np.eye(16, dtype=np.float32)),
        i2bf=np.ascontiguousarray(i2.astype(BF)),
        masklow=np.ascontiguousarray(np.tile(low2, (1, 8)).astype(BF)),
        maskup=np.ascontiguousarray(np.tile(up2, (1, 8)).astype(BF)),
        maskneg=np.ascontiguousarray(np.tile(neg, (1, 8)).astype(BF)),
        onesc=np.ascontiguousarray(ones2.astype(BF)),
        ones1=np.ascontiguousarray(ones1.astype(BF)),
        ones1f=np.ascontiguousarray(ones1.astype(np.float32)),
        hsum2=np.ascontiguousarray(hsum.astype(BF)),
        expander2=np.ascontiguousarray(exp2.astype(BF)),
    )


_NC_CACHE = {}


def kernel(L0_attn_x, L0_attn_Wq, L0_attn_Wk, L0_attn_Wv, L0_attn_Wo,
           L0_attn_alpha_raw, L0_attn_beta_raw, L0_ffn_w1, L0_ffn_conv_w,
           L0_ffn_conv_b, L0_ffn_w2, _debug=False, _cores=8):
    f32 = np.float32
    wqT = np.ascontiguousarray(np.asarray(L0_attn_Wq, f32).T)
    wkT = np.ascontiguousarray(np.asarray(L0_attn_Wk, f32).T)
    wvT = np.ascontiguousarray(np.asarray(L0_attn_Wv, f32).T)
    woT = np.ascontiguousarray(np.asarray(L0_attn_Wo, f32).T)
    w1T = np.ascontiguousarray(np.asarray(L0_ffn_w1, f32).T)
    w2T = np.ascontiguousarray(np.asarray(L0_ffn_w2, f32).T)
    cw = np.asarray(L0_ffn_conv_w, f32)[:, 0, :]
    taps = np.zeros((128, 96), f32)
    for k in range(3):
        taps[:, 32 * k:32 * k + 32] = cw[:, k].reshape(32, 128).T
    convb = np.ascontiguousarray(np.asarray(L0_ffn_conv_b, f32).reshape(32, 128).T)
    abraw = np.ascontiguousarray(np.stack(
        [np.asarray(L0_attn_alpha_raw, f32).reshape(16),
         np.asarray(L0_attn_beta_raw, f32).reshape(16)], 1))
    consts = _consts()

    key = ("dbg" if _debug else "main")
    if key not in _NC_CACHE:
        _NC_CACHE[key] = build_nc(debug=_debug)
    nc = _NC_CACHE[key]

    x = np.asarray(L0_attn_x, f32)
    in_maps = []
    for b in range(_cores):
        m = dict(xT=np.ascontiguousarray(x[b].T), wqT=wqT, wkT=wkT, wvT=wvT,
                 woT=woT, w1T=w1T, w2T=w2T, taps=taps, convb=convb, abraw=abraw)
        m.update(consts)
        in_maps.append(m)
    res = run_bass_kernel_spmd(nc, in_maps, core_ids=list(range(_cores)))
    outs = [np.ascontiguousarray(res.results[b]["outT"].T) for b in range(_cores)]
    if _cores < 8:
        outs = outs + [outs[0]] * (8 - _cores)
    full = np.stack(outs[:8])
    if _debug:
        return full, res.results
    return full


# version 10
# speedup vs baseline: 1.0346x; 1.0346x over previous
import os
import numpy as np
import ml_dtypes
import contextlib
import concourse.bass as bass
import concourse.mybir as mybir
from concourse.bass_utils import run_bass_kernel_spmd

F32 = mybir.dt.float32
BF16 = mybir.dt.bfloat16
AL = mybir.AluOpType
AX = mybir.AxisListType
AF = mybir.ActivationFunctionType
BF = ml_dtypes.bfloat16
MAGICF = float(np.float32(1.5 * 2.0 ** 23))

T = 4096
D = 1024
HD = 4096
SEG = 512
NSEG = T // SEG
CH = 8
NP = 8
EPS = 1e-5


class Sched:
    ENGINES = ("pe", "dve", "act", "gps", "sync")

    def __init__(self, nc, st):
        self.serial = int(os.environ.get("KSERIAL", "0"))
        self.steps = {e: [] for e in self.ENGINES}
        self.count = {e: 0 for e in self.ENGINES}
        self.dma_count = {e: 0 for e in ("sync", "gps", "act")}
        self.sem = {e: st.enter_context(nc.semaphore(f"s_{e}")) for e in self.ENGINES}
        self.dsem = {e: st.enter_context(nc.semaphore(f"d_{e}"))
                     for e in ("sync", "gps", "act")}

    def _allbar(self, eng, isdma=False):
        m = self.serial
        deps = []
        engs = ()
        dmas = ()
        if m == 1:
            engs = tuple(e for e in self.ENGINES if e != eng)
            dmas = ("sync", "gps", "act")
        elif m == 2:
            if isdma:
                engs = tuple(e for e in self.ENGINES if e != eng)
            else:
                dmas = ("sync", "gps", "act")
        elif m == 3 and not isdma:
            if eng == "pe":
                engs = ("dve",)
            elif eng == "dve":
                engs = ("pe",)
        elif m == 4 and not isdma:
            if eng == "act":
                engs = ("dve", "pe")
            elif eng in ("dve", "pe"):
                engs = ("act",)
        for e in engs:
            if self.count[e]:
                deps.append((self.sem[e], self.count[e]))
        for q in dmas:
            if self.dma_count[q]:
                deps.append((self.dsem[q], self.dma_count[q]))
        return deps

    def on(self, eng, fn, deps=()):
        deps = tuple(d for d in deps if d)
        if self.serial:
            deps = deps + tuple(self._allbar(eng))
        self.count[eng] += 1
        tok = (self.sem[eng], self.count[eng])
        self.steps[eng].append(("c", fn, deps, tok))
        return tok

    def dma(self, queue, fn, deps=()):
        deps = tuple(d for d in deps if d)
        if self.serial:
            deps = deps + tuple(self._allbar(queue, isdma=True))
        self.dma_count[queue] += 16
        tok = (self.dsem[queue], self.dma_count[queue])
        self.steps[queue].append(("d", fn, deps, tok))
        return tok

    def emit(self, eng, engobj):
        waited = {}
        for kind, fn, deps, tok in self.steps[eng]:
            if kind == "d":
                prior = tok[1] - 16
                if prior > 0:
                    deps = deps + ((tok[0], prior),)
            for sem, val in deps:
                key = id(sem)
                if waited.get(key, -1) >= val:
                    continue
                engobj.wait_ge(sem, val)
                waited[key] = val
            inst = fn(engobj)
            assert inst is not None, f"step on {eng} returned None"
            inst.then_inc(tok[0], 16 if kind == "d" else 1)

    def final_wait(self, eng):
        def fn(engobj):
            for q in ("sync", "gps", "act"):
                if self.dma_count[q]:
                    engobj.wait_ge(self.dsem[q], self.dma_count[q])
            for e in self.ENGINES:
                if e != eng and self.count[e]:
                    engobj.wait_ge(self.sem[e], self.count[e])
            return engobj.nop()
        self.count[eng] += 1
        self.steps[eng].append(("c", fn, (), (self.sem[eng], self.count[eng])))


def TL(*ts):
    return [t for t in ts if t is not None]


def build_nc(debug=False):
    nc = bass.Bass("TRN2", target_bir_lowering=False)

    xT = nc.dram_tensor("xT", [D, T], F32, kind="ExternalInput").ap()
    wqT = nc.dram_tensor("wqT", [D, D], F32, kind="ExternalInput").ap()
    wkT = nc.dram_tensor("wkT", [D, D], F32, kind="ExternalInput").ap()
    wvT = nc.dram_tensor("wvT", [D, D], F32, kind="ExternalInput").ap()
    woT = nc.dram_tensor("woT", [D, D], F32, kind="ExternalInput").ap()
    w1T = nc.dram_tensor("w1T", [D, HD], F32, kind="ExternalInput").ap()
    w2T = nc.dram_tensor("w2T", [HD, D], F32, kind="ExternalInput").ap()
    taps_in = nc.dram_tensor("taps", [128, 96], F32, kind="ExternalInput").ap()
    cb_in = nc.dram_tensor("convb", [128, 32], F32, kind="ExternalInput").ap()
    ab_in = nc.dram_tensor("abraw", [16, 2], F32, kind="ExternalInput").ap()
    c_identbf = nc.dram_tensor("identbf", [128, 128], BF16, kind="ExternalInput").ap()
    c_ident16 = nc.dram_tensor("ident16f", [16, 16], F32, kind="ExternalInput").ap()
    c_i2bf = nc.dram_tensor("i2bf", [128, 64], BF16, kind="ExternalInput").ap()
    c_masklow = nc.dram_tensor("masklow", [128, 512], BF16, kind="ExternalInput").ap()
    c_maskup = nc.dram_tensor("maskup", [128, 512], BF16, kind="ExternalInput").ap()
    c_maskneg = nc.dram_tensor("maskneg", [128, 512], BF16, kind="ExternalInput").ap()
    c_ones = nc.dram_tensor("onesc", [128, 2], BF16, kind="ExternalInput").ap()
    c_ones1 = nc.dram_tensor("ones1", [1, 128], BF16, kind="ExternalInput").ap()
    c_ones1f = nc.dram_tensor("ones1f", [1, 128], F32, kind="ExternalInput").ap()
    c_hsum = nc.dram_tensor("hsum2", [128, 2], BF16, kind="ExternalInput").ap()
    c_exp2 = nc.dram_tensor("expander2", [2, 128], BF16, kind="ExternalInput").ap()

    outT = nc.dram_tensor("outT", [D, T], F32, kind="ExternalOutput").ap()
    x2scr = nc.dram_tensor("x2scr", [D, T], F32, kind="Internal").ap()
    x2hiscr = nc.dram_tensor("x2hiscr", [D, T], BF16, kind="Internal").ap()
    x2loscr = nc.dram_tensor("x2loscr", [D, T], BF16, kind="Internal").ap()
    xq1scr = nc.dram_tensor("xq1scr", [T + 16, D], BF16, kind="Internal").ap()
    xq2scr = nc.dram_tensor("xq2scr", [T, HD], BF16, kind="Internal").ap()

    dbg = {}
    if debug:
        for nm, shp in [("d_q", [128, NP, SEG]), ("d_k", [128, NP, SEG]),
                        ("d_v", [128, NP, SEG]), ("d_kh", [128, NP, SEG]),
                        ("d_ln", [128, 8, SEG]), ("d_attn", [128, NP, SEG]),
                        ("d_ab", [128, 16]), ("d_r0", [128, 512]),
                        ("d_x0", [128, 512]), ("d_m1", [128, 512]),
                        ("d_att0", [128, 512]), ("d_n1", [128, 512]),
                        ("d_x1", [128, 512]), ("d_m2", [128, 512]),
                        ("d_x2s", [128, 8, SEG]), ("d_b1x2", [128, 1024]),
                        ("d_b1hi", [128, 1024]), ("d_b1lo", [128, 1024]),
                        ("d_xq1s", [256, 1024]), ("d_xq2s", [128, HD]),
                        ("d_xq1w", [128, 8, 272]),
                        ("d_d2", [128, 4]),
                        ("d_b1st", [128, 16]),
                        ("d_att1", [128, 512]), ("d_co1", [128, 512]),
                        ("d_khtm0", [128, 512]),
                        ("d_xq1", [128, D]), ("d_sh", [128, 32, 256]),
                        ("d_xq2", [128, HD])]:
            dbg[nm] = nc.dram_tensor(nm, shp, F32, kind="ExternalOutput").ap()

    st = contextlib.ExitStack()
    with st:
        def sb(name, shape, dt=F32):
            return st.enter_context(nc.sbuf_tensor(name, shape, dt))

        def mkpsum(name):
            return st.enter_context(nc.psum_tensor(name, [128, 512], F32))

        wU = sb("wU", [128, 8, 4 * D], BF16)
        w_attn = [wU[:, :, i * D:(i + 1) * D] for i in range(4)]
        w1qv = wU.rearrange("p a b -> p (a b)").rearrange("p (a b) -> p a b", a=8)

        u2 = sb("u2", [128, 32 * D], BF16)
        w2qv = u2.rearrange("p (a b) -> p a b", a=32)
        _o = [0]

        def carve2(n):
            v = u2[:, _o[0]:_o[0] + n]
            _o[0] += n
            return v
        af_flat = carve2(8 * SEG)
        attn_fm3 = af_flat.rearrange("p (a b) -> p a b", a=8)
        solN_P = [{k: carve2(512) for k in (1, 2, 4, 8, 16)} for _ in range(2)]
        solNT_P = [{k: carve2(512) for k in (1, 2, 4, 8, 16, 32)} for _ in range(2)]
        khtm_P = [carve2(512) for _ in range(2)]
        vtm_P = [carve2(512) for _ in range(2)]
        r_stk_P = [carve2(512) for _ in range(2)]
        ya_P = [carve2(512) for _ in range(2)]
        yb_P = [carve2(512) for _ in range(2)]
        pexp_P = [carve2(512) for _ in range(2)]
        pn_P = [carve2(512) for _ in range(2)]
        ptb_P = [carve2(512) for _ in range(2)]
        att_sb_P = [carve2(512) for _ in range(2)]
        co_bf_P = [carve2(512) for _ in range(2)]
        mA = carve2(512)
        mB = carve2(512)
        mtA = carve2(512)
        mtB = carve2(512)
        rows_bf = carve2(1024)
        norm2 = af_flat[0:2, 0:4096]
        _st0 = _o[0]
        x2st_hi = [carve2(512) for _ in range(2)]
        x2st_lo = [carve2(512) for _ in range(2)]
        b1_hi = u2[:, _st0:_st0 + 1024]
        b1_lo = u2[:, _st0 + 1024:_st0 + 2048]
        b1_sq = carve2(1024)
        assert _o[0] <= 32 * D, _o[0]

        u3 = sb("u3", [128, 6144], F32)
        xs3 = u3[:, 0:4096].rearrange("p (a b) -> p a b", a=8)
        sm_f_P = [u3[:, 4096:4608], u3[:, 4608:5120]]
        ftmp = u3[:, 4096:4608]
        rows_f = u3[:, 4096:5120]
        b1_x2tm = u3[:, 5120:6144]
        b1_ctm = b1_x2tm
        hwin = u3[:, 5120:5378]
        c1win = u3[:, 5378:5634]
        c2win = c1win
        c3win = u3[:, 5634:5890]

        u4 = sb("u4", [128, 24576], BF16)
        xbf3 = u4[:, 0:4096].rearrange("p (a b) -> p a b", a=8)
        qf3 = u4[:, 4096:8192].rearrange("p (a b) -> p a b", a=8)
        kf3 = u4[:, 8192:12288].rearrange("p (a b) -> p a b", a=8)
        vf3 = u4[:, 12288:16384].rearrange("p (a b) -> p a b", a=8)
        kh3 = u4[:, 16384:20480].rearrange("p (a b) -> p a b", a=8)
        kha3 = u4[:, 20480:24576].rearrange("p (a b) -> p a b", a=8)
        sqscr = u4[:, 24064:24576]
        sqscr = carve2(512)
        assert _o[0] <= 32 * D, _o[0]
        zmar = u4[:, 3072:4096]
        b2_sh3 = u4[:, 0:8192].rearrange("p (a b) -> p a b", a=32)
        b2_xq1w = u4[:, 8192:10368].rearrange("p (a b) -> p a b", a=8)
        b2_xq2b = [u4[:, 10368 + i * 512:10368 + (i + 1) * 512] for i in range(2)]
        b2_xq2tm = u4[:, 11392:15488]
        b2_stage = u4[:, 15488:23680]
        stage_f = b2_stage.bitcast(F32)

        identbf = sb("identbf_s", [128, 128], BF16)
        ident16 = sb("ident16_s", [16, 16], F32)
        i2bf = sb("i2bf_s", [128, 64], BF16)
        masklow = sb("masklow_s", [128, 512], BF16)
        maskup = sb("maskup_s", [128, 512], BF16)
        maskneg = sb("maskneg_s", [128, 512], BF16)
        onesc = sb("onesc_s", [128, 2], BF16)
        onescf = sb("onescf_s", [128, 1], F32)
        ones1 = sb("ones1_s", [1, 128], BF16)
        ones1f = sb("ones1f_s", [1, 128], F32)
        hsum2 = sb("hsum2_s", [128, 2], BF16)
        exp2 = sb("exp2_s", [2, 128], BF16)
        ab_sb = sb("ab_sb", [16, 2], F32)
        ab_row = sb("ab_row", [1, 32], F32)
        atab = sb("atab", [128, NP], F32)
        btab = sb("btab", [128, NP], F32)
        stat_sc = sb("stat_sc", [128, 32], F32)
        taps_sb = sb("taps_sb", [128, 96], F32)
        cb_sb = sb("cb_sb", [128, 32], F32)
        wstat = sb("wstat", [128, 40], F32)
        mwcol = sb("mwcol", [128, 4], F32)
        d2tab = sb("d2tab", [128, 4], BF16)
        d2row = sb("d2row", [1, 512], BF16)
        d2bc = sb("d2bc", [128, 512], BF16)
        eps_col = sb("eps_col", [128, 1], F32)
        magic_col = sb("magic_col", [128, 1], F32)

        PB = [mkpsum(f"pb{i}") for i in range(6)]
        BM = mkpsum("pbM")
        BMT = mkpsum("pbMT")

        sch = Sched(nc, st)
        bank_guard = {id(b): None for b in PB + [BM, BMT]}
        bank_i = [0]

        def next_bank():
            b = PB[bank_i[0] % 6]
            bank_i[0] += 1
            return b, bank_guard[id(b)]

        def set_guard(b, tok):
            bank_guard[id(b)] = tok

        def pair_mms(e, b, lhs3, rhs3, lcol, rcol):
            last = None
            for p in range(NP):
                lh = lhs3[:, p, lcol] if lcol is not None else lhs3[:, p, :]
                rh = rhs3[:, p, rcol] if rcol is not None else rhs3[:, p, :]
                last = e.matmul(b[0:64, 64 * p:64 * p + 64], lh[0:64], rh[0:64],
                                start=True, stop=True, tile_position=(0, 0))
                last = e.matmul(b[64:128, 64 * p:64 * p + 64], lh[64:128],
                                rh[64:128], start=True, stop=True,
                                tile_position=(64, 64))
            return last

        def pair_trans(e, b, src3, col):
            last = None
            for p in range(NP):
                sp = src3[:, p, col] if col is not None else src3[:, p, :]
                last = e.matmul(b[0:64, 64 * p:64 * p + 64], sp[0:64],
                                identbf[0:64, 0:64], start=True, stop=True,
                                tile_position=(0, 0))
                last = e.matmul(b[64:128, 64 * p:64 * p + 64], sp[64:128],
                                i2bf[64:128, :], start=True, stop=True,
                                tile_position=(64, 64))
            return last

        cl = []
        for dst, src in ((identbf, c_identbf), (ident16, c_ident16),
                         (i2bf, c_i2bf), (masklow, c_masklow), (maskup, c_maskup),
                         (maskneg, c_maskneg), (onesc, c_ones), (ones1, c_ones1),
                         (ones1f, c_ones1f), (hsum2, c_hsum), (exp2, c_exp2),
                         (ab_sb, ab_in), (taps_sb, taps_in), (cb_sb, cb_in)):
            cl.append(sch.dma("sync", (lambda e, dst=dst, src=src:
                                       e.dma_start(out=dst[:], in_=src))))
        t_consts = cl[-1]
        t_w = []
        for i, w in enumerate((wqT, wkT, wvT, woT)):
            wv_ = w.rearrange("(a p) o -> p a o", p=128)
            t_w.append(sch.dma("gps", (lambda e, i=i, wv_=wv_:
                                       e.dma_start(out=w_attn[i], in_=wv_))))
        t_of = sch.on("dve", lambda e: (e.memset(onescf[:], 1.0),
                                        e.drain())[-1])
        t_eps = sch.on("dve", lambda e: (e.memset(eps_col[:], EPS),
                                         e.drain())[-1])
        t_mgc = sch.on("dve", lambda e: (e.memset(magic_col[:], MAGICF),
                                         e.drain())[-1])
        t_zm = sch.on("dve", lambda e: e.memset(zmar[0:16, :], 0.0))
        t_zs1 = sch.dma("sync", lambda e: e.dma_start(out=xq1scr[0:8, :],
                                                      in_=zmar[0:8, :]), deps=[t_zm])
        t_zs2 = sch.dma("sync", lambda e: e.dma_start(out=xq1scr[T + 8:T + 16, :],
                                                      in_=zmar[8:16, :]), deps=[t_zm])

        t_sig = sch.on("act", lambda e: e.activation(ab_sb[:], ab_sb[:], AF.Sigmoid),
                       deps=[cl[11]])
        t_s1 = sch.on("dve", lambda e: (e.tensor_scalar(
            out=ab_sb[:, 0:1], in0=ab_sb[:, 0:1], scalar1=0.5, scalar2=None,
            op0=AL.mult), e.drain())[-1], deps=[t_sig])
        t_s2 = sch.on("dve", lambda e: (e.tensor_scalar(
            out=ab_sb[:, 1:2], in0=ab_sb[:, 1:2], scalar1=2.0, scalar2=None,
            op0=AL.mult), e.drain())[-1], deps=[t_sig])
        bA, gA = next_bank()
        t_abt = sch.on("pe", (lambda e, b=bA: (
            e.matmul(b[0:1, 0:16], ab_sb[0:16, 0:1], ident16[:, :],
                     start=True, stop=True),
            e.matmul(b[0:1, 16:32], ab_sb[0:16, 1:2], ident16[:, :],
                     start=True, stop=True))[-1]), deps=TL(t_s1, t_s2, gA, cl[1]))
        t_abr = sch.on("dve", (lambda e, b=bA: e.tensor_copy(ab_row[0:1, 0:16],
                                                             b[0:1, 0:16])),
                       deps=[t_abt])
        t_abr2 = sch.on("dve", (lambda e, b=bA: (e.tensor_copy(
            ab_row[0:1, 16:32], b[0:1, 16:32]), e.drain())[-1]),
                        deps=[t_abt])
        set_guard(bA, t_abr2)
        bA, gA = next_bank()

        def mk_ab(e, b=bA):
            last = None
            for p in range(NP):
                for h in range(2):
                    for which in range(2):
                        rhs = ab_row[0:1, 16 * which + 2 * p + h:
                                     16 * which + 2 * p + h + 1]
                        tp = (0, 0) if h == 0 else (0, 64)
                        last = e.matmul(b[64 * h:64 * h + 64,
                                          8 * which + p:8 * which + p + 1],
                                        ones1f[0:1, 0:64], rhs,
                                        start=True, stop=True, tile_position=tp)
            return last
        t_abm = sch.on("pe", mk_ab, deps=TL(t_abr, t_abr2, gA, cl[8]))

        def ev_ab(e, b=bA):
            e.tensor_copy(atab[:, :], b[:, 0:8])
            e.tensor_copy(btab[:, :], b[:, 8:16])
            return e.drain()
        t_abe = sch.on("dve", ev_ab, deps=[t_abm])
        set_guard(bA, t_abe)

        if debug:
            t_dab = sch.on("dve", lambda e: (
                e.tensor_copy(stat_sc[:, 0:8], atab[:, :]),
                e.tensor_copy(stat_sc[:, 8:16], btab[:, :]))[-1], deps=[t_abe])
            sch.dma("gps", lambda e: e.dma_start(out=dbg["d_ab"],
                                                 in_=stat_sc[:, 0:16]),
                    deps=[t_dab])
        t_mz = sch.on("dve", lambda e: (
            e.memset(mA[:], 0.0), e.memset(mtA[:], 0.0))[-1])

        xT_v = xT.rearrange("(a p) t -> p a t", p=128)
        x2scr_v = x2scr.rearrange("(a p) t -> p a t", p=128)
        x2hi_v = x2hiscr.rearrange("(a p) t -> p a t", p=128)
        x2lo_v = x2loscr.rearrange("(a p) t -> p a t", p=128)
        outT_v = outT.rearrange("(a p) t -> p a t", p=128)
        w1T_v = w1T.rearrange("(a p) o -> p a o", p=128)
        w2T_v = w2T.rearrange("(a p) o -> p a o", p=128)

        RR = lambda t: t.rearrange("p (a b) -> p a b", a=NP)
        solNv_P = [{k: RR(v) for k, v in d.items()} for d in solN_P]
        solNTv_P = [{k: RR(v) for k, v in d.items()} for d in solNT_P]
        n2v = norm2.rearrange("p (a b) -> p a b", a=8)

        tok_seg_done = None
        tok_x2_store = {}
        tok_m_ready = t_mz
        last_xq1 = t_zs2
        tok_b1_prev = None

        for s in range(NSEG):
            seg = slice(s * SEG, (s + 1) * SEG)
            m_base, m_run = (mA, mB) if s % 2 == 0 else (mB, mA)
            mt_base, mt_run = (mtA, mtB) if s % 2 == 0 else (mtB, mtA)
            m_runv = m_run.rearrange("p (a b) -> p a b", a=NP)
            mt_runv = mt_run.rearrange("p (a b) -> p a b", a=NP)

            t_ldx = sch.dma("sync", (lambda e, seg=seg: e.dma_start(
                out=xs3[:, :, :], in_=xT_v[:, :, seg])), deps=TL(tok_seg_done))

            def cast_x(e):
                last = None
                for c in range(8):
                    last = e.tensor_copy(xbf3[:, c, :], xs3[:, c, :])
                return last
            t_cast = sch.on("dve", cast_x, deps=TL(
                t_ldx, t_zs1 if s == 0 else None, t_zs2 if s == 0 else None))

            bS0, g0 = next_bank()
            bS1, g1 = next_bank()
            st_last = None
            for c in range(8):
                t_sqc = sch.on("act", (lambda e, c=c: e.activation(
                    sqscr[:, :], xbf3[:, c, :], AF.Square)),
                    deps=TL(t_cast, st_last))
                sch.on("pe", (lambda e, c=c, b=bS0: e.matmul(
                    b[0:1, 0:512], onesc[:, 0:1], xbf3[:, c, :],
                    start=(c == 0), stop=(c == 7))),
                    deps=TL(t_cast, t_consts, g0 if c == 0 else None, *t_w))
                st_last = sch.on("pe", (lambda e, c=c, b=bS1: e.matmul(
                    b[0:1, 0:512], onesc[:, 0:1], sqscr[:, :],
                    start=(c == 0), stop=(c == 7))),
                    deps=TL(t_sqc, g1 if c == 0 else None))
            t_stat = st_last

            def rowmath(e, b0=bS0, b1=bS1):
                e.tensor_scalar(out=rows_f[0:1, 0:512], in0=b0[0:1, 0:512],
                                scalar1=1.0 / D, scalar2=None, op0=AL.mult)
                e.tensor_scalar(out=rows_bf[0:1, 0:512], in0=rows_f[0:1, 0:512],
                                scalar1=-1.0, scalar2=None, op0=AL.mult)
                e.tensor_tensor(rows_f[0:1, 512:1024], rows_f[0:1, 0:512],
                                rows_f[0:1, 0:512], AL.mult)
                e.tensor_scalar(out=rows_f[0:1, 0:512], in0=b1[0:1, 0:512],
                                scalar1=1.0 / D, scalar2=None, op0=AL.mult)
                return e.tensor_tensor(rows_f[0:1, 0:512], rows_f[0:1, 0:512],
                                       rows_f[0:1, 512:1024], AL.subtract)
            t_rm = sch.on("dve", rowmath, deps=[t_stat])
            t_sd = sch.on("act", lambda e: e.activation(
                rows_f[0:1, 512:1024], rows_f[0:1, 0:512], AF.Sqrt,
                bias=eps_col[0:1, 0:1], scale=1.0), deps=TL(t_rm, t_eps))
            t_rr = sch.on("dve", lambda e: e.reciprocal(
                rows_bf[0:1, 512:1024], rows_f[0:1, 512:1024]), deps=[t_sd])
            set_guard(bS0, t_rm)
            set_guard(bS1, t_rm)

            bB0, g0 = next_bank()
            bB1, g1 = next_bank()
            t_bc = sch.on("pe", (lambda e, b0=bB0, b1=bB1: (
                e.matmul(b0[:, 0:512], ones1[0:1, :], rows_bf[0:1, 0:512],
                         start=True, stop=True),
                e.matmul(b1[:, 0:512], ones1[0:1, :], rows_bf[0:1, 512:1024],
                         start=True, stop=True))[-1]), deps=TL(t_rm, t_rr, g0, g1))

            def ln_apply(e, b0=bB0, b1=bB1):
                last = None
                for c in range(8):
                    e.tensor_tensor(ftmp[:, :], xs3[:, c, :], b0[:, 0:512], AL.add)
                    last = e.tensor_tensor(xbf3[:, c, :], ftmp[:, :], b1[:, 0:512],
                                           AL.mult)
                return last
            t_ln = sch.on("dve", ln_apply, deps=TL(t_bc, t_stat))
            set_guard(bB0, t_ln)
            set_guard(bB1, t_ln)
            if debug and s == 0:
                sch.dma("gps", lambda e: e.dma_start(out=dbg["d_ln"],
                                                     in_=xbf3[:, :, :]),
                        deps=[t_ln])

            evac_toks = []
            for wi, dst in ((0, qf3), (1, kf3), (2, vf3)):
                for ot in range(8):
                    b, g = next_bank()

                    def mmproj(e, wi=wi, ot=ot, b=b):
                        last = None
                        for kc in range(8):
                            last = e.matmul(b[:, 0:512],
                                            w_attn[wi][:, kc, 128 * ot:128 * ot + 128],
                                            xbf3[:, kc, :],
                                            start=(kc == 0), stop=(kc == 7))
                        return last
                    t_mm = sch.on("pe", mmproj, deps=TL(t_ln, g, *t_w))
                    t_ev = sch.on("dve", (lambda e, dst=dst, ot=ot, b=b:
                                          e.tensor_copy(dst[:, ot, :], b[:, 0:512])),
                                  deps=[t_mm])
                    set_guard(b, t_ev)
                    evac_toks.append(t_ev)
            t_qkv = evac_toks[-1]

            nrm_toks = []
            sq_prev = None
            for p in range(NP):
                t_ks = sch.on("act", (lambda e, p=p: e.activation(
                    sqscr[:, :], kf3[:, p, :], AF.Square)),
                    deps=TL(evac_toks[8 + p], sq_prev, t_stat))
                b, g = next_bank()
                t_nm = sch.on("pe", (lambda e, b=b: e.matmul(
                    b[0:2, 0:512], hsum2[:, :], sqscr[:, :],
                    start=True, stop=True)), deps=TL(t_ks, g))
                sq_prev = t_nm
                t_ne = sch.on("dve", (lambda e, p=p, b=b: e.tensor_copy(
                    n2v[0:2, p, :], b[0:2, 0:512])), deps=[t_nm])
                set_guard(b, t_ne)
                nrm_toks.append(t_ne)
            t_sd2 = sch.on("act", lambda e: e.activation(
                norm2[0:2, :], norm2[0:2, :], AF.Sqrt), deps=nrm_toks)
            t_mx2 = sch.on("dve", lambda e: e.tensor_scalar_max(
                norm2[0:2, :], norm2[0:2, :], 1e-12), deps=[t_sd2])
            t_rsc = sch.on("dve", lambda e: e.reciprocal(
                norm2[0:2, :], norm2[0:2, :]), deps=[t_mx2])
            kh_toks = []
            for p in range(NP):
                b, g = next_bank()
                t_bm = sch.on("pe", (lambda e, p=p, b=b: e.matmul(
                    b[:, 0:512], exp2[0:2, :], n2v[0:2, p, :],
                    start=True, stop=True)), deps=TL(t_rsc, g))
                t_khe = sch.on("dve", (lambda e, p=p, b=b: e.tensor_tensor(
                    kh3[:, p, :], kf3[:, p, :], b[:, 0:512], AL.mult)), deps=[t_bm])
                t_kha = sch.on("dve", (lambda e, p=p: e.tensor_scalar(
                    out=kha3[:, p, :], in0=kh3[:, p, :], scalar1=atab[:, p:p + 1],
                    scalar2=None, op0=AL.mult)), deps=TL(t_khe, t_abe))
                set_guard(b, t_kha)
                kh_toks.append(t_kha)
            t_khall = kh_toks[-1]
            if debug and s == 0:
                for nm, src, dep in (("d_q", qf3, t_qkv), ("d_k", kf3, t_qkv),
                                     ("d_v", vf3, t_qkv), ("d_kh", kh3, t_khall)):
                    sch.dma("gps", (lambda e, nm=nm, src=src: e.dma_start(
                        out=dbg[nm], in_=src[:, :, :])), deps=[dep])

            mstate = {"prev_mevac": tok_m_ready}
            CSTATE = {}

            def cvars(c):
                par = c % 2
                d = dict(
                    col=slice(64 * c, 64 * c + 64), par=par,
                    khtm=khtm_P[par], vtm=vtm_P[par], r_stk=r_stk_P[par],
                    ya=ya_P[par], yb=yb_P[par], pexp=pexp_P[par],
                    pn=pn_P[par], ptb=ptb_P[par], att_sb=att_sb_P[par],
                    co_bf=co_bf_P[par], sm_f=sm_f_P[par],
                    solN=solN_P[par], solNT=solNT_P[par],
                    solNv=solNv_P[par], solNTv=solNTv_P[par],
                    so=16 * par,
                )
                d["khtmv"] = RR(d["khtm"])
                d["vtmv"] = RR(d["vtm"])
                d["ptv"] = RR(d["ptb"])
                d["pexpv"] = RR(d["pexp"])
                d["r_stkv"] = RR(d["r_stk"])
                d["yav"], d["ybv"] = RR(d["ya"]), RR(d["yb"])
                d["co_bfv"] = RR(d["co_bf"])
                d["pnv"] = RR(d["pn"])
                return d

            def stage_head_units(c):
                v = CSTATE[c] = cvars(c)
                units = []

                def u_t1():
                    b, g = next_bank()
                    t1 = sch.on("pe", (lambda e, b=b, col=v["col"]: pair_trans(
                        e, b, kh3, col)), deps=TL(t_khall, g, t_consts))
                    v["t1e"] = sch.on("dve", (lambda e, b=b, khtm=v["khtm"]:
                                              e.tensor_copy(khtm[:, :],
                                                            b[:, 0:512])),
                                      deps=[t1])
                    set_guard(b, v["t1e"])
                units.append(u_t1)

                def u_t2():
                    b, g = next_bank()
                    t2 = sch.on("pe", (lambda e, b=b, col=v["col"]: pair_trans(
                        e, b, vf3, col)), deps=TL(t_qkv, g))
                    v["t2e"] = sch.on("act", (lambda e, b=b, vtm=v["vtm"]: e.copy(
                        vtm[:, :], b[:, 0:512])), deps=[t2])
                    set_guard(b, v["t2e"])
                units.append(u_t2)

                def u_g():
                    b, g = next_bank()
                    tg = sch.on("pe", (lambda e, b=b, col=v["col"]: pair_mms(
                        e, b, kh3, kha3, col, col)), deps=TL(t_khall, g))
                    tn1 = sch.on("dve", (lambda e, b=b, solN=v["solN"]:
                                         e.tensor_tensor(solN[1][:, :],
                                                         b[:, 0:512],
                                                         masklow[:, :], AL.mult)),
                                 deps=[tg])
                    tnt1 = sch.on("dve", (lambda e, b=b, solNT=v["solNT"]:
                                          e.tensor_tensor(solNT[1][:, :],
                                                          b[:, 0:512],
                                                          maskup[:, :], AL.mult)),
                                  deps=[tg])
                    set_guard(b, tnt1)
                    v["form_tok"] = {1: (tn1, tnt1)}
                units.append(u_g)

                def u_s():
                    so = v["so"]
                    b, g = next_bank()
                    ts_ = sch.on("pe", (lambda e, b=b, col=v["col"]: pair_mms(
                        e, b, qf3, kh3, col, col)), deps=TL(t_qkv, t_khall, g))
                    tsm = sch.on("dve", (lambda e, b=b, sm_f=v["sm_f"]:
                                         e.tensor_tensor(sm_f[:, :], b[:, 0:512],
                                                         maskneg[:, :], AL.add)),
                                 deps=[ts_])
                    set_guard(b, tsm)
                    tmx = sch.on("dve", (lambda e, sm_f=v["sm_f"], so=so: (
                        e.reduce_max(stat_sc[:, so:so + 1], sm_f[:, :],
                                     axis=AX.X), e.drain())[-1]), deps=[tsm])
                    tnb = sch.on("dve", (lambda e, so=so: (e.tensor_scalar(
                        out=stat_sc[:, so + 1:so + 2],
                        in0=stat_sc[:, so:so + 1], scalar1=-0.125, scalar2=None,
                        op0=AL.mult), e.drain())[-1]), deps=[tmx])
                    texp = sch.on("act", (lambda e, sm_f=v["sm_f"],
                                          pexp=v["pexp"], so=so: e.activation(
                                              pexp[:, :], sm_f[:, :], AF.Exp,
                                              bias=stat_sc[:, so + 1:so + 2],
                                              scale=0.125)), deps=[tnb])
                    trs = sch.on("dve", (lambda e, pexpv=v["pexpv"], so=so: (
                        e.reduce_sum(stat_sc[:, so + 2:so + 10], pexpv,
                                     axis=AX.X), e.drain())[-1]), deps=[texp])
                    trr2 = sch.on("dve", (lambda e, so=so: (e.reciprocal(
                        stat_sc[:, so + 2:so + 10],
                        stat_sc[:, so + 2:so + 10]), e.drain())[-1]),
                        deps=[trs])

                    def pnorm(e, pexp=v["pexp"], pn=v["pn"], so=so):
                        for p in range(NP):
                            e.tensor_scalar(
                                out=pn[:, 64 * p:64 * p + 64],
                                in0=pexp[:, 64 * p:64 * p + 64],
                                scalar1=stat_sc[:, so + 2 + p:so + 3 + p],
                                scalar2=None, op0=AL.mult)
                        return e.drain()
                    v["tpn"] = sch.on("dve", pnorm, deps=[trr2])
                units.append(u_s)

                for lv_src, lv_dst in ((1, 2), (2, 4), (4, 8), (8, 16), (16, 32)):
                    def u_form(lv_src=lv_src, lv_dst=lv_dst):
                        bN, gN = next_bank()
                        bT_, gT = next_bank()
                        last_only = (lv_dst == 32)

                        def form(e, bN=bN, bT_=bT_, srcN=v["solNv"][lv_src],
                                 srcNT=v["solNTv"][lv_src], last_only=last_only):
                            last = None
                            if not last_only:
                                last = pair_mms(e, bN, srcNT, srcN, None, None)
                            last = pair_mms(e, bT_, srcN, srcNT, None, None)
                            return last
                        tf = sch.on("pe", form,
                                    deps=TL(*v["form_tok"][lv_src], gN, gT))
                        toks = []
                        if not last_only:
                            tevN = sch.on("act", (lambda e, lv=lv_dst, bN=bN,
                                                  solN=v["solN"]: e.copy(
                                                      solN[lv][:, :],
                                                      bN[:, 0:512])), deps=[tf])
                            set_guard(bN, tevN)
                            toks.append(tevN)
                        if lv_dst in (4,):
                            tevT = sch.on("act", (lambda e, lv=lv_dst, bT_=bT_,
                                                  solNT=v["solNT"]: e.copy(
                                                      solNT[lv][:, :],
                                                      bT_[:, 0:512])), deps=[tf])
                        else:
                            tevT = sch.on("dve", (lambda e, lv=lv_dst, bT_=bT_,
                                                  solNT=v["solNT"]:
                                                  e.tensor_copy(
                                                      solNT[lv][:, :],
                                                      bT_[:, 0:512])), deps=[tf])
                        set_guard(bT_, tevT)
                        toks.append(tevT)
                        v["form_tok"][lv_dst] = tuple(toks)
                    units.append(u_form)
                return units

            def stage_tail_units(c):
                v = CSTATE[c]
                units = []

                def u_r():
                    r_src = mt_base if c == 0 else mt_run
                    r_srcv = RR(r_src)
                    b, g = next_bank()
                    tr = sch.on("pe", (lambda e, b=b, col=v["col"], mtv=r_srcv:
                                       pair_mms(e, b, kha3, mtv, col, None)),
                                deps=TL(t_khall, g, mstate["prev_mevac"]))

                    def rbuild(e, b=b, r_stk=v["r_stk"], vtm=v["vtm"]):
                        for p in range(NP):
                            e.scalar_tensor_tensor(
                                out=r_stk[:, 64 * p:64 * p + 64],
                                in0=vtm[:, 64 * p:64 * p + 64],
                                scalar=btab[:, p:p + 1],
                                in1=b[:, 64 * p:64 * p + 64],
                                op0=AL.mult, op1=AL.subtract)
                        return e.drain()
                    v["trb"] = sch.on("dve", rbuild,
                                      deps=TL(tr, v["t2e"], t_abe))
                    set_guard(b, v["trb"])
                    v["ysrc"], v["ysrcv"] = v["r_stk"], v["r_stkv"]
                    v["ycur"] = v["ya"]
                    v["ytok"] = v["trb"]
                units.append(u_r)

                for k in (32, 16, 8, 4, 2, 1):
                    def u_ap(k=k):
                        b, g = next_bank()
                        tap = sch.on("pe", (lambda e, b=b, ntv=v["solNTv"][k],
                                            yv=v["ysrcv"]: pair_mms(
                                                e, b, ntv, yv, None, None)),
                                     deps=TL(*v["form_tok"][k], v["ytok"], g))
                        op = AL.subtract if k == 1 else AL.add
                        ycur = v["ycur"]
                        tye = sch.on("dve", (lambda e, b=b, ysrc=v["ysrc"],
                                             ycur=ycur, op=op: e.tensor_tensor(
                                                 ycur[:, :], ysrc[:, :],
                                                 b[:, 0:512], op)), deps=[tap])
                        set_guard(b, tye)
                        nxt = v["yb"] if ycur is v["ya"] else v["ya"]
                        v["ysrc"] = ycur
                        v["ysrcv"] = v["yav"] if ycur is v["ya"] else v["ybv"]
                        v["ycur"] = nxt
                        v["ytok"] = tye
                    units.append(u_ap)

                def u_pt():
                    v["xfinv"] = v["ysrcv"]
                    v["t_x"] = v["ytok"]
                    b, g = next_bank()
                    tpt = sch.on("pe", (lambda e, b=b, pnv=v["pnv"]: pair_trans(
                        e, b, pnv, None)), deps=TL(v["tpn"], g))
                    v["tpte"] = sch.on("act", (lambda e, b=b, ptb=v["ptb"]:
                                               e.copy(ptb[:, :], b[:, 0:512])),
                                       deps=[tpt])
                    set_guard(b, v["tpte"])
                units.append(u_pt)

                def u_av():
                    bAV, gAV = next_bank()
                    tav = sch.on("pe", (lambda e, b=bAV, ptv=v["ptv"],
                                        vtmv=v["vtmv"]: pair_mms(
                                            e, b, ptv, vtmv, None, None)),
                                 deps=TL(v["tpte"], v["t2e"], gAV))
                    v["tave"] = sch.on("dve", (lambda e, b=bAV,
                                               att_sb=v["att_sb"]:
                                               e.tensor_copy(att_sb[:, :],
                                                             b[:, 0:512])),
                                       deps=[tav])
                    set_guard(bAV, v["tave"])
                units.append(u_av)

                def u_me():
                    mem_src = m_base if c == 0 else m_run
                    msrcv = RR(mem_src)
                    bME, gME = next_bank()
                    tme = sch.on("pe", (lambda e, b=bME, col=v["col"],
                                        mv=msrcv: pair_mms(
                                            e, b, qf3, mv, col, None)),
                                 deps=TL(t_qkv, mstate["prev_mevac"], gME))
                    v["tco"] = sch.on("dve", (lambda e, b=bME, co_bf=v["co_bf"],
                                              att_sb=v["att_sb"]:
                                              e.scalar_tensor_tensor(
                                                  out=co_bf[:, :],
                                                  in0=b[:, 0:512], scalar=0.1,
                                                  in1=att_sb[:, :],
                                                  op0=AL.mult, op1=AL.add)),
                                      deps=TL(tme, v["tave"]))
                    set_guard(bME, v["tco"])
                units.append(u_me)

                def u_of():
                    bOF, gOF = next_bank()
                    tof = sch.on("pe", (lambda e, b=bOF, co_bfv=v["co_bfv"]:
                                        pair_trans(e, b, co_bfv, None)),
                                 deps=TL(v["tco"], gOF))
                    v["t_outfm_ev"] = sch.on("dve", (lambda e, b=bOF, c=c:
                                                     e.tensor_copy(
                                                         attn_fm3[:, :,
                                                                  64 * c:
                                                                  64 * c + 64],
                                                         b[:, 0:512].rearrange(
                                                             "p (a b) -> p a b",
                                                             a=NP))),
                                             deps=[tof])
                    set_guard(bOF, v["t_outfm_ev"])
                units.append(u_of)

                def u_m():
                    def mupd(e, xv=v["xfinv"], khtmv=v["khtmv"]):
                        last = None
                        for p in range(NP):
                            last = e.matmul(BM[0:64, 64 * p:64 * p + 64],
                                            xv[:, p, :][0:64],
                                            khtmv[:, p, :][0:64],
                                            start=True, stop=True,
                                            tile_position=(0, 0))
                            last = e.matmul(BM[64:128, 64 * p:64 * p + 64],
                                            xv[:, p, :][64:128],
                                            khtmv[:, p, :][64:128],
                                            start=True, stop=True,
                                            tile_position=(64, 64))
                        for p in range(NP):
                            last = e.matmul(BMT[0:64, 64 * p:64 * p + 64],
                                            khtmv[:, p, :][0:64],
                                            xv[:, p, :][0:64],
                                            start=True, stop=True,
                                            tile_position=(0, 0))
                            last = e.matmul(BMT[64:128, 64 * p:64 * p + 64],
                                            khtmv[:, p, :][64:128],
                                            xv[:, p, :][64:128],
                                            start=True, stop=True,
                                            tile_position=(64, 64))
                        return last
                    t_mupd = sch.on("pe", mupd, deps=TL(
                        v["t_x"], v["t1e"], mstate["prev_mevac"],
                        bank_guard[id(BM)], bank_guard[id(BMT)]))

                    def mevac(e, c=c, mb=m_base, mr=m_run, mtb=mt_base,
                              mtr=mt_run):
                        srcm = mb if c == 0 else mr
                        srcmt = mtb if c == 0 else mtr
                        e.tensor_tensor(mr[:, :], srcm[:, :], BM[:, 0:512],
                                        AL.add)
                        return e.tensor_tensor(mtr[:, :], srcmt[:, :],
                                               BMT[:, 0:512], AL.add)
                    t_mevac = sch.on("dve", mevac, deps=TL(t_mupd, v["tco"]))
                    set_guard(BM, t_mevac)
                    set_guard(BMT, t_mevac)
                    mstate["prev_mevac"] = t_mevac
                units.append(u_m)
                return units

            def interleave(a, b):
                out = []
                ia = ib = 0
                while ia < len(a) or ib < len(b):
                    if ia < len(a):
                        out.append(a[ia]); ia += 1
                    if ib < len(b):
                        out.append(b[ib]); ib += 1
                return out

            pending = stage_head_units(0)
            for u in pending:
                u()
            for c in range(CH):
                tail = stage_tail_units(c)
                head = stage_head_units(c + 1) if c + 1 < CH else []
                for u in interleave(tail, head):
                    u()
            t_outfm_ev = CSTATE[CH - 1]["t_outfm_ev"]
            prev_mevac = mstate["prev_mevac"]

            tok_m_ready = prev_mevac
            if debug and s == 0:
                sch.dma("gps", lambda e: e.dma_start(
                    out=dbg["d_attn"], in_=attn_fm3[:, :, :]), deps=[t_outfm_ev])

            x2_sts = []
            hi_sts = []
            lo_sts = []
            res_toks = []
            prev_hi = [tok_b1_prev, tok_b1_prev]
            for ot in range(8):
                b, g = next_bank()

                def mmwo(e, ot=ot, b=b):
                    last = None
                    for kc in range(8):
                        last = e.matmul(b[:, 0:512],
                                        w_attn[3][:, kc, 128 * ot:128 * ot + 128],
                                        attn_fm3[:, kc, :],
                                        start=(kc == 0), stop=(kc == 7))
                    return last
                t_mm = sch.on("pe", mmwo, deps=TL(t_outfm_ev, g))
                t_res = sch.on("dve", (lambda e, ot=ot, b=b: e.tensor_tensor(
                    xs3[:, ot, :], xs3[:, ot, :], b[:, 0:512], AL.add)),
                    deps=[t_mm])
                set_guard(b, t_res)
                res_toks.append(t_res)
                hi = x2st_hi[ot % 2]
                lo = x2st_lo[ot % 2]
                t_hi = sch.on("dve", (lambda e, ot=ot, hi=hi: e.tensor_copy(
                    hi, xs3[:, ot, :])), deps=TL(t_res, prev_hi[ot % 2]))
                t_lo = sch.on("dve", (lambda e, ot=ot, hi=hi, lo=lo:
                                      e.scalar_tensor_tensor(
                                          out=lo, in0=hi, scalar=-1.0,
                                          in1=xs3[:, ot, :], op0=AL.mult,
                                          op1=AL.add)), deps=[t_hi])
                s2 = sch.dma("act", (lambda e, ot=ot, seg=seg, hi=hi: e.dma_start(
                    out=x2hi_v[:, ot, seg], in_=hi)), deps=[t_hi])
                s3 = sch.dma("gps", (lambda e, ot=ot, seg=seg, lo=lo: e.dma_start(
                    out=x2lo_v[:, ot, seg], in_=lo)), deps=[t_lo])
                hi_sts.append(s2)
                lo_sts.append(s3)
                prev_hi[ot % 2] = s3
            s1 = sch.dma("sync", (lambda e, seg=seg: e.dma_start(
                out=x2scr_v[:, :, seg], in_=xs3[:, :, :])), deps=res_toks)
            x2_sts = [s1] + hi_sts + lo_sts
            tok_x2_store[s] = x2_sts
            if debug and s == 0:
                sch.dma("gps", lambda e: e.dma_start(
                    out=dbg["d_x2s"], in_=xs3[:, :, :]), deps=[x2_sts[-1]])
            if debug and s == 0:
                sch.dma("gps", lambda e: e.dma_start(
                    out=dbg.setdefault("d_x2_unused", dbg["d_ln"]),
                    in_=xs3[:, :, :]), deps=[x2_sts[-1]]) if False else None

            b1_sts = []
            prev_q = None
            for tb in range(4):
                tok0 = s * SEG + tb * 128
                tcs = slice(tok0, tok0 + 128)
                t_h = sch.dma("sync", (lambda e, tcs=tcs: e.dma_start_transpose(
                    b1_hi[:, :], x2hiscr[:, tcs])),
                    deps=TL(hi_sts[-1], hi_sts[-2], prev_q))
                t_l = sch.dma("act", (lambda e, tcs=tcs: e.dma_start_transpose(
                    b1_lo[:, :], x2loscr[:, tcs])),
                    deps=TL(lo_sts[-1], lo_sts[-2], prev_q))
                t_rb = sch.on("dve", lambda e: e.scalar_tensor_tensor(
                    out=b1_x2tm[:, :], in0=b1_hi[:, :], scalar=1.0,
                    in1=b1_lo[:, :], op0=AL.mult, op1=AL.add,
                    accum_out=stat_sc[:, 26:27]), deps=[t_h, t_l])
                t_s2_ = sch.on("dve", lambda e: (e.scalar_tensor_tensor(
                    out=b1_sq[:, :], in0=b1_hi[:, :], scalar=1.0, in1=b1_hi[:, :],
                    op0=AL.mult, op1=AL.mult,
                    accum_out=stat_sc[:, 27:28]), e.drain())[-1],
                    deps=[t_h, t_rb])

                def b1math(e):
                    sc = stat_sc
                    e.tensor_scalar(out=sc[:, 28:29], in0=sc[:, 26:27],
                                    scalar1=1.0 / D, scalar2=None, op0=AL.mult)
                    e.drain()
                    e.tensor_scalar(out=sc[:, 29:30], in0=sc[:, 28:29],
                                    scalar1=-1.0, scalar2=None, op0=AL.mult)
                    e.tensor_tensor(sc[:, 30:31], sc[:, 28:29], sc[:, 28:29],
                                    AL.mult)
                    e.drain()
                    e.scalar_tensor_tensor(
                        out=sc[:, 31:32], in0=sc[:, 27:28], scalar=1.0 / D,
                        in1=sc[:, 30:31], op0=AL.mult, op1=AL.subtract)
                    return e.drain()
                t_bm_ = sch.on("dve", b1math, deps=[t_rb, t_s2_])
                t_bsd = sch.on("act", lambda e: e.activation(
                    stat_sc[:, 30:31], stat_sc[:, 31:32], AF.Sqrt,
                    bias=eps_col[:, 0:1], scale=1.0), deps=TL(t_bm_, t_eps))
                t_br = sch.on("dve", lambda e: (e.reciprocal(
                    stat_sc[:, 31:32], stat_sc[:, 30:31]), e.drain())[-1],
                    deps=[t_bsd])
                t_c_ = sch.on("dve", lambda e: e.tensor_scalar(
                    out=b1_ctm[:, :], in0=b1_x2tm[:, :],
                    scalar1=stat_sc[:, 29:30], scalar2=None, op0=AL.add),
                    deps=[t_bm_])
                t_mxq = sch.on("dve", lambda e: (e.reduce_max(
                    stat_sc[:, 26:27], b1_ctm[:, :], axis=AX.X,
                    apply_absolute_value=True), e.drain())[-1], deps=[t_c_])

                def b1math2(e):
                    sc = stat_sc
                    e.tensor_tensor(sc[:, 27:28], sc[:, 31:32], sc[:, 26:27],
                                    AL.mult)
                    e.drain()
                    e.tensor_scalar_max(sc[:, 27:28], sc[:, 27:28], 1e-5)
                    e.drain()
                    e.reciprocal(sc[:, 28:29], sc[:, 27:28])
                    e.drain()
                    e.tensor_scalar(out=sc[:, 28:29], in0=sc[:, 28:29],
                                    scalar1=127.0, scalar2=None, op0=AL.mult)
                    e.drain()
                    e.tensor_tensor(sc[:, 29:30], sc[:, 31:32], sc[:, 28:29],
                                    AL.mult)
                    e.tensor_scalar(out=sc[:, 30:31], in0=sc[:, 27:28],
                                    scalar1=1.0 / 127.0, scalar2=None,
                                    op0=AL.mult)
                    return e.drain()
                t_bm2 = sch.on("dve", b1math2, deps=[t_mxq, t_br])
                t_t1 = sch.on("dve", lambda e: e.tensor_scalar(
                    out=b1_ctm[:, :], in0=b1_ctm[:, :], scalar1=stat_sc[:, 29:30],
                    scalar2=MAGICF, op0=AL.mult, op1=AL.add), deps=[t_bm2])
                t_xq1 = sch.on("dve", lambda e: e.tensor_scalar(
                    out=b1_hi[:, :], in0=b1_ctm[:, :], scalar1=MAGICF,
                    scalar2=stat_sc[:, 30:31], op0=AL.subtract, op1=AL.mult),
                    deps=[t_t1])
                if debug and s == 0 and tb == 0:
                    sch.dma("gps", lambda e: e.dma_start(
                        out=dbg["d_b1hi"], in_=b1_hi[:, :]), deps=[t_h, t_rb])
                    sch.dma("gps", lambda e: e.dma_start(
                        out=dbg["d_b1lo"], in_=b1_lo[:, :]), deps=[t_l, t_rb])
                    sch.dma("gps", lambda e: e.dma_start(
                        out=dbg["d_b1x2"], in_=b1_x2tm[:, :]), deps=[t_rb, t_c_])
                    sch.dma("gps", lambda e: e.dma_start(
                        out=dbg["d_b1st"], in_=stat_sc[:, :]), deps=[t_bm2])
                t_st_ = sch.dma("gps", (lambda e, tok0=tok0: e.dma_start(
                    out=xq1scr[8 + tok0:8 + tok0 + 128, :], in_=b1_hi[:, :])),
                    deps=[t_xq1])
                b1_sts.append(t_st_)
                prev_q = t_st_
                if debug and s == 0 and tb == 0:
                    sch.dma("gps", lambda e: e.dma_start(
                        out=dbg["d_xq1"], in_=b1_hi[:, :]), deps=[t_xq1])
            tok_b1_prev = b1_sts[-1]
            last_xq1 = b1_sts[-1]
            tok_seg_done = s1

        def ternarize(w_v, nchunk, ncols, wq_dst3, mw_col, dep0):
            tred = []
            prev = dep0
            for kc in range(nchunk):
                t_ld = sch.dma("sync", (lambda e, kc=kc, w_v=w_v, ncols=ncols:
                                        e.dma_start(out=stage_f[:, 0:ncols],
                                                    in_=w_v[:, kc, :])),
                               deps=TL(prev))
                t_r = sch.on("dve", (lambda e, kc=kc, ncols=ncols: (
                    e.tensor_reduce(
                        wstat[:, kc:kc + 1], stage_f[:, 0:ncols], axis=AX.X,
                        op=AL.add, apply_absolute_value=True),
                    e.drain())[-1]), deps=[t_ld])
                prev = t_r
                tred.append(t_r)
            t_r2 = sch.on("dve", (lambda e, nchunk=nchunk: (e.tensor_reduce(
                wstat[:, 36:37], wstat[:, 0:nchunk], axis=AX.X, op=AL.add),
                e.drain())[-1]), deps=tred)
            b, g = next_bank()
            t_sum = sch.on("pe", (lambda e, b=b: e.matmul(
                b[0:1, 0:1], onescf[:, :], wstat[:, 36:37], start=True,
                stop=True)), deps=TL(t_r2, t_of, g))

            def mmath(e, b=b, n=nchunk * ncols * 128):
                e.tensor_scalar(out=wstat[0:1, 37:38], in0=b[0:1, 0:1],
                                scalar1=1.0 / n, scalar2=None, op0=AL.mult)
                e.drain()
                e.tensor_scalar_max(wstat[0:1, 37:38], wstat[0:1, 37:38], 1e-5)
                e.drain()
                e.reciprocal(wstat[0:1, 38:39], wstat[0:1, 37:38])
                e.tensor_copy(wstat[0:1, 39:40], wstat[0:1, 37:38])
                return e.drain()
            t_mm_ = sch.on("dve", mmath, deps=[t_sum])
            set_guard(b, t_mm_)
            b2_, g2 = next_bank()
            t_bcm = sch.on("pe", (lambda e, b=b2_: (
                e.matmul(b[:, 0:1], ones1f[0:1, :], wstat[0:1, 38:39],
                         start=True, stop=True),
                e.matmul(b[:, 1:2], ones1f[0:1, :], wstat[0:1, 39:40],
                         start=True, stop=True))[-1]), deps=TL(t_mm_, g2))
            t_mwc = sch.on("dve", (lambda e, b=b2_, mw=mw_col: (
                e.tensor_copy(mwcol[:, mw:mw + 1], b[:, 0:1]),
                e.tensor_copy(mwcol[:, mw + 1:mw + 2], b[:, 1:2]),
                e.drain())[-1]), deps=[t_bcm])
            set_guard(b2_, t_mwc)
            prev = t_mwc
            last = None
            for kc in range(nchunk):
                t_ld = sch.dma("sync", (lambda e, kc=kc, w_v=w_v, ncols=ncols:
                                        e.dma_start(out=stage_f[:, 0:ncols],
                                                    in_=w_v[:, kc, :])),
                               deps=TL(prev))

                def tern(e, kc=kc, ncols=ncols, mw=mw_col, wq_dst3=wq_dst3):
                    e.tensor_scalar(out=stage_f[:, 0:ncols],
                                    in0=stage_f[:, 0:ncols],
                                    scalar1=mwcol[:, mw:mw + 1], scalar2=MAGICF,
                                    op0=AL.mult, op1=AL.add)
                    e.tensor_scalar(out=stage_f[:, 0:ncols],
                                    in0=stage_f[:, 0:ncols], scalar1=MAGICF,
                                    scalar2=None, op0=AL.subtract)
                    return e.tensor_scalar(out=wq_dst3[:, kc, :],
                                           in0=stage_f[:, 0:ncols], scalar1=1.0,
                                           scalar2=-1.0, op0=AL.min, op1=AL.max)
                t_t = sch.on("dve", tern, deps=[t_ld])
                prev = t_t
                last = t_t
            return last

        t_w1q = ternarize(w1T_v, 8, HD, w1qv, 0, tok_seg_done)
        t_w2q = ternarize(w2T_v, 32, D, w2qv, 2, t_w1q)
        t_tape = sch.on("dve", lambda e: e.tensor_scalar(
            out=taps_sb[:, :], in0=taps_sb[:, :], scalar1=mwcol[:, 1:2],
            scalar2=None, op0=AL.mult), deps=TL(t_w1q, t_consts))

        out_toks = []
        prev_w1_last = None
        prev_q2 = None
        prev_asm = t_w2q

        def emit_xq1w_loads(s, w, first):
            T0 = s * SEG + 256 * w
            lds = []
            for kc in range(8):
                t_lw = sch.dma("sync" if kc % 2 == 0 else "act",
                               (lambda e, kc=kc, T0=T0:
                                e.dma_start_transpose(
                                    b2_xq1w[:, kc, :],
                                    xq1scr[T0:T0 + 272,
                                           128 * kc:128 * kc + 128])),
                               deps=TL(last_xq1, prev_w1_last,
                                       t_tape if first else None))
                lds.append(t_lw)
            return lds

        pending_lds = emit_xq1w_loads(0, 0, True)
        for s in range(NSEG):
            seg = slice(s * SEG, (s + 1) * SEG)
            t_ldx2 = sch.dma("sync", (lambda e, seg=seg: e.dma_start(
                out=xs3[:, :, :], in_=x2scr_v[:, :, seg])),
                deps=TL(prev_asm, tok_x2_store[s][0]))
            d2_toks = []
            blk_q2st = []
            for w in range(2):
                t_lds = pending_lds
                if debug and s == 0 and w == 0:
                    sch.dma("gps", lambda e: e.dma_start(
                        out=dbg["d_xq1w"], in_=b2_xq1w[:, :, :]), deps=t_lds)
                sh_toks = []
                prev_chain = (None, None, None, None)
                for hc in range(32):
                    b, g = next_bank()

                    def mmw1(e, hc=hc, b=b):
                        last = None
                        for kc in range(8):
                            last = e.matmul(b[:, 0:258],
                                            w1qv[:, kc, 128 * hc:128 * hc + 128],
                                            b2_xq1w[:, kc, 7:265],
                                            start=(kc == 0), stop=(kc == 7))
                        return last
                    t_mm = sch.on("pe", mmw1, deps=TL(*t_lds, t_w1q, g))
                    t_ev = sch.on("act", (lambda e, b=b: e.copy(
                        hwin[:, :], b[:, 0:258])),
                        deps=TL(t_mm, prev_chain[1], prev_chain[2]))
                    set_guard(b, t_ev)
                    t_e1 = sch.on("act", (lambda e, hc=hc: e.activation(
                        c1win[:, :], hwin[:, 0:256], AF.Identity,
                        scale=taps_sb[:, hc:hc + 1])), deps=TL(t_ev, t_tape))
                    t_e2 = sch.on("dve", (lambda e, hc=hc: e.scalar_tensor_tensor(
                        out=c2win[:, :], in0=hwin[:, 1:257],
                        scalar=taps_sb[:, 32 + hc:33 + hc], in1=c1win[:, :],
                        op0=AL.mult, op1=AL.add)), deps=TL(t_e1, prev_chain[2]))
                    t_e3 = sch.on("dve", (lambda e, hc=hc: e.scalar_tensor_tensor(
                        out=c3win[:, :], in0=hwin[:, 2:258],
                        scalar=taps_sb[:, 64 + hc:65 + hc], in1=c2win[:, :],
                        op0=AL.mult, op1=AL.add)), deps=TL(t_e2, prev_chain[3]))
                    t_sl = sch.on("act", (lambda e, hc=hc: e.activation(
                        b2_sh3[:, hc, :], c3win[:, :], AF.Silu,
                        bias=cb_sb[:, hc:hc + 1], scale=1.0)),
                        deps=TL(t_e3, t_consts))
                    prev_chain = (t_ev, t_e2, t_e3, t_sl)
                    sh_toks.append(t_sl)
                    prev_w1_last = t_mm
                nw = 2 * s + w + 1
                if nw < 2 * NSEG:
                    pending_lds = emit_xq1w_loads(nw // 2, nw % 2, False)
                if debug and s == 0 and w == 0:
                    sch.dma("gps", lambda e: e.dma_start(
                        out=dbg["d_sh"], in_=b2_sh3[:, :, :]), deps=sh_toks)

                for tb in range(2):
                    blki = 2 * w + tb
                    tok0 = s * SEG + 128 * blki
                    banks8 = PB + [BM, BMT]
                    tr_toks = []
                    for bi in range(8):
                        b8 = banks8[bi]

                        def trx(e, bi=bi, b8=b8, tb=tb):
                            last = None
                            for j in range(4):
                                hc = bi * 4 + j
                                last = e.matmul(b8[:, 128 * j:128 * j + 128],
                                                b2_sh3[:, hc,
                                                       128 * tb:128 * tb + 128],
                                                identbf[:, :], start=True,
                                                stop=True)
                            return last
                        t_tr = sch.on("pe", trx, deps=TL(
                            *sh_toks, bank_guard[id(b8)], prev_q2))
                        tr_toks.append(t_tr)
                    mx_toks = []
                    for bi in range(8):
                        t_m = sch.on("dve", (lambda e, bi=bi, b8=banks8[bi]: (
                            e.reduce_max(
                                stat_sc[:, 2 + bi:3 + bi],
                                b8[:, 0:512], axis=AX.X,
                                apply_absolute_value=True),
                            e.drain() if bi == 7 else e.nop())[-1]),
                                     deps=[tr_toks[bi]])
                        mx_toks.append(t_m)
                    t_mxa = sch.on("dve", lambda e: (e.reduce_max(
                        stat_sc[:, 10:11], stat_sc[:, 2:10], axis=AX.X),
                        e.drain())[-1], deps=mx_toks)

                    def q2math(e, blki=blki):
                        sc = stat_sc
                        e.tensor_scalar_max(sc[:, 11:12], sc[:, 10:11], 1e-5)
                        e.drain()
                        e.reciprocal(sc[:, 12:13], sc[:, 11:12])
                        e.drain()
                        e.tensor_scalar(out=sc[:, 12:13], in0=sc[:, 12:13],
                                        scalar1=127.0, scalar2=None, op0=AL.mult)
                        e.scalar_tensor_tensor(
                            out=d2tab[:, blki:blki + 1], in0=sc[:, 11:12],
                            scalar=1.0 / 127.0, in1=mwcol[:, 3:4],
                            op0=AL.mult, op1=AL.mult)
                        return e.drain()
                    t_q2m = sch.on("dve", q2math, deps=TL(t_mxa, t_w2q))
                    d2_toks.append(t_q2m)
                    rp_toks = []
                    for bi in range(8):
                        t_r1 = sch.on("dve", (lambda e, bi=bi, b8=banks8[bi]:
                                              e.tensor_scalar(
                                                  out=ftmp[:, :], in0=b8[:, 0:512],
                                                  scalar1=stat_sc[:, 12:13],
                                                  scalar2=MAGICF, op0=AL.mult,
                                                  op1=AL.add)), deps=[t_q2m])
                        t_r2_ = sch.on("dve", (lambda e, bi=bi: e.tensor_scalar(
                            out=b2_xq2tm[:, 512 * bi:512 * bi + 512],
                            in0=ftmp[:, :], scalar1=MAGICF, scalar2=None,
                            op0=AL.subtract)), deps=[t_r1])
                        set_guard(banks8[bi], t_r2_)
                        rp_toks.append(t_r2_)
                    t_stq = sch.dma("gps", (lambda e, tok0=tok0: e.dma_start(
                        out=xq2scr[tok0:tok0 + 128, :], in_=b2_xq2tm[:, :])),
                        deps=rp_toks)
                    blk_q2st.append(t_stq)
                    prev_q2 = t_stq
                    if debug and s == 0 and blki == 0:
                        sch.dma("gps", lambda e: e.dma_start(
                            out=dbg["d_xq2"], in_=b2_xq2tm[:, :]), deps=rp_toks)

            bD, gD = next_bank()
            t_trd = sch.on("pe", (lambda e, b=bD: (
                [e.matmul(b[0:1, 128 * i:128 * i + 128], d2tab[:, i:i + 1],
                          identbf[:, :], start=True, stop=True)
                 for i in range(4)])[-1]), deps=TL(*d2_toks, gD))
            t_d2r = sch.on("dve", (lambda e, b=bD: (e.tensor_copy(
                d2row[0:1, :], b[0:1, 0:512]), e.drain())[-1]), deps=[t_trd])
            set_guard(bD, t_d2r)
            bD2, gD2 = next_bank()
            t_d2b = sch.on("pe", (lambda e, b=bD2: e.matmul(
                b[:, 0:512], ones1[0:1, :], d2row[0:1, :], start=True,
                stop=True)), deps=TL(t_d2r, gD2))
            t_d2bc = sch.on("dve", (lambda e, b=bD2: e.tensor_copy(
                d2bc[:, :], b[:, 0:512])), deps=[t_d2b])
            set_guard(bD2, t_d2bc)

            banks8 = PB + [BM, BMT]
            mm_toks = []
            ld_toks = []
            for kc in range(32):
                buf = b2_xq2b[kc % 2]
                t_lq = sch.dma("sync" if kc % 2 == 0 else "act",
                               (lambda e, kc=kc, buf=buf, seg=seg:
                                e.dma_start_transpose(
                                    buf,
                                    xq2scr[seg, 128 * kc:128 * kc + 128])),
                               deps=TL(blk_q2st[-1],
                                       mm_toks[kc - 2] if kc >= 2 else None))
                ld_toks.append(t_lq)
                depk = [t_lq, t_w2q]
                if kc == 0:
                    depk += [bank_guard[id(b_)] for b_ in banks8]
                    depk += [t_d2bc]

                def mmw2(e, kc=kc, buf=buf):
                    last = None
                    for ot in range(8):
                        last = e.matmul(banks8[ot][:, 0:512],
                                        w2qv[:, kc, 128 * ot:128 * ot + 128],
                                        buf, start=(kc == 0), stop=(kc == 31))
                    return last
                t_mmq = sch.on("pe", mmw2, deps=TL(*depk))
                mm_toks.append(t_mmq)
            asm_toks = []
            for ot in range(8):
                t_as1 = sch.on("dve", (lambda e, ot=ot, b=banks8[ot]:
                                       e.tensor_tensor(ftmp[:, :], b[:, 0:512],
                                                       d2bc[:, :], AL.mult)),
                               deps=TL(mm_toks[-1], t_d2bc,
                                       asm_toks[-1] if asm_toks else None))
                t_as2 = sch.on("dve", (lambda e, ot=ot: e.tensor_tensor(
                    xs3[:, ot, :], xs3[:, ot, :], ftmp[:, :], AL.add)),
                    deps=[t_as1, t_ldx2])
                set_guard(banks8[ot], t_as2)
                asm_toks.append(t_as2)
            t_out = sch.dma("sync", (lambda e, seg=seg: e.dma_start(
                out=outT_v[:, :, seg], in_=xs3[:, :, :])), deps=asm_toks)
            out_toks.append(t_out)
            prev_asm = t_out

        if debug:
            sch.dma("gps", lambda e: e.dma_start(
                out=dbg["d_xq1s"], in_=xq1scr[8:264, :]), deps=[last_xq1])
            sch.dma("gps", lambda e: e.dma_start(
                out=dbg["d_xq2s"], in_=xq2scr[0:128, :]), deps=[out_toks[-1]])
            t_dd2 = sch.on("dve", lambda e: e.tensor_copy(
                stat_sc[:, 0:4], d2tab[:, :]), deps=[out_toks[-1]])
            sch.dma("gps", lambda e: e.dma_start(
                out=dbg["d_d2"], in_=stat_sc[:, 0:4]), deps=[t_dd2])
        sch.final_wait("sync")

        blk = st.enter_context(nc.Block())
        st.enter_context(nc.allow_low_precision(
            reason="bf16 operands with fp32 psum accumulation are within this "
                   "problem's stated tolerance; precision-sensitive steps "
                   "(round/clip/stats/residual) are kept in fp32 explicitly"))

        @blk.tensor
        def _(eng):
            sch.emit("pe", eng)

        @blk.vector
        def _(eng):
            sch.emit("dve", eng)

        @blk.scalar
        def _(eng):
            sch.emit("act", eng)

        @blk.gpsimd
        def _(eng):
            sch.emit("gps", eng)

        @blk.sync
        def _(eng):
            sch.emit("sync", eng)

    return nc



def _consts():
    ident = np.eye(128)
    i2 = np.concatenate([np.eye(64), np.eye(64)], 0)
    low = np.tril(np.ones((64, 64)), -1)
    up = np.triu(np.ones((64, 64)), 1)
    low2 = np.concatenate([low, low], 0)
    up2 = np.concatenate([up, up], 0)
    neg = np.where(np.concatenate([up, up], 0) > 0, -1e30, 0.0)
    ones2 = np.zeros((128, 2))
    ones2[:, 0] = 1.0
    ones1 = np.ones((1, 128))
    hsum = np.zeros((128, 2))
    hsum[0:64, 0] = 1.0
    hsum[64:128, 1] = 1.0
    exp2 = np.zeros((2, 128))
    exp2[0, 0:64] = 1.0
    exp2[1, 64:128] = 1.0
    return dict(
        identbf=np.ascontiguousarray(ident.astype(BF)),
        ident16f=np.ascontiguousarray(np.eye(16, dtype=np.float32)),
        i2bf=np.ascontiguousarray(i2.astype(BF)),
        masklow=np.ascontiguousarray(np.tile(low2, (1, 8)).astype(BF)),
        maskup=np.ascontiguousarray(np.tile(up2, (1, 8)).astype(BF)),
        maskneg=np.ascontiguousarray(np.tile(neg, (1, 8)).astype(BF)),
        onesc=np.ascontiguousarray(ones2.astype(BF)),
        ones1=np.ascontiguousarray(ones1.astype(BF)),
        ones1f=np.ascontiguousarray(ones1.astype(np.float32)),
        hsum2=np.ascontiguousarray(hsum.astype(BF)),
        expander2=np.ascontiguousarray(exp2.astype(BF)),
    )


_NC_CACHE = {}


def kernel(L0_attn_x, L0_attn_Wq, L0_attn_Wk, L0_attn_Wv, L0_attn_Wo,
           L0_attn_alpha_raw, L0_attn_beta_raw, L0_ffn_w1, L0_ffn_conv_w,
           L0_ffn_conv_b, L0_ffn_w2, _debug=False, _cores=8):
    f32 = np.float32
    wqT = np.ascontiguousarray(np.asarray(L0_attn_Wq, f32).T)
    wkT = np.ascontiguousarray(np.asarray(L0_attn_Wk, f32).T)
    wvT = np.ascontiguousarray(np.asarray(L0_attn_Wv, f32).T)
    woT = np.ascontiguousarray(np.asarray(L0_attn_Wo, f32).T)
    w1T = np.ascontiguousarray(np.asarray(L0_ffn_w1, f32).T)
    w2T = np.ascontiguousarray(np.asarray(L0_ffn_w2, f32).T)
    cw = np.asarray(L0_ffn_conv_w, f32)[:, 0, :]
    taps = np.zeros((128, 96), f32)
    for k in range(3):
        taps[:, 32 * k:32 * k + 32] = cw[:, k].reshape(32, 128).T
    convb = np.ascontiguousarray(np.asarray(L0_ffn_conv_b, f32).reshape(32, 128).T)
    abraw = np.ascontiguousarray(np.stack(
        [np.asarray(L0_attn_alpha_raw, f32).reshape(16),
         np.asarray(L0_attn_beta_raw, f32).reshape(16)], 1))
    consts = _consts()

    key = ("dbg" if _debug else "main")
    if key not in _NC_CACHE:
        _NC_CACHE[key] = build_nc(debug=_debug)
    nc = _NC_CACHE[key]

    x = np.asarray(L0_attn_x, f32)
    in_maps = []
    for b in range(_cores):
        m = dict(xT=np.ascontiguousarray(x[b].T), wqT=wqT, wkT=wkT, wvT=wvT,
                 woT=woT, w1T=w1T, w2T=w2T, taps=taps, convb=convb, abraw=abraw)
        m.update(consts)
        in_maps.append(m)
    res = run_bass_kernel_spmd(nc, in_maps, core_ids=list(range(_cores)))
    outs = [np.ascontiguousarray(res.results[b]["outT"].T) for b in range(_cores)]
    if _cores < 8:
        outs = outs + [outs[0]] * (8 - _cores)
    full = np.stack(outs[:8])
    if _debug:
        return full, res.results
    return full


# version 11 (final)
# speedup vs baseline: 1.0355x; 1.0009x over previous
import os
import numpy as np
import ml_dtypes
import contextlib
import concourse.bass as bass
import concourse.mybir as mybir
from concourse.bass_utils import run_bass_kernel_spmd

F32 = mybir.dt.float32
BF16 = mybir.dt.bfloat16
AL = mybir.AluOpType
AX = mybir.AxisListType
AF = mybir.ActivationFunctionType
BF = ml_dtypes.bfloat16
MAGICF = float(np.float32(1.5 * 2.0 ** 23))

T = 4096
D = 1024
HD = 4096
SEG = 512
NSEG = T // SEG
CH = 8
NP = 8
EPS = 1e-5


class Sched:
    ENGINES = ("pe", "dve", "act", "gps", "sync")

    def __init__(self, nc, st):
        self.serial = int(os.environ.get("KSERIAL", "0"))
        self.steps = {e: [] for e in self.ENGINES}
        self.count = {e: 0 for e in self.ENGINES}
        self.dma_count = {e: 0 for e in ("sync", "gps", "act")}
        self.sem = {e: st.enter_context(nc.semaphore(f"s_{e}")) for e in self.ENGINES}
        self.dsem = {e: st.enter_context(nc.semaphore(f"d_{e}"))
                     for e in ("sync", "gps", "act")}

    def _allbar(self, eng, isdma=False):
        m = self.serial
        deps = []
        engs = ()
        dmas = ()
        if m == 1:
            engs = tuple(e for e in self.ENGINES if e != eng)
            dmas = ("sync", "gps", "act")
        elif m == 2:
            if isdma:
                engs = tuple(e for e in self.ENGINES if e != eng)
            else:
                dmas = ("sync", "gps", "act")
        elif m == 3 and not isdma:
            if eng == "pe":
                engs = ("dve",)
            elif eng == "dve":
                engs = ("pe",)
        elif m == 4 and not isdma:
            if eng == "act":
                engs = ("dve", "pe")
            elif eng in ("dve", "pe"):
                engs = ("act",)
        for e in engs:
            if self.count[e]:
                deps.append((self.sem[e], self.count[e]))
        for q in dmas:
            if self.dma_count[q]:
                deps.append((self.dsem[q], self.dma_count[q]))
        return deps

    def on(self, eng, fn, deps=()):
        deps = tuple(d for d in deps if d)
        if self.serial:
            deps = deps + tuple(self._allbar(eng))
        self.count[eng] += 1
        tok = (self.sem[eng], self.count[eng])
        self.steps[eng].append(("c", fn, deps, tok))
        return tok

    def dma(self, queue, fn, deps=()):
        deps = tuple(d for d in deps if d)
        if self.serial:
            deps = deps + tuple(self._allbar(queue, isdma=True))
        self.dma_count[queue] += 16
        tok = (self.dsem[queue], self.dma_count[queue])
        self.steps[queue].append(("d", fn, deps, tok))
        return tok

    def emit(self, eng, engobj):
        waited = {}
        for kind, fn, deps, tok in self.steps[eng]:
            if kind == "d":
                prior = tok[1] - 16
                if prior > 0:
                    deps = deps + ((tok[0], prior),)
            for sem, val in deps:
                key = id(sem)
                if waited.get(key, -1) >= val:
                    continue
                engobj.wait_ge(sem, val)
                waited[key] = val
            inst = fn(engobj)
            assert inst is not None, f"step on {eng} returned None"
            inst.then_inc(tok[0], 16 if kind == "d" else 1)

    def final_wait(self, eng):
        def fn(engobj):
            for q in ("sync", "gps", "act"):
                if self.dma_count[q]:
                    engobj.wait_ge(self.dsem[q], self.dma_count[q])
            for e in self.ENGINES:
                if e != eng and self.count[e]:
                    engobj.wait_ge(self.sem[e], self.count[e])
            return engobj.nop()
        self.count[eng] += 1
        self.steps[eng].append(("c", fn, (), (self.sem[eng], self.count[eng])))


def TL(*ts):
    return [t for t in ts if t is not None]


def build_nc(debug=False):
    nc = bass.Bass("TRN2", target_bir_lowering=False)

    xT = nc.dram_tensor("xT", [D, T], F32, kind="ExternalInput").ap()
    wqT = nc.dram_tensor("wqT", [D, D], F32, kind="ExternalInput").ap()
    wkT = nc.dram_tensor("wkT", [D, D], F32, kind="ExternalInput").ap()
    wvT = nc.dram_tensor("wvT", [D, D], F32, kind="ExternalInput").ap()
    woT = nc.dram_tensor("woT", [D, D], F32, kind="ExternalInput").ap()
    w1T = nc.dram_tensor("w1T", [D, HD], F32, kind="ExternalInput").ap()
    w2T = nc.dram_tensor("w2T", [HD, D], F32, kind="ExternalInput").ap()
    taps_in = nc.dram_tensor("taps", [128, 96], F32, kind="ExternalInput").ap()
    cb_in = nc.dram_tensor("convb", [128, 32], F32, kind="ExternalInput").ap()
    ab_in = nc.dram_tensor("abraw", [16, 2], F32, kind="ExternalInput").ap()
    c_identbf = nc.dram_tensor("identbf", [128, 128], BF16, kind="ExternalInput").ap()
    c_ident16 = nc.dram_tensor("ident16f", [16, 16], F32, kind="ExternalInput").ap()
    c_i2bf = nc.dram_tensor("i2bf", [128, 64], BF16, kind="ExternalInput").ap()
    c_masklow = nc.dram_tensor("masklow", [128, 512], BF16, kind="ExternalInput").ap()
    c_maskup = nc.dram_tensor("maskup", [128, 512], BF16, kind="ExternalInput").ap()
    c_maskneg = nc.dram_tensor("maskneg", [128, 512], BF16, kind="ExternalInput").ap()
    c_ones = nc.dram_tensor("onesc", [128, 2], BF16, kind="ExternalInput").ap()
    c_ones1 = nc.dram_tensor("ones1", [1, 128], BF16, kind="ExternalInput").ap()
    c_ones1f = nc.dram_tensor("ones1f", [1, 128], F32, kind="ExternalInput").ap()
    c_hsum = nc.dram_tensor("hsum2", [128, 2], BF16, kind="ExternalInput").ap()
    c_exp2 = nc.dram_tensor("expander2", [2, 128], BF16, kind="ExternalInput").ap()

    outT = nc.dram_tensor("outT", [D, T], F32, kind="ExternalOutput").ap()
    x2scr = nc.dram_tensor("x2scr", [D, T], F32, kind="Internal").ap()
    x2hiscr = nc.dram_tensor("x2hiscr", [D, T], BF16, kind="Internal").ap()
    x2loscr = nc.dram_tensor("x2loscr", [D, T], BF16, kind="Internal").ap()
    xq1scr = nc.dram_tensor("xq1scr", [T + 16, D], BF16, kind="Internal").ap()
    xq2scr = nc.dram_tensor("xq2scr", [T, HD], BF16, kind="Internal").ap()

    dbg = {}
    if debug:
        for nm, shp in [("d_q", [128, NP, SEG]), ("d_k", [128, NP, SEG]),
                        ("d_v", [128, NP, SEG]), ("d_kh", [128, NP, SEG]),
                        ("d_ln", [128, 8, SEG]), ("d_attn", [128, NP, SEG]),
                        ("d_ab", [128, 16]), ("d_r0", [128, 512]),
                        ("d_x0", [128, 512]), ("d_m1", [128, 512]),
                        ("d_att0", [128, 512]), ("d_n1", [128, 512]),
                        ("d_x1", [128, 512]), ("d_m2", [128, 512]),
                        ("d_x2s", [128, 8, SEG]), ("d_b1x2", [128, 1024]),
                        ("d_b1hi", [128, 1024]), ("d_b1lo", [128, 1024]),
                        ("d_xq1s", [256, 1024]), ("d_xq2s", [128, HD]),
                        ("d_xq1w", [128, 8, 272]),
                        ("d_d2", [128, 4]),
                        ("d_b1st", [128, 16]),
                        ("d_att1", [128, 512]), ("d_co1", [128, 512]),
                        ("d_khtm0", [128, 512]),
                        ("d_xq1", [128, D]), ("d_sh", [128, 32, 256]),
                        ("d_xq2", [128, HD])]:
            dbg[nm] = nc.dram_tensor(nm, shp, F32, kind="ExternalOutput").ap()

    st = contextlib.ExitStack()
    with st:
        def sb(name, shape, dt=F32):
            return st.enter_context(nc.sbuf_tensor(name, shape, dt))

        def mkpsum(name):
            return st.enter_context(nc.psum_tensor(name, [128, 512], F32))

        wU = sb("wU", [128, 8, 4 * D], BF16)
        w_attn = [wU[:, :, i * D:(i + 1) * D] for i in range(4)]
        w1qv = wU.rearrange("p a b -> p (a b)").rearrange("p (a b) -> p a b", a=8)

        u2 = sb("u2", [128, 32 * D], BF16)
        w2qv = u2.rearrange("p (a b) -> p a b", a=32)
        _o = [0]

        def carve2(n):
            v = u2[:, _o[0]:_o[0] + n]
            _o[0] += n
            return v
        af_flat = carve2(8 * SEG)
        attn_fm3 = af_flat.rearrange("p (a b) -> p a b", a=8)
        solN_P = [{k: carve2(512) for k in (1, 2, 4, 8, 16)} for _ in range(2)]
        solNT_P = [{k: carve2(512) for k in (1, 2, 4, 8, 16, 32)} for _ in range(2)]
        khtm_P = [carve2(512) for _ in range(2)]
        vtm_P = [carve2(512) for _ in range(2)]
        r_stk_P = [carve2(512) for _ in range(2)]
        ya_P = [carve2(512) for _ in range(2)]
        yb_P = [carve2(512) for _ in range(2)]
        pexp_P = [carve2(512) for _ in range(2)]
        pn_P = [carve2(512) for _ in range(2)]
        ptb_P = [carve2(512) for _ in range(2)]
        att_sb_P = [carve2(512) for _ in range(2)]
        co_bf_P = [carve2(512) for _ in range(2)]
        mA = carve2(512)
        mB = carve2(512)
        mtA = carve2(512)
        mtB = carve2(512)
        rows_bf = carve2(1024)
        norm2 = af_flat[0:2, 0:4096]
        _st0 = _o[0]
        x2st_hi = [carve2(512) for _ in range(2)]
        x2st_lo = [carve2(512) for _ in range(2)]
        b1_hi = u2[:, _st0:_st0 + 1024]
        b1_lo = u2[:, _st0 + 1024:_st0 + 2048]
        b1_sq = carve2(1024)
        assert _o[0] <= 32 * D, _o[0]

        u3 = sb("u3", [128, 6144], F32)
        xs3 = u3[:, 0:4096].rearrange("p (a b) -> p a b", a=8)
        sm_f_P = [u3[:, 4096:4608], u3[:, 4608:5120]]
        ftmp = u3[:, 4096:4608]
        rows_f = u3[:, 4096:5120]
        b1_x2tm = u3[:, 5120:6144]
        b1_ctm = b1_x2tm
        hwin = u3[:, 5120:5378]
        c1win = u3[:, 5378:5634]
        c2win = c1win
        c3win = u3[:, 5634:5890]

        u4 = sb("u4", [128, 24576], BF16)
        xbf3 = u4[:, 0:4096].rearrange("p (a b) -> p a b", a=8)
        qf3 = u4[:, 4096:8192].rearrange("p (a b) -> p a b", a=8)
        kf3 = u4[:, 8192:12288].rearrange("p (a b) -> p a b", a=8)
        vf3 = u4[:, 12288:16384].rearrange("p (a b) -> p a b", a=8)
        kh3 = u4[:, 16384:20480].rearrange("p (a b) -> p a b", a=8)
        kha3 = u4[:, 20480:24576].rearrange("p (a b) -> p a b", a=8)
        sqscr = u4[:, 24064:24576]
        sqscr = carve2(512)
        assert _o[0] <= 32 * D, _o[0]
        zmar = u4[:, 3072:4096]
        b2_sh3 = u4[:, 0:8192].rearrange("p (a b) -> p a b", a=32)
        b2_xq1w = u4[:, 8192:10368].rearrange("p (a b) -> p a b", a=8)
        b2_xq2b = [u4[:, 10368 + i * 512:10368 + (i + 1) * 512] for i in range(2)]
        b2_xq2tm = u4[:, 11392:15488]
        b2_stage = u4[:, 15488:23680]
        stage_f = b2_stage.bitcast(F32)

        identbf = sb("identbf_s", [128, 128], BF16)
        ident16 = sb("ident16_s", [16, 16], F32)
        i2bf = sb("i2bf_s", [128, 64], BF16)
        masklow = sb("masklow_s", [128, 512], BF16)
        maskup = sb("maskup_s", [128, 512], BF16)
        maskneg = sb("maskneg_s", [128, 512], BF16)
        onesc = sb("onesc_s", [128, 2], BF16)
        onescf = sb("onescf_s", [128, 1], F32)
        ones1 = sb("ones1_s", [1, 128], BF16)
        ones1f = sb("ones1f_s", [1, 128], F32)
        hsum2 = sb("hsum2_s", [128, 2], BF16)
        exp2 = sb("exp2_s", [2, 128], BF16)
        ab_sb = sb("ab_sb", [16, 2], F32)
        ab_row = sb("ab_row", [1, 32], F32)
        atab = sb("atab", [128, NP], F32)
        btab = sb("btab", [128, NP], F32)
        stat_sc = sb("stat_sc", [128, 32], F32)
        taps_sb = sb("taps_sb", [128, 96], F32)
        cb_sb = sb("cb_sb", [128, 32], F32)
        wstat = sb("wstat", [128, 40], F32)
        mwcol = sb("mwcol", [128, 4], F32)
        d2tab = sb("d2tab", [128, 4], BF16)
        d2row = sb("d2row", [1, 512], BF16)
        d2bc = sb("d2bc", [128, 512], BF16)
        eps_col = sb("eps_col", [128, 1], F32)
        magic_col = sb("magic_col", [128, 1], F32)

        PB = [mkpsum(f"pb{i}") for i in range(6)]
        BM = mkpsum("pbM")
        BMT = mkpsum("pbMT")

        sch = Sched(nc, st)
        bank_guard = {id(b): None for b in PB + [BM, BMT]}
        bank_i = [0]

        def next_bank():
            b = PB[bank_i[0] % 6]
            bank_i[0] += 1
            return b, bank_guard[id(b)]

        def set_guard(b, tok):
            bank_guard[id(b)] = tok

        def pair_mms(e, b, lhs3, rhs3, lcol, rcol):
            last = None
            for p in range(NP):
                lh = lhs3[:, p, lcol] if lcol is not None else lhs3[:, p, :]
                rh = rhs3[:, p, rcol] if rcol is not None else rhs3[:, p, :]
                last = e.matmul(b[0:64, 64 * p:64 * p + 64], lh[0:64], rh[0:64],
                                start=True, stop=True, tile_position=(0, 0))
                last = e.matmul(b[64:128, 64 * p:64 * p + 64], lh[64:128],
                                rh[64:128], start=True, stop=True,
                                tile_position=(64, 64))
            return last

        def pair_trans(e, b, src3, col):
            last = None
            for p in range(NP):
                sp = src3[:, p, col] if col is not None else src3[:, p, :]
                last = e.matmul(b[0:64, 64 * p:64 * p + 64], sp[0:64],
                                identbf[0:64, 0:64], start=True, stop=True,
                                tile_position=(0, 0))
                last = e.matmul(b[64:128, 64 * p:64 * p + 64], sp[64:128],
                                i2bf[64:128, :], start=True, stop=True,
                                tile_position=(64, 64))
            return last

        cl = []
        for dst, src in ((identbf, c_identbf), (ident16, c_ident16),
                         (i2bf, c_i2bf), (masklow, c_masklow), (maskup, c_maskup),
                         (maskneg, c_maskneg), (onesc, c_ones), (ones1, c_ones1),
                         (ones1f, c_ones1f), (hsum2, c_hsum), (exp2, c_exp2),
                         (ab_sb, ab_in), (taps_sb, taps_in), (cb_sb, cb_in)):
            cl.append(sch.dma("sync", (lambda e, dst=dst, src=src:
                                       e.dma_start(out=dst[:], in_=src))))
        t_consts = cl[-1]
        t_w = []
        for i, w in enumerate((wqT, wkT, wvT, woT)):
            wv_ = w.rearrange("(a p) o -> p a o", p=128)
            t_w.append(sch.dma("gps", (lambda e, i=i, wv_=wv_:
                                       e.dma_start(out=w_attn[i], in_=wv_))))
        t_of = sch.on("dve", lambda e: (e.memset(onescf[:], 1.0),
                                        e.drain())[-1])
        t_eps = sch.on("dve", lambda e: (e.memset(eps_col[:], EPS),
                                         e.drain())[-1])
        t_mgc = sch.on("dve", lambda e: (e.memset(magic_col[:], MAGICF),
                                         e.drain())[-1])
        t_zm = sch.on("dve", lambda e: e.memset(zmar[0:16, :], 0.0))
        t_zs1 = sch.dma("sync", lambda e: e.dma_start(out=xq1scr[0:8, :],
                                                      in_=zmar[0:8, :]), deps=[t_zm])
        t_zs2 = sch.dma("sync", lambda e: e.dma_start(out=xq1scr[T + 8:T + 16, :],
                                                      in_=zmar[8:16, :]), deps=[t_zm])

        t_sig = sch.on("act", lambda e: e.activation(ab_sb[:], ab_sb[:], AF.Sigmoid),
                       deps=[cl[11]])
        t_s1 = sch.on("dve", lambda e: (e.tensor_scalar(
            out=ab_sb[:, 0:1], in0=ab_sb[:, 0:1], scalar1=0.5, scalar2=None,
            op0=AL.mult), e.drain())[-1], deps=[t_sig])
        t_s2 = sch.on("dve", lambda e: (e.tensor_scalar(
            out=ab_sb[:, 1:2], in0=ab_sb[:, 1:2], scalar1=2.0, scalar2=None,
            op0=AL.mult), e.drain())[-1], deps=[t_sig])
        bA, gA = next_bank()
        t_abt = sch.on("pe", (lambda e, b=bA: (
            e.matmul(b[0:1, 0:16], ab_sb[0:16, 0:1], ident16[:, :],
                     start=True, stop=True),
            e.matmul(b[0:1, 16:32], ab_sb[0:16, 1:2], ident16[:, :],
                     start=True, stop=True))[-1]), deps=TL(t_s1, t_s2, gA, cl[1]))
        t_abr = sch.on("dve", (lambda e, b=bA: e.tensor_copy(ab_row[0:1, 0:16],
                                                             b[0:1, 0:16])),
                       deps=[t_abt])
        t_abr2 = sch.on("dve", (lambda e, b=bA: (e.tensor_copy(
            ab_row[0:1, 16:32], b[0:1, 16:32]), e.drain())[-1]),
                        deps=[t_abt])
        set_guard(bA, t_abr2)
        bA, gA = next_bank()

        def mk_ab(e, b=bA):
            last = None
            for p in range(NP):
                for h in range(2):
                    for which in range(2):
                        rhs = ab_row[0:1, 16 * which + 2 * p + h:
                                     16 * which + 2 * p + h + 1]
                        tp = (0, 0) if h == 0 else (0, 64)
                        last = e.matmul(b[64 * h:64 * h + 64,
                                          8 * which + p:8 * which + p + 1],
                                        ones1f[0:1, 0:64], rhs,
                                        start=True, stop=True, tile_position=tp)
            return last
        t_abm = sch.on("pe", mk_ab, deps=TL(t_abr, t_abr2, gA, cl[8]))

        def ev_ab(e, b=bA):
            e.tensor_copy(atab[:, :], b[:, 0:8])
            e.tensor_copy(btab[:, :], b[:, 8:16])
            return e.drain()
        t_abe = sch.on("dve", ev_ab, deps=[t_abm])
        set_guard(bA, t_abe)

        if debug:
            t_dab = sch.on("dve", lambda e: (
                e.tensor_copy(stat_sc[:, 0:8], atab[:, :]),
                e.tensor_copy(stat_sc[:, 8:16], btab[:, :]))[-1], deps=[t_abe])
            sch.dma("gps", lambda e: e.dma_start(out=dbg["d_ab"],
                                                 in_=stat_sc[:, 0:16]),
                    deps=[t_dab])
        t_mz = sch.on("dve", lambda e: (
            e.memset(mA[:], 0.0), e.memset(mtA[:], 0.0))[-1])

        xT_v = xT.rearrange("(a p) t -> p a t", p=128)
        x2scr_v = x2scr.rearrange("(a p) t -> p a t", p=128)
        x2hi_v = x2hiscr.rearrange("(a p) t -> p a t", p=128)
        x2lo_v = x2loscr.rearrange("(a p) t -> p a t", p=128)
        outT_v = outT.rearrange("(a p) t -> p a t", p=128)
        w1T_v = w1T.rearrange("(a p) o -> p a o", p=128)
        w2T_v = w2T.rearrange("(a p) o -> p a o", p=128)

        RR = lambda t: t.rearrange("p (a b) -> p a b", a=NP)
        solNv_P = [{k: RR(v) for k, v in d.items()} for d in solN_P]
        solNTv_P = [{k: RR(v) for k, v in d.items()} for d in solNT_P]
        n2v = norm2.rearrange("p (a b) -> p a b", a=8)

        tok_seg_done = None
        tok_x2_store = {}
        tok_m_ready = t_mz
        last_xq1 = t_zs2
        tok_b1_prev = None

        for s in range(NSEG):
            seg = slice(s * SEG, (s + 1) * SEG)
            m_base, m_run = (mA, mB) if s % 2 == 0 else (mB, mA)
            mt_base, mt_run = (mtA, mtB) if s % 2 == 0 else (mtB, mtA)
            m_runv = m_run.rearrange("p (a b) -> p a b", a=NP)
            mt_runv = mt_run.rearrange("p (a b) -> p a b", a=NP)

            if s == 0:
                ldxa = sch.dma("sync", (lambda e, seg=seg: e.dma_start(
                    out=xs3[:, 0:4, :], in_=xT_v[:, 0:4, seg])), deps=())
                ldxb = sch.dma("sync", (lambda e, seg=seg: e.dma_start(
                    out=xs3[:, 4:8, :], in_=xT_v[:, 4:8, seg])), deps=())
            else:
                ldxa, ldxb = pending_ldx

            def cast_h(e, c0):
                last = None
                for c in range(c0, c0 + 4):
                    last = e.tensor_copy(xbf3[:, c, :], xs3[:, c, :])
                return last
            cast_a = sch.on("dve", (lambda e: cast_h(e, 0)), deps=TL(
                ldxa, t_zs1 if s == 0 else None, t_zs2 if s == 0 else None))
            cast_b = sch.on("dve", (lambda e: cast_h(e, 4)), deps=TL(ldxb))
            t_cast = cast_b

            bS0, g0 = next_bank()
            bS1, g1 = next_bank()
            st_last = None
            for c in range(8):
                t_ch = cast_a if c < 4 else cast_b
                t_sqc = sch.on("act", (lambda e, c=c: e.activation(
                    sqscr[:, :], xbf3[:, c, :], AF.Square)),
                    deps=TL(t_ch, st_last))
                sch.on("pe", (lambda e, c=c, b=bS0: e.matmul(
                    b[0:1, 0:512], onesc[:, 0:1], xbf3[:, c, :],
                    start=(c == 0), stop=(c == 7))),
                    deps=TL(t_ch, t_consts, g0 if c == 0 else None, *t_w))
                st_last = sch.on("pe", (lambda e, c=c, b=bS1: e.matmul(
                    b[0:1, 0:512], onesc[:, 0:1], sqscr[:, :],
                    start=(c == 0), stop=(c == 7))),
                    deps=TL(t_sqc, g1 if c == 0 else None))
            t_stat = st_last

            def rowmath(e, b0=bS0, b1=bS1):
                e.tensor_scalar(out=rows_f[0:1, 0:512], in0=b0[0:1, 0:512],
                                scalar1=1.0 / D, scalar2=None, op0=AL.mult)
                e.tensor_scalar(out=rows_bf[0:1, 0:512], in0=rows_f[0:1, 0:512],
                                scalar1=-1.0, scalar2=None, op0=AL.mult)
                e.tensor_tensor(rows_f[0:1, 512:1024], rows_f[0:1, 0:512],
                                rows_f[0:1, 0:512], AL.mult)
                e.tensor_scalar(out=rows_f[0:1, 0:512], in0=b1[0:1, 0:512],
                                scalar1=1.0 / D, scalar2=None, op0=AL.mult)
                return e.tensor_tensor(rows_f[0:1, 0:512], rows_f[0:1, 0:512],
                                       rows_f[0:1, 512:1024], AL.subtract)
            t_rm = sch.on("dve", rowmath, deps=[t_stat])
            t_sd = sch.on("act", lambda e: e.activation(
                rows_f[0:1, 512:1024], rows_f[0:1, 0:512], AF.Sqrt,
                bias=eps_col[0:1, 0:1], scale=1.0), deps=TL(t_rm, t_eps))
            t_rr = sch.on("dve", lambda e: e.reciprocal(
                rows_bf[0:1, 512:1024], rows_f[0:1, 512:1024]), deps=[t_sd])
            set_guard(bS0, t_rm)
            set_guard(bS1, t_rm)

            bB0, g0 = next_bank()
            bB1, g1 = next_bank()
            t_bc = sch.on("pe", (lambda e, b0=bB0, b1=bB1: (
                e.matmul(b0[:, 0:512], ones1[0:1, :], rows_bf[0:1, 0:512],
                         start=True, stop=True),
                e.matmul(b1[:, 0:512], ones1[0:1, :], rows_bf[0:1, 512:1024],
                         start=True, stop=True))[-1]), deps=TL(t_rm, t_rr, g0, g1))

            def ln_apply(e, b0=bB0, b1=bB1):
                last = None
                for c in range(8):
                    e.tensor_tensor(ftmp[:, :], xs3[:, c, :], b0[:, 0:512], AL.add)
                    last = e.tensor_tensor(xbf3[:, c, :], ftmp[:, :], b1[:, 0:512],
                                           AL.mult)
                return last
            t_ln = sch.on("dve", ln_apply, deps=TL(t_bc, t_stat))
            set_guard(bB0, t_ln)
            set_guard(bB1, t_ln)
            if debug and s == 0:
                sch.dma("gps", lambda e: e.dma_start(out=dbg["d_ln"],
                                                     in_=xbf3[:, :, :]),
                        deps=[t_ln])

            evac_toks = []
            for wi, dst in ((0, qf3), (1, kf3), (2, vf3)):
                for ot in range(8):
                    b, g = next_bank()

                    def mmproj(e, wi=wi, ot=ot, b=b):
                        last = None
                        for kc in range(8):
                            last = e.matmul(b[:, 0:512],
                                            w_attn[wi][:, kc, 128 * ot:128 * ot + 128],
                                            xbf3[:, kc, :],
                                            start=(kc == 0), stop=(kc == 7))
                        return last
                    t_mm = sch.on("pe", mmproj, deps=TL(t_ln, g, *t_w))
                    t_ev = sch.on("dve", (lambda e, dst=dst, ot=ot, b=b:
                                          e.tensor_copy(dst[:, ot, :], b[:, 0:512])),
                                  deps=[t_mm])
                    set_guard(b, t_ev)
                    evac_toks.append(t_ev)
            t_qkv = evac_toks[-1]

            nrm_toks = []
            sq_prev = None
            for p in range(NP):
                t_ks = sch.on("act", (lambda e, p=p: e.activation(
                    sqscr[:, :], kf3[:, p, :], AF.Square)),
                    deps=TL(evac_toks[8 + p], sq_prev, t_stat))
                b, g = next_bank()
                t_nm = sch.on("pe", (lambda e, b=b: e.matmul(
                    b[0:2, 0:512], hsum2[:, :], sqscr[:, :],
                    start=True, stop=True)), deps=TL(t_ks, g))
                sq_prev = t_nm
                t_ne = sch.on("dve", (lambda e, p=p, b=b: e.tensor_copy(
                    n2v[0:2, p, :], b[0:2, 0:512])), deps=[t_nm])
                set_guard(b, t_ne)
                nrm_toks.append(t_ne)
            t_sd2 = sch.on("act", lambda e: e.activation(
                norm2[0:2, :], norm2[0:2, :], AF.Sqrt), deps=nrm_toks)
            t_mx2 = sch.on("dve", lambda e: e.tensor_scalar_max(
                norm2[0:2, :], norm2[0:2, :], 1e-12), deps=[t_sd2])
            t_rsc = sch.on("dve", lambda e: e.reciprocal(
                norm2[0:2, :], norm2[0:2, :]), deps=[t_mx2])
            kh_toks = []
            for p in range(NP):
                b, g = next_bank()
                t_bm = sch.on("pe", (lambda e, p=p, b=b: e.matmul(
                    b[:, 0:512], exp2[0:2, :], n2v[0:2, p, :],
                    start=True, stop=True)), deps=TL(t_rsc, g))
                t_khe = sch.on("dve", (lambda e, p=p, b=b: e.tensor_tensor(
                    kh3[:, p, :], kf3[:, p, :], b[:, 0:512], AL.mult)), deps=[t_bm])
                t_kha = sch.on("dve", (lambda e, p=p: e.tensor_scalar(
                    out=kha3[:, p, :], in0=kh3[:, p, :], scalar1=atab[:, p:p + 1],
                    scalar2=None, op0=AL.mult)), deps=TL(t_khe, t_abe))
                set_guard(b, t_kha)
                kh_toks.append(t_kha)
            t_khall = kh_toks[-1]
            if debug and s == 0:
                for nm, src, dep in (("d_q", qf3, t_qkv), ("d_k", kf3, t_qkv),
                                     ("d_v", vf3, t_qkv), ("d_kh", kh3, t_khall)):
                    sch.dma("gps", (lambda e, nm=nm, src=src: e.dma_start(
                        out=dbg[nm], in_=src[:, :, :])), deps=[dep])

            mstate = {"prev_mevac": tok_m_ready}
            CSTATE = {}

            def cvars(c):
                par = c % 2
                d = dict(
                    col=slice(64 * c, 64 * c + 64), par=par,
                    khtm=khtm_P[par], vtm=vtm_P[par], r_stk=r_stk_P[par],
                    ya=ya_P[par], yb=yb_P[par], pexp=pexp_P[par],
                    pn=pn_P[par], ptb=ptb_P[par], att_sb=att_sb_P[par],
                    co_bf=co_bf_P[par], sm_f=sm_f_P[par],
                    solN=solN_P[par], solNT=solNT_P[par],
                    solNv=solNv_P[par], solNTv=solNTv_P[par],
                    so=16 * par,
                )
                d["khtmv"] = RR(d["khtm"])
                d["vtmv"] = RR(d["vtm"])
                d["ptv"] = RR(d["ptb"])
                d["pexpv"] = RR(d["pexp"])
                d["r_stkv"] = RR(d["r_stk"])
                d["yav"], d["ybv"] = RR(d["ya"]), RR(d["yb"])
                d["co_bfv"] = RR(d["co_bf"])
                d["pnv"] = RR(d["pn"])
                return d

            def stage_head_units(c):
                v = CSTATE[c] = cvars(c)
                units = []

                def u_t1():
                    b, g = next_bank()
                    t1 = sch.on("pe", (lambda e, b=b, col=v["col"]: pair_trans(
                        e, b, kh3, col)), deps=TL(t_khall, g, t_consts))
                    v["t1e"] = sch.on("dve", (lambda e, b=b, khtm=v["khtm"]:
                                              e.tensor_copy(khtm[:, :],
                                                            b[:, 0:512])),
                                      deps=[t1])
                    set_guard(b, v["t1e"])
                units.append(u_t1)

                def u_t2():
                    b, g = next_bank()
                    t2 = sch.on("pe", (lambda e, b=b, col=v["col"]: pair_trans(
                        e, b, vf3, col)), deps=TL(t_qkv, g))
                    v["t2e"] = sch.on("act", (lambda e, b=b, vtm=v["vtm"]: e.copy(
                        vtm[:, :], b[:, 0:512])), deps=[t2])
                    set_guard(b, v["t2e"])
                units.append(u_t2)

                def u_g():
                    b, g = next_bank()
                    tg = sch.on("pe", (lambda e, b=b, col=v["col"]: pair_mms(
                        e, b, kh3, kha3, col, col)), deps=TL(t_khall, g))
                    tn1 = sch.on("dve", (lambda e, b=b, solN=v["solN"]:
                                         e.tensor_tensor(solN[1][:, :],
                                                         b[:, 0:512],
                                                         masklow[:, :], AL.mult)),
                                 deps=[tg])
                    tnt1 = sch.on("dve", (lambda e, b=b, solNT=v["solNT"]:
                                          e.tensor_tensor(solNT[1][:, :],
                                                          b[:, 0:512],
                                                          maskup[:, :], AL.mult)),
                                  deps=[tg])
                    set_guard(b, tnt1)
                    v["form_tok"] = {1: (tn1, tnt1)}
                units.append(u_g)

                def u_s():
                    so = v["so"]
                    b, g = next_bank()
                    ts_ = sch.on("pe", (lambda e, b=b, col=v["col"]: pair_mms(
                        e, b, qf3, kh3, col, col)), deps=TL(t_qkv, t_khall, g))
                    tsm = sch.on("dve", (lambda e, b=b, sm_f=v["sm_f"]:
                                         e.tensor_tensor(sm_f[:, :], b[:, 0:512],
                                                         maskneg[:, :], AL.add)),
                                 deps=[ts_])
                    set_guard(b, tsm)
                    tmx = sch.on("dve", (lambda e, sm_f=v["sm_f"], so=so: (
                        e.reduce_max(stat_sc[:, so:so + 1], sm_f[:, :],
                                     axis=AX.X), e.drain())[-1]), deps=[tsm])
                    tnb = sch.on("dve", (lambda e, so=so: (e.tensor_scalar(
                        out=stat_sc[:, so + 1:so + 2],
                        in0=stat_sc[:, so:so + 1], scalar1=-0.125, scalar2=None,
                        op0=AL.mult), e.drain())[-1]), deps=[tmx])
                    texp = sch.on("act", (lambda e, sm_f=v["sm_f"],
                                          pexp=v["pexp"], so=so: e.activation(
                                              pexp[:, :], sm_f[:, :], AF.Exp,
                                              bias=stat_sc[:, so + 1:so + 2],
                                              scale=0.125)), deps=[tnb])
                    trs = sch.on("dve", (lambda e, pexpv=v["pexpv"], so=so: (
                        e.reduce_sum(stat_sc[:, so + 2:so + 10], pexpv,
                                     axis=AX.X), e.drain())[-1]), deps=[texp])
                    trr2 = sch.on("dve", (lambda e, so=so: (e.reciprocal(
                        stat_sc[:, so + 2:so + 10],
                        stat_sc[:, so + 2:so + 10]), e.drain())[-1]),
                        deps=[trs])

                    def pnorm(e, pexp=v["pexp"], pn=v["pn"], so=so):
                        for p in range(NP):
                            e.tensor_scalar(
                                out=pn[:, 64 * p:64 * p + 64],
                                in0=pexp[:, 64 * p:64 * p + 64],
                                scalar1=stat_sc[:, so + 2 + p:so + 3 + p],
                                scalar2=None, op0=AL.mult)
                        return e.drain()
                    v["tpn"] = sch.on("dve", pnorm, deps=[trr2])
                units.append(u_s)

                for lv_src, lv_dst in ((1, 2), (2, 4), (4, 8), (8, 16), (16, 32)):
                    def u_form(lv_src=lv_src, lv_dst=lv_dst):
                        bN, gN = next_bank()
                        bT_, gT = next_bank()
                        last_only = (lv_dst == 32)

                        def form(e, bN=bN, bT_=bT_, srcN=v["solNv"][lv_src],
                                 srcNT=v["solNTv"][lv_src], last_only=last_only):
                            last = None
                            if not last_only:
                                last = pair_mms(e, bN, srcNT, srcN, None, None)
                            last = pair_mms(e, bT_, srcN, srcNT, None, None)
                            return last
                        tf = sch.on("pe", form,
                                    deps=TL(*v["form_tok"][lv_src], gN, gT))
                        toks = []
                        if not last_only:
                            tevN = sch.on("act", (lambda e, lv=lv_dst, bN=bN,
                                                  solN=v["solN"]: e.copy(
                                                      solN[lv][:, :],
                                                      bN[:, 0:512])), deps=[tf])
                            set_guard(bN, tevN)
                            toks.append(tevN)
                        if lv_dst in (4,):
                            tevT = sch.on("act", (lambda e, lv=lv_dst, bT_=bT_,
                                                  solNT=v["solNT"]: e.copy(
                                                      solNT[lv][:, :],
                                                      bT_[:, 0:512])), deps=[tf])
                        else:
                            tevT = sch.on("dve", (lambda e, lv=lv_dst, bT_=bT_,
                                                  solNT=v["solNT"]:
                                                  e.tensor_copy(
                                                      solNT[lv][:, :],
                                                      bT_[:, 0:512])), deps=[tf])
                        set_guard(bT_, tevT)
                        toks.append(tevT)
                        v["form_tok"][lv_dst] = tuple(toks)
                    units.append(u_form)
                return units

            def stage_tail_units(c):
                v = CSTATE[c]
                units = []

                def u_r():
                    r_src = mt_base if c == 0 else mt_run
                    r_srcv = RR(r_src)
                    b, g = next_bank()
                    tr = sch.on("pe", (lambda e, b=b, col=v["col"], mtv=r_srcv:
                                       pair_mms(e, b, kha3, mtv, col, None)),
                                deps=TL(t_khall, g, mstate["prev_mevac"]))

                    def rbuild(e, b=b, r_stk=v["r_stk"], vtm=v["vtm"]):
                        for p in range(NP):
                            e.scalar_tensor_tensor(
                                out=r_stk[:, 64 * p:64 * p + 64],
                                in0=vtm[:, 64 * p:64 * p + 64],
                                scalar=btab[:, p:p + 1],
                                in1=b[:, 64 * p:64 * p + 64],
                                op0=AL.mult, op1=AL.subtract)
                        return e.drain()
                    v["trb"] = sch.on("dve", rbuild,
                                      deps=TL(tr, v["t2e"], t_abe))
                    set_guard(b, v["trb"])
                    v["ysrc"], v["ysrcv"] = v["r_stk"], v["r_stkv"]
                    v["ycur"] = v["ya"]
                    v["ytok"] = v["trb"]
                units.append(u_r)

                for k in (32, 16, 8, 4, 2, 1):
                    def u_ap(k=k):
                        b, g = next_bank()
                        tap = sch.on("pe", (lambda e, b=b, ntv=v["solNTv"][k],
                                            yv=v["ysrcv"]: pair_mms(
                                                e, b, ntv, yv, None, None)),
                                     deps=TL(*v["form_tok"][k], v["ytok"], g))
                        op = AL.subtract if k == 1 else AL.add
                        ycur = v["ycur"]
                        tye = sch.on("dve", (lambda e, b=b, ysrc=v["ysrc"],
                                             ycur=ycur, op=op: e.tensor_tensor(
                                                 ycur[:, :], ysrc[:, :],
                                                 b[:, 0:512], op)), deps=[tap])
                        set_guard(b, tye)
                        nxt = v["yb"] if ycur is v["ya"] else v["ya"]
                        v["ysrc"] = ycur
                        v["ysrcv"] = v["yav"] if ycur is v["ya"] else v["ybv"]
                        v["ycur"] = nxt
                        v["ytok"] = tye
                    units.append(u_ap)

                def u_pt():
                    v["xfinv"] = v["ysrcv"]
                    v["t_x"] = v["ytok"]
                    b, g = next_bank()
                    tpt = sch.on("pe", (lambda e, b=b, pnv=v["pnv"]: pair_trans(
                        e, b, pnv, None)), deps=TL(v["tpn"], g))
                    v["tpte"] = sch.on("act", (lambda e, b=b, ptb=v["ptb"]:
                                               e.copy(ptb[:, :], b[:, 0:512])),
                                       deps=[tpt])
                    set_guard(b, v["tpte"])
                units.append(u_pt)

                def u_av():
                    bAV, gAV = next_bank()
                    tav = sch.on("pe", (lambda e, b=bAV, ptv=v["ptv"],
                                        vtmv=v["vtmv"]: pair_mms(
                                            e, b, ptv, vtmv, None, None)),
                                 deps=TL(v["tpte"], v["t2e"], gAV))
                    v["tave"] = sch.on("dve", (lambda e, b=bAV,
                                               att_sb=v["att_sb"]:
                                               e.tensor_copy(att_sb[:, :],
                                                             b[:, 0:512])),
                                       deps=[tav])
                    set_guard(bAV, v["tave"])
                units.append(u_av)

                def u_me():
                    mem_src = m_base if c == 0 else m_run
                    msrcv = RR(mem_src)
                    bME, gME = next_bank()
                    tme = sch.on("pe", (lambda e, b=bME, col=v["col"],
                                        mv=msrcv: pair_mms(
                                            e, b, qf3, mv, col, None)),
                                 deps=TL(t_qkv, mstate["prev_mevac"], gME))
                    v["tco"] = sch.on("dve", (lambda e, b=bME, co_bf=v["co_bf"],
                                              att_sb=v["att_sb"]:
                                              e.scalar_tensor_tensor(
                                                  out=co_bf[:, :],
                                                  in0=b[:, 0:512], scalar=0.1,
                                                  in1=att_sb[:, :],
                                                  op0=AL.mult, op1=AL.add)),
                                      deps=TL(tme, v["tave"]))
                    set_guard(bME, v["tco"])
                units.append(u_me)

                def u_of():
                    bOF, gOF = next_bank()
                    tof = sch.on("pe", (lambda e, b=bOF, co_bfv=v["co_bfv"]:
                                        pair_trans(e, b, co_bfv, None)),
                                 deps=TL(v["tco"], gOF))
                    v["t_outfm_ev"] = sch.on("dve", (lambda e, b=bOF, c=c:
                                                     e.tensor_copy(
                                                         attn_fm3[:, :,
                                                                  64 * c:
                                                                  64 * c + 64],
                                                         b[:, 0:512].rearrange(
                                                             "p (a b) -> p a b",
                                                             a=NP))),
                                             deps=[tof])
                    set_guard(bOF, v["t_outfm_ev"])
                units.append(u_of)

                def u_m():
                    def mupd(e, xv=v["xfinv"], khtmv=v["khtmv"]):
                        last = None
                        for p in range(NP):
                            last = e.matmul(BM[0:64, 64 * p:64 * p + 64],
                                            xv[:, p, :][0:64],
                                            khtmv[:, p, :][0:64],
                                            start=True, stop=True,
                                            tile_position=(0, 0))
                            last = e.matmul(BM[64:128, 64 * p:64 * p + 64],
                                            xv[:, p, :][64:128],
                                            khtmv[:, p, :][64:128],
                                            start=True, stop=True,
                                            tile_position=(64, 64))
                        for p in range(NP):
                            last = e.matmul(BMT[0:64, 64 * p:64 * p + 64],
                                            khtmv[:, p, :][0:64],
                                            xv[:, p, :][0:64],
                                            start=True, stop=True,
                                            tile_position=(0, 0))
                            last = e.matmul(BMT[64:128, 64 * p:64 * p + 64],
                                            khtmv[:, p, :][64:128],
                                            xv[:, p, :][64:128],
                                            start=True, stop=True,
                                            tile_position=(64, 64))
                        return last
                    t_mupd = sch.on("pe", mupd, deps=TL(
                        v["t_x"], v["t1e"], mstate["prev_mevac"],
                        bank_guard[id(BM)], bank_guard[id(BMT)]))

                    def mevac(e, c=c, mb=m_base, mr=m_run, mtb=mt_base,
                              mtr=mt_run):
                        srcm = mb if c == 0 else mr
                        srcmt = mtb if c == 0 else mtr
                        e.tensor_tensor(mr[:, :], srcm[:, :], BM[:, 0:512],
                                        AL.add)
                        return e.tensor_tensor(mtr[:, :], srcmt[:, :],
                                               BMT[:, 0:512], AL.add)
                    t_mevac = sch.on("dve", mevac, deps=TL(t_mupd, v["tco"]))
                    set_guard(BM, t_mevac)
                    set_guard(BMT, t_mevac)
                    mstate["prev_mevac"] = t_mevac
                units.append(u_m)
                return units

            def interleave(a, b):
                out = []
                ia = ib = 0
                while ia < len(a) or ib < len(b):
                    if ia < len(a):
                        out.append(a[ia]); ia += 1
                    if ib < len(b):
                        out.append(b[ib]); ib += 1
                return out

            pending = stage_head_units(0)
            for u in pending:
                u()
            for c in range(CH):
                tail = stage_tail_units(c)
                head = stage_head_units(c + 1) if c + 1 < CH else []
                for u in interleave(tail, head):
                    u()
            t_outfm_ev = CSTATE[CH - 1]["t_outfm_ev"]
            prev_mevac = mstate["prev_mevac"]

            tok_m_ready = prev_mevac
            if debug and s == 0:
                sch.dma("gps", lambda e: e.dma_start(
                    out=dbg["d_attn"], in_=attn_fm3[:, :, :]), deps=[t_outfm_ev])

            x2_sts = []
            hi_sts = []
            lo_sts = []
            res_toks = []
            lo_cp = []
            prev_hi = [tok_b1_prev, tok_b1_prev]
            for ot in range(8):
                b, g = next_bank()

                def mmwo(e, ot=ot, b=b):
                    last = None
                    for kc in range(8):
                        last = e.matmul(b[:, 0:512],
                                        w_attn[3][:, kc, 128 * ot:128 * ot + 128],
                                        attn_fm3[:, kc, :],
                                        start=(kc == 0), stop=(kc == 7))
                    return last
                t_mm = sch.on("pe", mmwo, deps=TL(t_outfm_ev, g))
                t_res = sch.on("dve", (lambda e, ot=ot, b=b: e.tensor_tensor(
                    xs3[:, ot, :], xs3[:, ot, :], b[:, 0:512], AL.add)),
                    deps=[t_mm])
                set_guard(b, t_res)
                res_toks.append(t_res)
                hi = x2st_hi[ot % 2]
                lo = x2st_lo[ot % 2]
                t_hi = sch.on("dve", (lambda e, ot=ot, hi=hi: e.tensor_copy(
                    hi, xs3[:, ot, :])), deps=TL(t_res, prev_hi[ot % 2]))
                t_lo = sch.on("dve", (lambda e, ot=ot, hi=hi, lo=lo:
                                      e.scalar_tensor_tensor(
                                          out=lo, in0=hi, scalar=-1.0,
                                          in1=xs3[:, ot, :], op0=AL.mult,
                                          op1=AL.add)), deps=[t_hi])
                s2 = sch.dma("act", (lambda e, ot=ot, seg=seg, hi=hi: e.dma_start(
                    out=x2hi_v[:, ot, seg], in_=hi)), deps=[t_hi])
                s3 = sch.dma("gps", (lambda e, ot=ot, seg=seg, lo=lo: e.dma_start(
                    out=x2lo_v[:, ot, seg], in_=lo)), deps=[t_lo])
                hi_sts.append(s2)
                lo_sts.append(s3)
                lo_cp.append(t_lo)
                prev_hi[ot % 2] = s3
            s1a = sch.dma("sync", (lambda e, seg=seg: e.dma_start(
                out=x2scr_v[:, 0:4, seg], in_=xs3[:, 0:4, :])),
                deps=res_toks[0:4])
            if s + 1 < NSEG:
                nsl = slice((s + 1) * SEG, (s + 2) * SEG)
                n_ldxa = sch.dma("sync", (lambda e, nsl=nsl: e.dma_start(
                    out=xs3[:, 0:4, :], in_=xT_v[:, 0:4, nsl])),
                    deps=[s1a, lo_cp[3]])
            s1b = sch.dma("sync", (lambda e, seg=seg: e.dma_start(
                out=x2scr_v[:, 4:8, seg], in_=xs3[:, 4:8, :])),
                deps=res_toks[4:8])
            if s + 1 < NSEG:
                n_ldxb = sch.dma("sync", (lambda e, nsl=nsl: e.dma_start(
                    out=xs3[:, 4:8, :], in_=xT_v[:, 4:8, nsl])),
                    deps=[s1b, lo_cp[7]])
                pending_ldx = (n_ldxa, n_ldxb)
            s1 = s1b
            x2_sts = [s1a, s1b] + hi_sts + lo_sts
            tok_x2_store[s] = x2_sts
            if debug and s == 0:
                sch.dma("gps", lambda e: e.dma_start(
                    out=dbg["d_x2s"], in_=xs3[:, :, :]), deps=[x2_sts[-1]])
            if debug and s == 0:
                sch.dma("gps", lambda e: e.dma_start(
                    out=dbg.setdefault("d_x2_unused", dbg["d_ln"]),
                    in_=xs3[:, :, :]), deps=[x2_sts[-1]]) if False else None

            b1_sts = []
            prev_q = None
            for tb in range(4):
                tok0 = s * SEG + tb * 128
                tcs = slice(tok0, tok0 + 128)
                t_h = sch.dma("sync", (lambda e, tcs=tcs: e.dma_start_transpose(
                    b1_hi[:, :], x2hiscr[:, tcs])),
                    deps=TL(hi_sts[-1], hi_sts[-2], prev_q))
                t_l = sch.dma("act", (lambda e, tcs=tcs: e.dma_start_transpose(
                    b1_lo[:, :], x2loscr[:, tcs])),
                    deps=TL(lo_sts[-1], lo_sts[-2], prev_q))
                t_rb = sch.on("dve", lambda e: e.scalar_tensor_tensor(
                    out=b1_x2tm[:, :], in0=b1_hi[:, :], scalar=1.0,
                    in1=b1_lo[:, :], op0=AL.mult, op1=AL.add,
                    accum_out=stat_sc[:, 26:27]), deps=[t_h, t_l])
                t_s2_ = sch.on("dve", lambda e: (e.scalar_tensor_tensor(
                    out=b1_sq[:, :], in0=b1_hi[:, :], scalar=1.0, in1=b1_hi[:, :],
                    op0=AL.mult, op1=AL.mult,
                    accum_out=stat_sc[:, 27:28]), e.drain())[-1],
                    deps=[t_h, t_rb])

                def b1math(e):
                    sc = stat_sc
                    e.tensor_scalar(out=sc[:, 28:29], in0=sc[:, 26:27],
                                    scalar1=1.0 / D, scalar2=None, op0=AL.mult)
                    e.drain()
                    e.tensor_scalar(out=sc[:, 29:30], in0=sc[:, 28:29],
                                    scalar1=-1.0, scalar2=None, op0=AL.mult)
                    e.tensor_tensor(sc[:, 30:31], sc[:, 28:29], sc[:, 28:29],
                                    AL.mult)
                    e.drain()
                    e.scalar_tensor_tensor(
                        out=sc[:, 31:32], in0=sc[:, 27:28], scalar=1.0 / D,
                        in1=sc[:, 30:31], op0=AL.mult, op1=AL.subtract)
                    return e.drain()
                t_bm_ = sch.on("dve", b1math, deps=[t_rb, t_s2_])
                t_bsd = sch.on("act", lambda e: e.activation(
                    stat_sc[:, 30:31], stat_sc[:, 31:32], AF.Sqrt,
                    bias=eps_col[:, 0:1], scale=1.0), deps=TL(t_bm_, t_eps))
                t_br = sch.on("dve", lambda e: (e.reciprocal(
                    stat_sc[:, 31:32], stat_sc[:, 30:31]), e.drain())[-1],
                    deps=[t_bsd])
                t_c_ = sch.on("dve", lambda e: e.tensor_scalar(
                    out=b1_ctm[:, :], in0=b1_x2tm[:, :],
                    scalar1=stat_sc[:, 29:30], scalar2=None, op0=AL.add),
                    deps=[t_bm_])
                t_mxq = sch.on("dve", lambda e: (e.reduce_max(
                    stat_sc[:, 26:27], b1_ctm[:, :], axis=AX.X,
                    apply_absolute_value=True), e.drain())[-1], deps=[t_c_])

                def b1math2(e):
                    sc = stat_sc
                    e.tensor_tensor(sc[:, 27:28], sc[:, 31:32], sc[:, 26:27],
                                    AL.mult)
                    e.drain()
                    e.tensor_scalar_max(sc[:, 27:28], sc[:, 27:28], 1e-5)
                    e.drain()
                    e.reciprocal(sc[:, 28:29], sc[:, 27:28])
                    e.drain()
                    e.tensor_scalar(out=sc[:, 28:29], in0=sc[:, 28:29],
                                    scalar1=127.0, scalar2=None, op0=AL.mult)
                    e.drain()
                    e.tensor_tensor(sc[:, 29:30], sc[:, 31:32], sc[:, 28:29],
                                    AL.mult)
                    e.tensor_scalar(out=sc[:, 30:31], in0=sc[:, 27:28],
                                    scalar1=1.0 / 127.0, scalar2=None,
                                    op0=AL.mult)
                    return e.drain()
                t_bm2 = sch.on("dve", b1math2, deps=[t_mxq, t_br])
                t_t1 = sch.on("dve", lambda e: e.tensor_scalar(
                    out=b1_ctm[:, :], in0=b1_ctm[:, :], scalar1=stat_sc[:, 29:30],
                    scalar2=MAGICF, op0=AL.mult, op1=AL.add), deps=[t_bm2])
                t_xq1 = sch.on("dve", lambda e: e.tensor_scalar(
                    out=b1_hi[:, :], in0=b1_ctm[:, :], scalar1=MAGICF,
                    scalar2=stat_sc[:, 30:31], op0=AL.subtract, op1=AL.mult),
                    deps=[t_t1])
                if debug and s == 0 and tb == 0:
                    sch.dma("gps", lambda e: e.dma_start(
                        out=dbg["d_b1hi"], in_=b1_hi[:, :]), deps=[t_h, t_rb])
                    sch.dma("gps", lambda e: e.dma_start(
                        out=dbg["d_b1lo"], in_=b1_lo[:, :]), deps=[t_l, t_rb])
                    sch.dma("gps", lambda e: e.dma_start(
                        out=dbg["d_b1x2"], in_=b1_x2tm[:, :]), deps=[t_rb, t_c_])
                    sch.dma("gps", lambda e: e.dma_start(
                        out=dbg["d_b1st"], in_=stat_sc[:, :]), deps=[t_bm2])
                t_st_ = sch.dma("gps", (lambda e, tok0=tok0: e.dma_start(
                    out=xq1scr[8 + tok0:8 + tok0 + 128, :], in_=b1_hi[:, :])),
                    deps=[t_xq1])
                b1_sts.append(t_st_)
                prev_q = t_st_
                if debug and s == 0 and tb == 0:
                    sch.dma("gps", lambda e: e.dma_start(
                        out=dbg["d_xq1"], in_=b1_hi[:, :]), deps=[t_xq1])
            tok_b1_prev = b1_sts[-1]
            last_xq1 = b1_sts[-1]
            tok_seg_done = s1

        def ternarize(w_v, nchunk, ncols, wq_dst3, mw_col, dep0):
            tred = []
            prev = dep0
            for kc in range(nchunk):
                t_ld = sch.dma("sync", (lambda e, kc=kc, w_v=w_v, ncols=ncols:
                                        e.dma_start(out=stage_f[:, 0:ncols],
                                                    in_=w_v[:, kc, :])),
                               deps=TL(prev))
                t_r = sch.on("dve", (lambda e, kc=kc, ncols=ncols: (
                    e.tensor_reduce(
                        wstat[:, kc:kc + 1], stage_f[:, 0:ncols], axis=AX.X,
                        op=AL.add, apply_absolute_value=True),
                    e.drain())[-1]), deps=[t_ld])
                prev = t_r
                tred.append(t_r)
            t_r2 = sch.on("dve", (lambda e, nchunk=nchunk: (e.tensor_reduce(
                wstat[:, 36:37], wstat[:, 0:nchunk], axis=AX.X, op=AL.add),
                e.drain())[-1]), deps=tred)
            b, g = next_bank()
            t_sum = sch.on("pe", (lambda e, b=b: e.matmul(
                b[0:1, 0:1], onescf[:, :], wstat[:, 36:37], start=True,
                stop=True)), deps=TL(t_r2, t_of, g))

            def mmath(e, b=b, n=nchunk * ncols * 128):
                e.tensor_scalar(out=wstat[0:1, 37:38], in0=b[0:1, 0:1],
                                scalar1=1.0 / n, scalar2=None, op0=AL.mult)
                e.drain()
                e.tensor_scalar_max(wstat[0:1, 37:38], wstat[0:1, 37:38], 1e-5)
                e.drain()
                e.reciprocal(wstat[0:1, 38:39], wstat[0:1, 37:38])
                e.tensor_copy(wstat[0:1, 39:40], wstat[0:1, 37:38])
                return e.drain()
            t_mm_ = sch.on("dve", mmath, deps=[t_sum])
            set_guard(b, t_mm_)
            b2_, g2 = next_bank()
            t_bcm = sch.on("pe", (lambda e, b=b2_: (
                e.matmul(b[:, 0:1], ones1f[0:1, :], wstat[0:1, 38:39],
                         start=True, stop=True),
                e.matmul(b[:, 1:2], ones1f[0:1, :], wstat[0:1, 39:40],
                         start=True, stop=True))[-1]), deps=TL(t_mm_, g2))
            t_mwc = sch.on("dve", (lambda e, b=b2_, mw=mw_col: (
                e.tensor_copy(mwcol[:, mw:mw + 1], b[:, 0:1]),
                e.tensor_copy(mwcol[:, mw + 1:mw + 2], b[:, 1:2]),
                e.drain())[-1]), deps=[t_bcm])
            set_guard(b2_, t_mwc)
            prev = t_mwc
            last = None
            for kc in range(nchunk):
                t_ld = sch.dma("sync", (lambda e, kc=kc, w_v=w_v, ncols=ncols:
                                        e.dma_start(out=stage_f[:, 0:ncols],
                                                    in_=w_v[:, kc, :])),
                               deps=TL(prev))

                def tern(e, kc=kc, ncols=ncols, mw=mw_col, wq_dst3=wq_dst3):
                    e.tensor_scalar(out=stage_f[:, 0:ncols],
                                    in0=stage_f[:, 0:ncols],
                                    scalar1=mwcol[:, mw:mw + 1], scalar2=MAGICF,
                                    op0=AL.mult, op1=AL.add)
                    e.tensor_scalar(out=stage_f[:, 0:ncols],
                                    in0=stage_f[:, 0:ncols], scalar1=MAGICF,
                                    scalar2=None, op0=AL.subtract)
                    return e.tensor_scalar(out=wq_dst3[:, kc, :],
                                           in0=stage_f[:, 0:ncols], scalar1=1.0,
                                           scalar2=-1.0, op0=AL.min, op1=AL.max)
                t_t = sch.on("dve", tern, deps=[t_ld])
                prev = t_t
                last = t_t
            return last

        t_w1q = ternarize(w1T_v, 8, HD, w1qv, 0, tok_seg_done)
        t_w2q = ternarize(w2T_v, 32, D, w2qv, 2, t_w1q)
        t_tape = sch.on("dve", lambda e: e.tensor_scalar(
            out=taps_sb[:, :], in0=taps_sb[:, :], scalar1=mwcol[:, 1:2],
            scalar2=None, op0=AL.mult), deps=TL(t_w1q, t_consts))

        out_toks = []
        prev_w1_last = None
        prev_q2 = None
        prev_asm = t_w2q

        def emit_xq1w_loads(s, w, first):
            T0 = s * SEG + 256 * w
            lds = []
            for kc in range(8):
                t_lw = sch.dma("sync" if kc % 2 == 0 else "act",
                               (lambda e, kc=kc, T0=T0:
                                e.dma_start_transpose(
                                    b2_xq1w[:, kc, :],
                                    xq1scr[T0:T0 + 272,
                                           128 * kc:128 * kc + 128])),
                               deps=TL(last_xq1, prev_w1_last,
                                       t_tape if first else None))
                lds.append(t_lw)
            return lds

        pending_lds = emit_xq1w_loads(0, 0, True)
        for s in range(NSEG):
            seg = slice(s * SEG, (s + 1) * SEG)
            t_ldx2 = sch.dma("sync", (lambda e, seg=seg: e.dma_start(
                out=xs3[:, :, :], in_=x2scr_v[:, :, seg])),
                deps=TL(prev_asm, tok_x2_store[s][0], tok_x2_store[s][1],
                        tok_x2_store[s][-1]))
            d2_toks = []
            blk_q2st = []
            for w in range(2):
                t_lds = pending_lds
                if debug and s == 0 and w == 0:
                    sch.dma("gps", lambda e: e.dma_start(
                        out=dbg["d_xq1w"], in_=b2_xq1w[:, :, :]), deps=t_lds)
                sh_toks = []
                prev_chain = (None, None, None, None)
                for hc in range(32):
                    b, g = next_bank()

                    def mmw1(e, hc=hc, b=b):
                        last = None
                        for kc in range(8):
                            last = e.matmul(b[:, 0:258],
                                            w1qv[:, kc, 128 * hc:128 * hc + 128],
                                            b2_xq1w[:, kc, 7:265],
                                            start=(kc == 0), stop=(kc == 7))
                        return last
                    t_mm = sch.on("pe", mmw1, deps=TL(*t_lds, t_w1q, g))
                    t_ev = sch.on("act", (lambda e, b=b: e.copy(
                        hwin[:, :], b[:, 0:258])),
                        deps=TL(t_mm, prev_chain[1], prev_chain[2]))
                    set_guard(b, t_ev)
                    t_e1 = sch.on("act", (lambda e, hc=hc: e.activation(
                        c1win[:, :], hwin[:, 0:256], AF.Identity,
                        scale=taps_sb[:, hc:hc + 1])), deps=TL(t_ev, t_tape))
                    t_e2 = sch.on("dve", (lambda e, hc=hc: e.scalar_tensor_tensor(
                        out=c2win[:, :], in0=hwin[:, 1:257],
                        scalar=taps_sb[:, 32 + hc:33 + hc], in1=c1win[:, :],
                        op0=AL.mult, op1=AL.add)), deps=TL(t_e1, prev_chain[2]))
                    t_e3 = sch.on("dve", (lambda e, hc=hc: e.scalar_tensor_tensor(
                        out=c3win[:, :], in0=hwin[:, 2:258],
                        scalar=taps_sb[:, 64 + hc:65 + hc], in1=c2win[:, :],
                        op0=AL.mult, op1=AL.add)), deps=TL(t_e2, prev_chain[3]))
                    t_sl = sch.on("act", (lambda e, hc=hc: e.activation(
                        b2_sh3[:, hc, :], c3win[:, :], AF.Silu,
                        bias=cb_sb[:, hc:hc + 1], scale=1.0)),
                        deps=TL(t_e3, t_consts))
                    prev_chain = (t_ev, t_e2, t_e3, t_sl)
                    sh_toks.append(t_sl)
                    prev_w1_last = t_mm
                nw = 2 * s + w + 1
                if nw < 2 * NSEG:
                    pending_lds = emit_xq1w_loads(nw // 2, nw % 2, False)
                if debug and s == 0 and w == 0:
                    sch.dma("gps", lambda e: e.dma_start(
                        out=dbg["d_sh"], in_=b2_sh3[:, :, :]), deps=sh_toks)

                for tb in range(2):
                    blki = 2 * w + tb
                    tok0 = s * SEG + 128 * blki
                    banks8 = PB + [BM, BMT]
                    tr_toks = []
                    for bi in range(8):
                        b8 = banks8[bi]

                        def trx(e, bi=bi, b8=b8, tb=tb):
                            last = None
                            for j in range(4):
                                hc = bi * 4 + j
                                last = e.matmul(b8[:, 128 * j:128 * j + 128],
                                                b2_sh3[:, hc,
                                                       128 * tb:128 * tb + 128],
                                                identbf[:, :], start=True,
                                                stop=True)
                            return last
                        t_tr = sch.on("pe", trx, deps=TL(
                            *sh_toks, bank_guard[id(b8)], prev_q2))
                        tr_toks.append(t_tr)
                    mx_toks = []
                    for bi in range(8):
                        t_m = sch.on("dve", (lambda e, bi=bi, b8=banks8[bi]: (
                            e.reduce_max(
                                stat_sc[:, 2 + bi:3 + bi],
                                b8[:, 0:512], axis=AX.X,
                                apply_absolute_value=True),
                            e.drain() if bi == 7 else e.nop())[-1]),
                                     deps=[tr_toks[bi]])
                        mx_toks.append(t_m)
                    t_mxa = sch.on("dve", lambda e: (e.reduce_max(
                        stat_sc[:, 10:11], stat_sc[:, 2:10], axis=AX.X),
                        e.drain())[-1], deps=mx_toks)

                    def q2math(e, blki=blki):
                        sc = stat_sc
                        e.tensor_scalar_max(sc[:, 11:12], sc[:, 10:11], 1e-5)
                        e.drain()
                        e.reciprocal(sc[:, 12:13], sc[:, 11:12])
                        e.drain()
                        e.tensor_scalar(out=sc[:, 12:13], in0=sc[:, 12:13],
                                        scalar1=127.0, scalar2=None, op0=AL.mult)
                        e.scalar_tensor_tensor(
                            out=d2tab[:, blki:blki + 1], in0=sc[:, 11:12],
                            scalar=1.0 / 127.0, in1=mwcol[:, 3:4],
                            op0=AL.mult, op1=AL.mult)
                        return e.drain()
                    t_q2m = sch.on("dve", q2math, deps=TL(t_mxa, t_w2q))
                    d2_toks.append(t_q2m)
                    rp_toks = []
                    for bi in range(8):
                        t_r1 = sch.on("dve", (lambda e, bi=bi, b8=banks8[bi]:
                                              e.tensor_scalar(
                                                  out=ftmp[:, :], in0=b8[:, 0:512],
                                                  scalar1=stat_sc[:, 12:13],
                                                  scalar2=MAGICF, op0=AL.mult,
                                                  op1=AL.add)), deps=[t_q2m])
                        t_r2_ = sch.on("dve", (lambda e, bi=bi: e.tensor_scalar(
                            out=b2_xq2tm[:, 512 * bi:512 * bi + 512],
                            in0=ftmp[:, :], scalar1=MAGICF, scalar2=None,
                            op0=AL.subtract)), deps=[t_r1])
                        set_guard(banks8[bi], t_r2_)
                        rp_toks.append(t_r2_)
                    t_stq = sch.dma("gps", (lambda e, tok0=tok0: e.dma_start(
                        out=xq2scr[tok0:tok0 + 128, :], in_=b2_xq2tm[:, :])),
                        deps=rp_toks)
                    blk_q2st.append(t_stq)
                    prev_q2 = t_stq
                    if debug and s == 0 and blki == 0:
                        sch.dma("gps", lambda e: e.dma_start(
                            out=dbg["d_xq2"], in_=b2_xq2tm[:, :]), deps=rp_toks)

            bD, gD = next_bank()
            t_trd = sch.on("pe", (lambda e, b=bD: (
                [e.matmul(b[0:1, 128 * i:128 * i + 128], d2tab[:, i:i + 1],
                          identbf[:, :], start=True, stop=True)
                 for i in range(4)])[-1]), deps=TL(*d2_toks, gD))
            t_d2r = sch.on("dve", (lambda e, b=bD: (e.tensor_copy(
                d2row[0:1, :], b[0:1, 0:512]), e.drain())[-1]), deps=[t_trd])
            set_guard(bD, t_d2r)
            bD2, gD2 = next_bank()
            t_d2b = sch.on("pe", (lambda e, b=bD2: e.matmul(
                b[:, 0:512], ones1[0:1, :], d2row[0:1, :], start=True,
                stop=True)), deps=TL(t_d2r, gD2))
            t_d2bc = sch.on("dve", (lambda e, b=bD2: e.tensor_copy(
                d2bc[:, :], b[:, 0:512])), deps=[t_d2b])
            set_guard(bD2, t_d2bc)

            banks8 = PB + [BM, BMT]
            mm_toks = []
            ld_toks = []
            for kc in range(32):
                buf = b2_xq2b[kc % 2]
                t_lq = sch.dma("sync" if kc % 2 == 0 else "act",
                               (lambda e, kc=kc, buf=buf, seg=seg:
                                e.dma_start_transpose(
                                    buf,
                                    xq2scr[seg, 128 * kc:128 * kc + 128])),
                               deps=TL(blk_q2st[-1],
                                       mm_toks[kc - 2] if kc >= 2 else None))
                ld_toks.append(t_lq)
                depk = [t_lq, t_w2q]
                if kc == 0:
                    depk += [bank_guard[id(b_)] for b_ in banks8]
                    depk += [t_d2bc]

                def mmw2(e, kc=kc, buf=buf):
                    last = None
                    for ot in range(8):
                        last = e.matmul(banks8[ot][:, 0:512],
                                        w2qv[:, kc, 128 * ot:128 * ot + 128],
                                        buf, start=(kc == 0), stop=(kc == 31))
                    return last
                t_mmq = sch.on("pe", mmw2, deps=TL(*depk))
                mm_toks.append(t_mmq)
            asm_toks = []
            for ot in range(8):
                t_as1 = sch.on("dve", (lambda e, ot=ot, b=banks8[ot]:
                                       e.tensor_tensor(ftmp[:, :], b[:, 0:512],
                                                       d2bc[:, :], AL.mult)),
                               deps=TL(mm_toks[-1], t_d2bc,
                                       asm_toks[-1] if asm_toks else None))
                t_as2 = sch.on("dve", (lambda e, ot=ot: e.tensor_tensor(
                    xs3[:, ot, :], xs3[:, ot, :], ftmp[:, :], AL.add)),
                    deps=[t_as1, t_ldx2])
                set_guard(banks8[ot], t_as2)
                asm_toks.append(t_as2)
            t_out = sch.dma("sync", (lambda e, seg=seg: e.dma_start(
                out=outT_v[:, :, seg], in_=xs3[:, :, :])), deps=asm_toks)
            out_toks.append(t_out)
            prev_asm = t_out

        if debug:
            sch.dma("gps", lambda e: e.dma_start(
                out=dbg["d_xq1s"], in_=xq1scr[8:264, :]), deps=[last_xq1])
            sch.dma("gps", lambda e: e.dma_start(
                out=dbg["d_xq2s"], in_=xq2scr[0:128, :]), deps=[out_toks[-1]])
            t_dd2 = sch.on("dve", lambda e: e.tensor_copy(
                stat_sc[:, 0:4], d2tab[:, :]), deps=[out_toks[-1]])
            sch.dma("gps", lambda e: e.dma_start(
                out=dbg["d_d2"], in_=stat_sc[:, 0:4]), deps=[t_dd2])
        sch.final_wait("sync")

        blk = st.enter_context(nc.Block())
        st.enter_context(nc.allow_low_precision(
            reason="bf16 operands with fp32 psum accumulation are within this "
                   "problem's stated tolerance; precision-sensitive steps "
                   "(round/clip/stats/residual) are kept in fp32 explicitly"))

        @blk.tensor
        def _(eng):
            sch.emit("pe", eng)

        @blk.vector
        def _(eng):
            sch.emit("dve", eng)

        @blk.scalar
        def _(eng):
            sch.emit("act", eng)

        @blk.gpsimd
        def _(eng):
            sch.emit("gps", eng)

        @blk.sync
        def _(eng):
            sch.emit("sync", eng)

    return nc



def _consts():
    ident = np.eye(128)
    i2 = np.concatenate([np.eye(64), np.eye(64)], 0)
    low = np.tril(np.ones((64, 64)), -1)
    up = np.triu(np.ones((64, 64)), 1)
    low2 = np.concatenate([low, low], 0)
    up2 = np.concatenate([up, up], 0)
    neg = np.where(np.concatenate([up, up], 0) > 0, -1e30, 0.0)
    ones2 = np.zeros((128, 2))
    ones2[:, 0] = 1.0
    ones1 = np.ones((1, 128))
    hsum = np.zeros((128, 2))
    hsum[0:64, 0] = 1.0
    hsum[64:128, 1] = 1.0
    exp2 = np.zeros((2, 128))
    exp2[0, 0:64] = 1.0
    exp2[1, 64:128] = 1.0
    return dict(
        identbf=np.ascontiguousarray(ident.astype(BF)),
        ident16f=np.ascontiguousarray(np.eye(16, dtype=np.float32)),
        i2bf=np.ascontiguousarray(i2.astype(BF)),
        masklow=np.ascontiguousarray(np.tile(low2, (1, 8)).astype(BF)),
        maskup=np.ascontiguousarray(np.tile(up2, (1, 8)).astype(BF)),
        maskneg=np.ascontiguousarray(np.tile(neg, (1, 8)).astype(BF)),
        onesc=np.ascontiguousarray(ones2.astype(BF)),
        ones1=np.ascontiguousarray(ones1.astype(BF)),
        ones1f=np.ascontiguousarray(ones1.astype(np.float32)),
        hsum2=np.ascontiguousarray(hsum.astype(BF)),
        expander2=np.ascontiguousarray(exp2.astype(BF)),
    )


_NC_CACHE = {}


def kernel(L0_attn_x, L0_attn_Wq, L0_attn_Wk, L0_attn_Wv, L0_attn_Wo,
           L0_attn_alpha_raw, L0_attn_beta_raw, L0_ffn_w1, L0_ffn_conv_w,
           L0_ffn_conv_b, L0_ffn_w2, _debug=False, _cores=8):
    f32 = np.float32
    wqT = np.ascontiguousarray(np.asarray(L0_attn_Wq, f32).T)
    wkT = np.ascontiguousarray(np.asarray(L0_attn_Wk, f32).T)
    wvT = np.ascontiguousarray(np.asarray(L0_attn_Wv, f32).T)
    woT = np.ascontiguousarray(np.asarray(L0_attn_Wo, f32).T)
    w1T = np.ascontiguousarray(np.asarray(L0_ffn_w1, f32).T)
    w2T = np.ascontiguousarray(np.asarray(L0_ffn_w2, f32).T)
    cw = np.asarray(L0_ffn_conv_w, f32)[:, 0, :]
    taps = np.zeros((128, 96), f32)
    for k in range(3):
        taps[:, 32 * k:32 * k + 32] = cw[:, k].reshape(32, 128).T
    convb = np.ascontiguousarray(np.asarray(L0_ffn_conv_b, f32).reshape(32, 128).T)
    abraw = np.ascontiguousarray(np.stack(
        [np.asarray(L0_attn_alpha_raw, f32).reshape(16),
         np.asarray(L0_attn_beta_raw, f32).reshape(16)], 1))
    consts = _consts()

    key = ("dbg" if _debug else "main")
    if key not in _NC_CACHE:
        _NC_CACHE[key] = build_nc(debug=_debug)
    nc = _NC_CACHE[key]

    x = np.asarray(L0_attn_x, f32)
    in_maps = []
    for b in range(_cores):
        m = dict(xT=np.ascontiguousarray(x[b].T), wqT=wqT, wkT=wkT, wvT=wvT,
                 woT=woT, w1T=w1T, w2T=w2T, taps=taps, convb=convb, abraw=abraw)
        m.update(consts)
        in_maps.append(m)
    res = run_bass_kernel_spmd(nc, in_maps, core_ids=list(range(_cores)))
    outs = [np.ascontiguousarray(res.results[b]["outT"].T) for b in range(_cores)]
    if _cores < 8:
        outs = outs + [outs[0]] * (8 - _cores)
    full = np.stack(outs[:8])
    if _debug:
        return full, res.results
    return full
